# version 30
# baseline (speedup 1.0000x reference)
import numpy as np
import ml_dtypes
import concourse.bass as bass
import concourse.mybir as mybir
from concourse.bass_utils import run_bass_kernel_spmd
from contextlib import ExitStack

F32 = mybir.dt.float32
BF16 = mybir.dt.bfloat16
AF = mybir.ActivationFunctionType
ALU = mybir.AluOpType
AX = mybir.AxisListType
BFNP = ml_dtypes.bfloat16

B = 2
N = 2048
E = 2048
H = 16
D = 128
HF = 8192
NT = N // 128
EB = E // 128
SCALE = 1.0 / float(np.sqrt(np.float64(D)))
LN_EPS = 1e-5
RMS_EPS = 1.1920928955078125e-07
QTHR = 0.5009765625

ENGS = ("T", "V", "S", "G", "Y")
DMA_SLOTS = {"Y": 8, "G": 6}


class Prog:
    def __init__(self):
        self.ops = []
        self.sem_ids = list(ENGS)
        for e, k in DMA_SLOTS.items():
            self.sem_ids += [(e, s) for s in range(k)]
        self.cnt = {sid: 0 for sid in self.sem_ids}
        self.dmac = {e: 0 for e in DMA_SLOTS}
        self.buf = {}

    def op(self, eng, fn, reads=(), writes=(), dma=False):
        waits = {}

        def add(tok):
            if tok is None:
                return
            sid, c2 = tok
            if waits.get(sid, 0) < c2:
                waits[sid] = c2

        for k in reads:
            b = self.buf.get(k)
            if b:
                add(b[0])
        for k in writes:
            b = self.buf.get(k)
            if b:
                add(b[0])
                for t in b[1]:
                    add(t)
        if dma:
            slot = (eng, self.dmac[eng] % DMA_SLOTS[eng])
            self.dmac[eng] += 1
            self.cnt[slot] += 16
            tok = (slot, self.cnt[slot])
            sid = slot
            inc = 16
        else:
            if fn is None:
                inc = 0
            else:
                inc = 1
            self.cnt[eng] += inc
            tok = (eng, self.cnt[eng])
            sid = eng
        for k in reads:
            self.buf.setdefault(k, [None, []])[1].append(tok)
        for k in writes:
            self.buf[k] = [tok, []]
        self.ops.append((eng, fn, waits, inc, sid))
        return tok

    def barrier(self):
        snap = dict(self.cnt)
        for e in ENGS:
            waits = {sid: c for sid, c in snap.items() if c > 0 and sid != e}
            self.ops.append((e, None, waits, 0, e))

    def emit(self, block, sems):
        per = {e: [] for e in ENGS}
        for o in self.ops:
            per[o[0]].append(o)

        def mk_body(name):
            def f(eng):
                water = {}
                for (_, fn, waits, inc, sid) in per[name]:
                    for s in sorted(waits, key=repr):
                        v = waits[s]
                        if water.get(s, 0) < v:
                            eng.wait_ge(sems[s], v)
                            water[s] = v
                    if fn is not None:
                        ins = fn(eng)
                        if inc:
                            ins.then_inc(sems[sid], inc)
            return f

        block.tensor(mk_body("T"))
        block.vector(mk_body("V"))
        block.scalar(mk_body("S"))
        block.gpsimd(mk_body("G"))
        block.sync(mk_body("Y"))


def build_nc(debug=False, upto=9):
    nc = bass.Bass("TRN2", target_bir_lowering=False, debug=False)
    ap = lambda t: t.ap()
    scr_kind = "ExternalOutput" if debug else "Internal"

    x_h = ap(nc.dram_tensor("x", [B, NT, 128, E], F32, kind="ExternalInput"))
    wq_h = ap(nc.dram_tensor("wq", [4, EB, 128, 512], F32, kind="ExternalInput"))
    wkv_h = ap(nc.dram_tensor("wkv", [EB, 128, 256], F32, kind="ExternalInput"))
    wkvh_h = ap(nc.dram_tensor("wkvh", [EB, 128, 32], F32, kind="ExternalInput"))
    wg_h = ap(nc.dram_tensor("wg", [EB, 128, 32], F32, kind="ExternalInput"))
    wout_h = ap(nc.dram_tensor("wout", [EB, 128, E], F32, kind="ExternalInput"))
    wgu_h = ap(nc.dram_tensor("wgu", [2 * HF, E], F32, kind="ExternalInput"))
    wp_h = ap(nc.dram_tensor("wp", [E, HF], F32, kind="ExternalInput"))
    id_h = ap(nc.dram_tensor("ident", [128, 128], BF16, kind="ExternalInput"))
    mask_h = ap(nc.dram_tensor("mask", [128, 128], F32, kind="ExternalInput"))
    cos_h = ap(nc.dram_tensor("cos_tok", [128, NT * 64], BF16, kind="ExternalInput"))
    sin_h = ap(nc.dram_tensor("sin_tok", [128, NT * 64], BF16, kind="ExternalInput"))
    ones_h = ap(nc.dram_tensor("ones_col", [128, 1], BF16, kind="ExternalInput"))

    out_h = ap(nc.dram_tensor("out", [B, NT, 128, E], F32, kind="ExternalOutput"))

    x1_scr = ap(nc.dram_tensor("x1_scr", [B, NT, 128, E], F32, kind=scr_kind))
    whq_scr = ap(nc.dram_tensor("whq_scr", [4, 64, 128, 512], BF16, kind=scr_kind))
    ln1T_scr = ap(nc.dram_tensor("ln1T_scr", [B, EB, 128, N], BF16, kind=scr_kind))
    h_scr = ap(nc.dram_tensor("h_scr", [B, NT, 64, 128, 128], BF16, kind=scr_kind))
    ffn_scr = ap(nc.dram_tensor("ffn_scr", [B, NT, 128, E], F32, kind=scr_kind))
    if debug:
        tap_cum = ap(nc.dram_tensor("tap_cum", [32, 6144], F32, kind="ExternalOutput"))
        tap_ctok = ap(nc.dram_tensor("tap_ctok", [128, 512], F32, kind="ExternalOutput"))
        tap_rkp = ap(nc.dram_tensor("tap_rkp", [128, 2048], BF16, kind="ExternalOutput"))
        tap_vp = ap(nc.dram_tensor("tap_vp", [128, 2048], BF16, kind="ExternalOutput"))
        tap_rqT = ap(nc.dram_tensor("tap_rqT", [128, 32768], BF16, kind="ExternalOutput"))
        tap_oT = ap(nc.dram_tensor("tap_oT", [128, 32768], BF16, kind="ExternalOutput"))

    st_ = ExitStack()
    with st_:
        SB = lambda nm, sh, dt: st_.enter_context(nc.sbuf_tensor(nm, sh, dt))
        bigA = SB("bigA", [128, 32768], BF16)
        bigB = SB("bigB", [128, 32768], BF16)
        wf32 = SB("wf32", [128, 4096], F32)
        wbf = SB("wbf", [128, 8192], BF16)
        ovl1 = SB("ovl1", [128, 12288], BF16)
        ovl2 = SB("ovl2", [128, 6144], BF16)
        ctok = SB("ctok", [128, 512], F32)
        id_bf = SB("id_bf", [128, 128], BF16)
        mask_s = SB("mask_s", [128, 128], F32)
        cos_s = SB("cos_s", [128, NT * 64], BF16)
        sin_s = SB("sin_s", [128, NT * 64], BF16)
        ones_s = SB("ones_s", [128, 1], BF16)
        cols = SB("cols", [128, 24], F32)
        qcols = SB("qcols", [128, 256], F32)
        ssq_sb = SB("ssq_sb", [128, 32], F32)
        inv_sb = SB("inv_sb", [128, 32], F32)
        qscb_t = SB("qscb_t", [128, 32], BF16)
        wscb_t = SB("wscb_t", [128, 32], BF16)

        s_ps = st_.enter_context(nc.psum_tensor("s_ps", [128, 2048], F32))
        dmy_ps = st_.enter_context(nc.psum_tensor("dmy_ps", [128, 512], F32))
        o_psA = st_.enter_context(nc.psum_tensor("o_psA", [128, 512], F32))
        o_psB = st_.enter_context(nc.psum_tensor("o_psB", [128, 512], F32))
        tr_ps = st_.enter_context(nc.psum_tensor("tr_ps", [128, 1024], BF16))

        sems = {}
        for e in ENGS:
            sems[e] = st_.enter_context(nc.semaphore("sem" + e))
        for e, k in DMA_SLOTS.items():
            for s2 in range(k):
                sems[(e, s2)] = st_.enter_context(
                    nc.semaphore("sem%s%d" % (e, s2)))
        block = st_.enter_context(nc.Block())
        p = Prog()

        cum = ovl1[:32, :].bitcast(F32)
        cA = cum[:, 0:2048]
        cB = cum[:, 2048:4096]
        cC = cum[:, 4096:6144]
        khT = ovl1[:, 0:2048]
        vh_sb = ovl1[:, 2048:4096]
        pT_sb = ovl1[:, 4096:6144]
        kstage = ovl1[:, 6144:8192]
        qa = ovl1[:, 0:2048]
        qsg = ovl1[:, 2048:4096]
        qm = ovl1[:, 4096:6144]
        rkp = ovl2[:, 0:2048]
        vp = ovl2[:, 2048:4096]
        rsc = ovl2[:, 4096:6144]
        ovl2f = ovl2[:, 0:4096].bitcast(F32)
        hst = lambda par: ovl2[:, par * 2048:(par + 1) * 2048]
        h2v = ovl2[:, 4096:6144]
        xslot = lambda i: wf32[:, i * 2048:(i + 1) * 2048]
        wbfs = lambda i: wbf[:, i * 2048:(i + 1) * 2048]
        bBf32 = bigB[:].bitcast(F32)
        gup_f = lambda par: bBf32[:, par * 2048:(par + 1) * 2048]
        up_f = lambda par: bBf32[:, (2 + par) * 2048:(3 + par) * 2048]
        gT = lambda par: bigB[:, 16384 + par * 2048:16384 + (par + 1) * 2048]
        uT = lambda par: bigB[:, 20480 + par * 2048:20480 + (par + 1) * 2048]

        mxc = cols[:, 0:1]
        biasc = cols[:, 1:2]
        lc = cols[:, 2:3]
        linvc = cols[:, 3:4]
        m1c = cols[:, 4:5]
        m2c = cols[:, 5:6]
        vrc = cols[:, 6:7]
        rsdc = cols[:, 7:8]
        t0c = cols[:, 8:9]
        qred = qcols[:, 0:32]
        qthr = qcols[:, 32:64]
        qsc_bf = qscb_t[:]

        ln0T = lambda eb: bigA[:, eb * N:(eb + 1) * N]
        oT = lambda h: bigA[:, h * N:(h + 1) * N]
        rqT = lambda h: bigB[:, h * N:(h + 1) * N]
        rq_tile = lambda h, i: bigB[:, ((h // 4) * 16 + i) * 512 + (h % 4) * 128:
                                    ((h // 4) * 16 + i) * 512 + (h % 4) * 128 + 128]
        woutb = lambda hdb: bigB[:, hdb * N:(hdb + 1) * N]
        ln1T = lambda eb: bigA[:, eb * N:(eb + 1) * N]
        wptq = lambda half: (bigA if half == 0 else bigB)

        for dst, src, key in [(id_bf[:], id_h, "id_bf"), (mask_s[:], mask_h, "mask"),
                              (cos_s[:], cos_h, "cos"), (sin_s[:], sin_h, "sin"),
                              (ones_s[:], ones_h, "ones")]:
            p.op("Y", (lambda d, s: lambda q: q.dma_start(out=d, in_=s))(dst, src),
                 writes=[key], dma=True)

        def dummy_mm(n):
            def mk(n=n):
                def f(t):
                    ins = None
                    for _ in range(n):
                        ins = t.matmul(dmy_ps[:, 0:512], id_bf[:],
                                       cos_s[:, 0:512], start=True, stop=True)
                    return ins
                return f
            p.op("T", mk(), reads=["id_bf", "cos"], writes=["dmy"])

        dummy_mm(30)

        def ln_cols(src_key, src_ap, n_free, sq_ap, sq_key, eps, junk_ap=None):
            if not isinstance(sq_key, list):
                sq_key = [sq_key]
            if junk_ap is None:
                junk_ap = sq_ap
            p.op("S", (lambda a, d: lambda s: s.activation(
                d, a, AF.Copy, accum_out=m1c))(src_ap, junk_ap),
                reads=[src_key], writes=sq_key + ["c_m1"])
            p.op("S", (lambda a, d: lambda s: s.activation(
                d, a, AF.Square, accum_out=m2c))(src_ap, sq_ap),
                reads=[src_key], writes=sq_key + ["c_m2"])
            p.op("V", (lambda n: lambda v: v.tensor_scalar(
                m1c, m1c, 1.0 / n, None, ALU.mult))(n_free),
                reads=["c_m1"], writes=["c_m1"])
            p.op("V", (lambda n: lambda v: v.tensor_scalar(
                m2c, m2c, 1.0 / n, None, ALU.mult))(n_free),
                reads=["c_m2"], writes=["c_m2"])
            p.op("V", lambda v: v.tensor_mul(t0c, m1c, m1c),
                 reads=["c_m1"], writes=["c_t0"])
            p.op("V", lambda v: v.tensor_sub(vrc, m2c, t0c),
                 reads=["c_m2", "c_t0"], writes=["c_vr"])
            p.op("V", (lambda e: lambda v: v.tensor_scalar(
                vrc, vrc, e, None, ALU.add))(eps),
                reads=["c_vr"], writes=["c_vr"])
            p.op("S", lambda s: s.activation(rsdc, vrc, AF.Ln),
                 reads=["c_vr"], writes=["c_rsd"])
            p.op("S", lambda s: s.activation(rsdc, rsdc, AF.Exp, scale=-0.5),
                 reads=["c_rsd"], writes=["c_rsd"])

        def ln_apply(dst_key, dst_ap, src_key, src_ap):
            if not isinstance(dst_key, list):
                dst_key = [dst_key]
            p.op("V", (lambda d, a: lambda v: v.tensor_scalar(
                d, a, m1c, rsdc, ALU.subtract, ALU.mult))(dst_ap, src_ap),
                reads=[src_key, "c_m1", "c_rsd"], writes=dst_key)

        def transpose_thunks(pairs, ident, ident_key):
            ths = []
            for base in range(0, len(pairs), 8):
                grp = pairs[base:base + 8]
                ths.append((lambda grp=grp: transpose_batch(grp, ident, ident_key)))
            return ths

        def transpose_batch(pairs, ident, ident_key, eng="V"):
            for base in range(0, len(pairs), 8):
                grp = pairs[base:base + 8]

                def mk_t(grp=grp):
                    def f(t):
                        ins = None
                        for gi, (sa, _, _, _) in enumerate(grp):
                            ins = t.transpose(
                                tr_ps[:, gi * 128:(gi + 1) * 128], sa, ident)
                        return ins
                    return f
                rk = sorted({g[1] for g in grp})
                p.op("T", mk_t(), reads=rk + [ident_key], writes=["tr_ps"])

                contig = len(grp) > 1
                for gi in range(1, len(grp)):
                    a, bq = grp[gi - 1][2], grp[gi][2]
                    if not (a.tensor is bq.tensor and
                            bq.offset == a.offset + 128):
                        contig = False
                        break
                if contig:
                    d0 = grp[0][2]
                    full = d0.tensor.ap()[:, d0.offset:d0.offset + 128 * len(grp)]
                    def mk_e1(grp=grp, full=full, eng=eng):
                        def f(v):
                            if eng == "V":
                                return v.tensor_copy(full, tr_ps[:, 0:128 * len(grp)])
                            return v.copy(full, tr_ps[:, 0:128 * len(grp)])
                        return f
                    wk = sorted({g[3] for g in grp})
                    p.op(eng, mk_e1(), reads=["tr_ps"], writes=wk)
                else:
                    def mk_e(grp=grp, eng=eng):
                        def f(v):
                            ins = None
                            for gi, (_, _, da, _) in enumerate(grp):
                                if eng == "V":
                                    ins = v.tensor_copy(da, tr_ps[:, gi * 128:(gi + 1) * 128])
                                else:
                                    ins = v.copy(da, tr_ps[:, gi * 128:(gi + 1) * 128])
                            return ins
                        return f
                    wk = sorted({g[3] for g in grp})
                    p.op(eng, mk_e(), reads=["tr_ps"], writes=wk)

        def rope_tile(src_key, tt, out_lo, out_hi, okl, okh, par=0):
            ct = cos_s[:, tt * 64:(tt + 1) * 64]
            st2 = sin_s[:, tt * 64:(tt + 1) * 64]
            t1 = rsc[:, par * 128:par * 128 + 64]
            t2 = rsc[:, par * 128 + 64:par * 128 + 128]
            r1 = qcols[:, par * 128:par * 128 + 64]
            r2 = qcols[:, par * 128 + 64:par * 128 + 128]
            p.op("V", lambda v: v.tensor_mul(r1, t1, ct),
                 reads=[src_key, "cos"], writes=["ro_r1"])
            p.op("V", lambda v: v.tensor_mul(r2, t2, st2),
                 reads=[src_key, "sin"], writes=["ro_r2"])
            p.op("V", (lambda o: lambda v: v.tensor_sub(o, r1, r2))(out_lo),
                 reads=["ro_r1", "ro_r2"], writes=[okl])
            p.op("V", lambda v: v.tensor_mul(r1, t2, ct),
                 reads=[src_key, "cos"], writes=["ro_r1"])
            p.op("V", lambda v: v.tensor_mul(r2, t1, st2),
                 reads=[src_key, "sin"], writes=["ro_r2"])
            p.op("V", (lambda o: lambda v: v.tensor_add(o, r1, r2))(out_hi),
                 reads=["ro_r1", "ro_r2"], writes=[okh])

        def quant_thunks(src_f32_ap, src_key, dst_bf_ap, dst_key):
            th = []
            th.append(lambda: p.op("S", (lambda a=src_f32_ap: lambda s: s.copy(qa, a))(),
                                   reads=[src_key], writes=["q_a"]))
            th.append(lambda: p.op("V", lambda v: v.tensor_reduce(
                qred, qa[:].rearrange("p (g s) -> p g s", s=64), axis=AX.X,
                op=ALU.add, apply_absolute_value=True),
                reads=["q_a"], writes=["q_red"]))
            th.append(lambda: p.op("V", lambda v: v.tensor_scalar(
                qsc_bf, qred, 1.0 / 64, 1e-8, ALU.mult, ALU.max),
                reads=["q_red"], writes=["q_sc"]))
            th.append(lambda: p.op("V", lambda v: v.tensor_scalar(
                qthr, qsc_bf, QTHR, None, ALU.mult),
                reads=["q_sc"], writes=["q_thr"]))
            th.append(lambda: p.op("S", lambda s: s.sign(qsg, qa),
                                   reads=["q_a"], writes=["q_sg"]))
            th.append(lambda: p.op("S", lambda s: s.activation(qm, qa, AF.Abs),
                                   reads=["q_a"], writes=["q_m", "q_hat"]))
            th.append(lambda: p.op("V", lambda v: v.tensor_tensor(
                qa[:].rearrange("p (g s) -> p g s", s=64),
                qm[:].rearrange("p (g s) -> p g s", s=64),
                qthr[:].rearrange("p (g o) -> p g o", o=1).broadcast_to([128, 32, 64]),
                ALU.is_gt), reads=["q_m", "q_thr"], writes=["q_a"]))
            th.append(lambda: p.op("V", lambda v: v.tensor_mul(qm, qa, qsg),
                                   reads=["q_a", "q_sg"], writes=["q_m", "q_hat"]))
            th.append(lambda: p.op("V", (lambda d=dst_bf_ap: lambda v: v.tensor_tensor(
                d.rearrange("p (g s) -> p g s", s=64),
                qm[:].rearrange("p (g s) -> p g s", s=64),
                qsc_bf[:].rearrange("p (g o) -> p g o", o=1).broadcast_to([128, 32, 64]),
                ALU.mult))(), reads=["q_m", "q_sc"],
                writes=[dst_key, "q_m"] if dst_key == "q_hat" else [dst_key]))
            return th

        def quant_chunk(src_f32_ap, src_key, dst_bf_ap, dst_key):
            p.op("S", (lambda a: lambda s: s.copy(qa, a))(src_f32_ap),
                 reads=[src_key], writes=["q_a"])
            p.op("V", lambda v: v.tensor_reduce(
                qred, qa[:].rearrange("p (g s) -> p g s", s=64), axis=AX.X,
                op=ALU.add, apply_absolute_value=True),
                reads=["q_a"], writes=["q_red"])
            p.op("V", lambda v: v.tensor_scalar(
                qsc_bf, qred, 1.0 / 64, 1e-8, ALU.mult, ALU.max),
                reads=["q_red"], writes=["q_sc"])
            p.op("V", lambda v: v.tensor_scalar(qthr, qsc_bf, QTHR, None, ALU.mult),
                 reads=["q_sc"], writes=["q_thr"])
            p.op("S", lambda s: s.sign(qsg, qa), reads=["q_a"], writes=["q_sg"])
            p.op("S", lambda s: s.activation(qm, qa, AF.Abs),
                 reads=["q_a"], writes=["q_m", "q_hat"])
            p.op("V", lambda v: v.tensor_tensor(
                qa[:].rearrange("p (g s) -> p g s", s=64),
                qm[:].rearrange("p (g s) -> p g s", s=64),
                qthr[:].rearrange("p (g o) -> p g o", o=1).broadcast_to([128, 32, 64]),
                ALU.is_gt), reads=["q_m", "q_thr"], writes=["q_a"])
            p.op("V", lambda v: v.tensor_mul(qm, qa, qsg),
                 reads=["q_a", "q_sg"], writes=["q_m", "q_hat"])
            p.op("V", (lambda d: lambda v: v.tensor_tensor(
                d.rearrange("p (g s) -> p g s", s=64),
                qm[:].rearrange("p (g s) -> p g s", s=64),
                qsc_bf[:].rearrange("p (g o) -> p g o", o=1).broadcast_to([128, 32, 64]),
                ALU.mult))(dst_bf_ap), reads=["q_m", "q_sc"],
                writes=[dst_key, "q_m"] if dst_key == "q_hat" else [dst_key])

        wstg = wf32[:, 512:1536]
        wfbf = wf32[:].bitcast(BF16)
        ctok_bf = ctok[:].bitcast(BF16)
        wqa_c = [wfbf[:, 3072:4096], ctok_bf[:, 0:1024]]
        wqb_c = [wfbf[:, 5120:6144], wfbf[:, 7168:8192]]
        wred_c = [qcols[:, 64:80], qcols[:, 80:96]]
        wthr_c = [qcols[:, 96:112], qcols[:, 112:128]]
        wscb_c = [wscb_t[:, 0:16], wscb_t[:, 16:32]]

        def wp_chunks():
            out = []
            cidx = 0
            for q in range(4):
                for db2 in range(4):
                    for hs2 in range(8):
                        pc = cidx % 2
                        A = wqa_c[pc]
                        Bq = wqb_c[pc]
                        red = wred_c[pc]
                        thr = wthr_c[pc]
                        scb = wscb_c[pc]
                        ka, kb, kt = "w_a%d" % pc, "w_b%d" % pc, "w_t%d" % pc
                        kr, ks, kh = ("w_red%d" % pc, "w_sc%d" % pc,
                                      "w_thr%d" % pc)
                        vth = []
                        vth.append((lambda q=q, db2=db2, hs2=hs2: p.op(
                            "Y", (lambda q=q, db2=db2, hs2=hs2:
                                  lambda qe: qe.dma_start(
                                out=wstg,
                                in_=wp_h[q * 512 + db2 * 128:
                                         q * 512 + (db2 + 1) * 128,
                                         hs2 * 1024:(hs2 + 1) * 1024]))(),
                            writes=["w_stg"], dma=True)))
                        vth.append((lambda A=A, ka=ka, kt=kt: p.op(
                            "S", lambda s: s.copy(A, wstg),
                            reads=["w_stg"], writes=[ka, kt])))
                        vth.append((lambda A=A, red=red, ka=ka, kr=kr: p.op(
                            "V", lambda v: v.tensor_reduce(
                                red, A.rearrange("p (g s) -> p g s", s=64),
                                axis=AX.X, op=ALU.add,
                                apply_absolute_value=True),
                            reads=[ka], writes=[kr])))
                        vth.append((lambda red=red, scb=scb, kr=kr, ks=ks: p.op(
                            "V", lambda v: v.tensor_scalar(
                                scb, red, 1.0 / 64, 1e-8, ALU.mult, ALU.max),
                            reads=[kr], writes=[ks])))
                        vth.append((lambda thr=thr, scb=scb, ks=ks, kh=kh: p.op(
                            "V", lambda v: v.tensor_scalar(
                                thr, scb, QTHR, None, ALU.mult),
                            reads=[ks], writes=[kh])))
                        vth.append((lambda A=A, Bq=Bq, ka=ka, kb=kb: p.op(
                            "S", lambda s: s.sign(Bq, A),
                            reads=[ka], writes=[kb])))
                        vth.append((lambda A=A, ka=ka: p.op(
                            "S", lambda s: s.activation(A, A, AF.Abs),
                            reads=[ka], writes=[ka])))
                        vth.append((lambda A=A, thr=thr, ka=ka, kh=kh: p.op(
                            "V", lambda v: v.tensor_tensor(
                                A.rearrange("p (g s) -> p g s", s=64),
                                A.rearrange("p (g s) -> p g s", s=64),
                                thr.rearrange("p (g o) -> p g o", o=1)
                                .broadcast_to([128, 16, 64]),
                                ALU.is_gt), reads=[ka, kh], writes=[ka])))
                        vth.append((lambda A=A, Bq=Bq, ka=ka, kb=kb: p.op(
                            "V", lambda v: v.tensor_mul(Bq, Bq, A),
                            reads=[ka, kb], writes=[kb])))
                        vth.append((lambda Bq=Bq, scb=scb, kb=kb, ks=ks: p.op(
                            "V", lambda v: v.tensor_tensor(
                                Bq.rearrange("p (g s) -> p g s", s=64),
                                Bq.rearrange("p (g s) -> p g s", s=64),
                                scb.rearrange("p (g o) -> p g o", o=1)
                                .broadcast_to([128, 16, 64]),
                                ALU.mult), reads=[kb, ks], writes=[kb])))
                        pairs = [(Bq[:, k * 128:(k + 1) * 128], kb,
                                  A[:, k * 128:(k + 1) * 128], kt)
                                 for k in [0, 2, 4, 6, 1, 3, 5, 7]]
                        tth = transpose_thunks(pairs, id_bf[:], "id_bf")
                        tth.append((lambda q=q, db2=db2, hs2=hs2, A=A, kt=kt:
                                    p.op("G", (lambda q=q, db2=db2, hs2=hs2,
                                               A=A: lambda qe: qe.dma_start(
                            out=whq_scr[q, hs2 * 8:(hs2 + 1) * 8, :,
                                        db2 * 128:(db2 + 1) * 128]
                            .rearrange("k p t -> p k t"),
                            in_=A.rearrange("p (k t) -> p k t", t=128)))(),
                            reads=[kt], writes=[("whqw", q, db2, hs2)],
                            dma=True)))
                        out.append((vth, tth))
                        cidx += 1
            return out

        def attention(b):
            p.barrier()
            for tt in range(NT):
                par = tt % 2
                xs = xslot(par)
                p.op("Y", (lambda b=b, tt=tt, d=xs: lambda q: q.dma_start(
                    out=d, in_=x_h[b, tt]))(), writes=[("xsl", par)], dma=True)
                ln_cols(("xsl", par), xs, E, wbfs(2), ["sq_s", ("wbf", 2)], LN_EPS)
                dummy_mm(3)
                ln_apply(["lnbf", ("wbf", 3)], wbfs(3), ("xsl", par), xs)
                dummy_mm(3)
                pairs = [(wbfs(3)[:, eb * 128:(eb + 1) * 128], ("wbf", 3),
                          ln0T(eb)[:, tt * 128:(tt + 1) * 128], ("ln0T", eb))
                         for eb in range(EB)]
                transpose_batch(pairs, id_bf[:], "id_bf")

            ln0_keys = [("ln0T", eb) for eb in range(EB)]

            def stream_w128(src_ap, par, wkey_f, wkey_b):
                p.op("Y", (lambda s=src_ap, d=xslot(par): lambda q: q.dma_start(
                    out=d.rearrange("p (e d) -> p e d", d=128),
                    in_=s.rearrange("e p d -> p e d")))(),
                    writes=[wkey_f, ("xsl", par)], dma=True)
                p.op("V", (lambda par=par: lambda v: v.tensor_copy(
                    wbfs(par), xslot(par)))(), reads=[("xsl", par)], writes=[wkey_b, ("wbf", par)])

            q4s = ovl1[:, 0:8192]
            kvs = ovl1[:, 8192:12288]

            def stream_q4(hg):
                for c in range(4):
                    par = c % 2
                    p.op("Y", (lambda hg=hg, c=c, par=par: lambda q: q.dma_start(
                        out=xslot(par).rearrange("p (e d) -> p e d", d=512),
                        in_=wq_h[hg, c * 4:(c + 1) * 4].rearrange("e p d -> p e d")))(),
                        writes=[("xsl", par)], dma=True)
                    p.op("V", (lambda c=c, par=par: lambda v: v.tensor_copy(
                        q4s[:, c * 2048:(c + 1) * 2048], xslot(par)))(),
                        reads=[("xsl", par)], writes=[("q4b", c)])

            def stream_kv():
                for c in range(2):
                    par = c % 2
                    p.op("Y", (lambda c=c, par=par: lambda q: q.dma_start(
                        out=xslot(par).rearrange("p (e d) -> p e d", d=256),
                        in_=wkv_h[c * 8:(c + 1) * 8].rearrange("e p d -> p e d")))(),
                        writes=[("xsl", par)], dma=True)
                    p.op("V", (lambda c=c, par=par: lambda v: v.tensor_copy(
                        kvs[:, c * 2048:(c + 1) * 2048], xslot(par)))(),
                        reads=[("xsl", par)], writes=[("kvb",)])

            def q4_mm(hg, tt, width, rhs_ap, rkeys):
                qpar = tt % 2
                qps = (o_psA if qpar == 0 else o_psB)[:, 0:width]

                def mk_q(tt=tt, qps=qps, rhs_ap=rhs_ap, width=width):
                    def f(t):
                        ins = None
                        for eb in range(EB):
                            ins = t.matmul(qps,
                                           ln0T(eb)[:, tt * 128:(tt + 1) * 128],
                                           rhs_ap[:, eb * width:(eb + 1) * width],
                                           start=(eb == 0), stop=(eb == EB - 1))
                        return ins
                    return f
                p.op("T", mk_q(), reads=rkeys + ln0_keys, writes=[("qps", qpar)])
                return qps, qpar

            def head_post(h, tt, qps, qpar, hsub, kind):
                if kind == "v":
                    p.op("S", (lambda qps=qps, tt=tt, hsub=hsub: lambda s: s.copy(
                        vp[:, tt * 128:(tt + 1) * 128],
                        qps[:, hsub * 128:(hsub + 1) * 128]))(),
                        reads=[("qps", qpar)], writes=[("vp", tt)])
                    return
                p.op("S", (lambda qps=qps, qpar=qpar, hsub=hsub: lambda s: s.copy(
                    rsc[:, qpar * 128:(qpar + 1) * 128],
                    qps[:, hsub * 128:(hsub + 1) * 128]))(),
                    reads=[("qps", qpar)], writes=[("qbf", qpar)])
                if kind == "q":
                    ol = rsc[:, 256 + qpar * 128:256 + qpar * 128 + 64]
                    oh = rsc[:, 256 + qpar * 128 + 64:256 + qpar * 128 + 128]
                    rope_tile(("qbf", qpar), tt, ol, oh, ("o1", qpar), ("o2", qpar),
                              par=qpar)
                    p.op("T", (lambda qpar=qpar, hsub=hsub: lambda t: t.transpose(
                        tr_ps[:, hsub * 128:(hsub + 1) * 128],
                        rsc[:, 256 + qpar * 128:256 + qpar * 128 + 128],
                        id_bf[:]))(),
                        reads=[("o1", qpar), ("o2", qpar), "id_bf"],
                        writes=[("trq", hsub)])
                    if hsub == 3:
                        p.op("V", (lambda h=h, tt=tt: lambda v: v.tensor_copy(
                            bigB[:, ((h // 4) * 16 + tt) * 512:
                                 ((h // 4) * 16 + tt) * 512 + 512],
                            tr_ps[:, 0:512]))(),
                            reads=[("trq", k2) for k2 in range(4)],
                            writes=[("rqT", h - 3), ("rqT", h - 2),
                                    ("rqT", h - 1), ("rqT", h)] + ["tr_ps"])
                else:
                    ol = rkp[:, tt * 128:tt * 128 + 64]
                    oh = rkp[:, tt * 128 + 64:(tt + 1) * 128]
                    rope_tile(("qbf", qpar), tt, ol, oh, ("rkp", tt), ("rkp", tt),
                              par=qpar)

            q4keys = [("q4b", c) for c in range(4)]
            pend = []
            for hg in range(4):
                stream_q4(hg)
                for tt in range(NT):
                    qps, qpar = q4_mm(hg, tt, 512, q4s, q4keys)
                    while pend:
                        head_post(*pend.pop(0))
                    for hsub in range(4):
                        pend.append((hg * 4 + hsub, tt, qps, qpar, hsub, "q"))
            stream_kv()
            for tt in range(NT):
                qps, qpar = q4_mm(0, tt, 256, kvs, [("kvb",)])
                while pend:
                    head_post(*pend.pop(0))
                pend.append((0, tt, qps, qpar, 0, "k"))
                pend.append((0, tt, qps, qpar, 1, "v"))
            while pend:
                head_post(*pend.pop(0))

            def proj32(wh, dstf, dkey):
                p.op("Y", (lambda s=wh: lambda q: q.dma_start(
                    out=xslot(0)[:, 0:512].rearrange("p (e c) -> p e c", c=32),
                    in_=s.rearrange("e p c -> p e c")))(),
                    writes=["wsmf", ("xsl", 0)], dma=True)
                p.op("V", lambda v: v.tensor_copy(wbfs(2)[:, 0:512], xslot(0)[:, 0:512]),
                     reads=[("xsl", 0)], writes=["wsmb", ("wbf", 2)])
                for tc in range(4):
                    def mk_g(tc=tc):
                        def f(t):
                            ins = None
                            for eb in range(EB):
                                ins = t.matmul(
                                    s_ps[:32, tc * 512:(tc + 1) * 512],
                                    wbfs(2)[:, eb * 32:(eb + 1) * 32],
                                    ln0T(eb)[:, tc * 512:(tc + 1) * 512],
                                    start=(eb == 0), stop=(eb == EB - 1))
                            return ins
                        return f
                    p.op("T", mk_g(), reads=[("wbf", 2)] + ln0_keys, writes=[("gps", tc)])
                def mk_ev(dstf=dstf):
                    def f(v):
                        return v.tensor_copy(dstf, s_ps[:32, :])
                    return f
                p.op("V", mk_ev(), reads=[("gps", tc2) for tc2 in range(4)] +
                     [("q4b", c) for c in range(4)] + [("kvb",)],
                     writes=[dkey])

            proj32(wg_h, cB, "cB")


            p.op("S", lambda s: s.activation(cC, cB, AF.Abs), reads=["cB"], writes=["cC"])
            p.op("S", lambda s: s.activation(cC, cC, AF.Exp, scale=-1.0),
                 reads=["cC"], writes=["cC"])
            p.op("S", lambda s: s.activation(cC, cC, AF.Ln, bias=1.0),
                 reads=["cC"], writes=["cC"])
            p.op("V", lambda v: v.tensor_scalar(cB, cB, 0.0, None, ALU.min),
                 reads=["cB"], writes=["cB"])
            p.op("V", lambda v: v.tensor_sub(cB, cB, cC), reads=["cC", "cB"],
                 writes=["cB"])
            p.op("V", lambda v: v.memset(cA, 1.0), writes=["cA"])
            p.op("V", lambda v: v.tensor_tensor_scan(cC, cA, cB, 0.0, ALU.mult, ALU.add),
                 reads=["cA", "cB"], writes=["cC"])
            p.op("V", lambda v: v.tensor_sub(cC, cC, cB), reads=["cB", "cC"],
                 writes=["cC"])
            p.op("S", lambda s: s.activation(cC, cC, AF.Exp), reads=["cC"], writes=["cC"])
            proj32(wkvh_h, cA, "cA")
            p.op("V", lambda v: v.tensor_mul(cA, cA, cC), reads=["cA", "cC"],
                 writes=["cA"])
            p.op("V", lambda v: v.memset(cB, 1.0), writes=["cB"])
            p.op("V", lambda v: v.tensor_tensor_scan(cC, cB, cA, 0.0, ALU.mult, ALU.add),
                 reads=["cA", "cB"], writes=["cC"])
            cCbf = ovl1[:32, 4096:6144]
            p.op("V", lambda v: v.tensor_copy(cCbf, cC), reads=["cC"], writes=["cCbf"])
            def mk_ct(t):
                ins = None
                for k in range(NT):
                    ins = t.transpose(tr_ps[:, k * 32:(k + 1) * 32],
                                      cCbf[:, k * 128:(k + 1) * 128], id_bf[0:32, 0:32])
                return ins
            p.op("T", mk_ct, reads=["cCbf", "id_bf"], writes=["tr_ps"])
            p.op("V", lambda v: v.tensor_copy(ctok[:], tr_ps[:, 0:512]),
                 reads=["tr_ps"], writes=["ctok"])
            return ln0_keys

        def p5_attention_core(b):
            p.barrier()
            khTp = lambda hp: ovl1[:, (0 if hp == 0 else 6144):(2048 if hp == 0 else 8192)]
            vhp = lambda hp: ovl1[:, (2048 if hp == 0 else 8192):(4096 if hp == 0 else 10240)]
            pTp = lambda ip: ovl1[:, (4096 if ip == 0 else 10240):(6144 if ip == 0 else 12288)]
            kst5 = wbfs(2)
            lc5 = lambda ip: cols[:, 2 + ip:3 + ip]
            linv5 = lambda ip: cols[:, 10 + ip:11 + ip]

            def k_build(h):
                hp = h % 2
                for tt in range(NT):
                    kc = ctok[:, tt * 32 + h:tt * 32 + h + 1]
                    p.op("V", (lambda tt=tt, kc=kc: lambda v: v.tensor_scalar(
                        kst5[:, tt * 128:(tt + 1) * 128],
                        rkp[:, tt * 128:(tt + 1) * 128],
                        kc, None, ALU.mult))(),
                        reads=[("rkp", tt), "ctok"], writes=[("kst", tt)])
                pairs = [(kst5[:, tt * 128:(tt + 1) * 128], ("kst", tt),
                          khTp(hp)[:, tt * 128:(tt + 1) * 128], ("khT", hp, tt))
                         for tt in range(NT)]
                transpose_batch(pairs, id_bf[:], "id_bf")
                for tt in range(NT):
                    vc = ctok[:, tt * 32 + 16 + h:tt * 32 + 17 + h]
                    p.op("V", (lambda tt=tt, vc=vc, hp=hp: lambda v: v.tensor_scalar(
                        vhp(hp)[:, tt * 128:(tt + 1) * 128],
                        vp[:, tt * 128:(tt + 1) * 128],
                        vc, None, ALU.mult))(),
                        reads=[("vp", tt), "ctok"], writes=[("vh", hp, tt)])

            def p5_base(i):
                return 1024 * (i % 2) if i <= 7 else 0

            def p5_mainT(h, i):
                hp = h % 2
                rng = (i + 1) * 128
                base = p5_base(i)
                kh_keys = [("khT", hp, tt) for tt in range(NT)]
                dlo = (i * 128 // 512) * 512
                dw = min(512, rng - dlo)
                p.op("T", (lambda i=i, h=h, hp=hp, dlo=dlo, dw=dw, base=base:
                           lambda t: t.matmul(
                    s_ps[:, base + dlo:base + dlo + dw],
                    rq_tile(h, i),
                    khTp(hp)[:, dlo:dlo + dw], start=True, stop=True))(),
                    reads=kh_keys + [("rqT", h)],
                    writes=[("spb", (base + dlo) // 512)])
                c0 = 0
                nch = 0
                chunks = []
                while c0 < rng:
                    w = min(512, rng - c0)
                    if c0 != dlo:
                        p.op("T", (lambda i=i, h=h, hp=hp, c0=c0, w=w, base=base:
                                   lambda t: t.matmul(
                            s_ps[:, base + c0:base + c0 + w],
                            rq_tile(h, i),
                            khTp(hp)[:, c0:c0 + w], start=True, stop=True))(),
                            reads=kh_keys + [("rqT", h)],
                            writes=[("spb", (base + c0) // 512)])
                        chunks.append((c0, w, nch))
                        nch += 1
                    c0 += w
                return chunks, dlo, dw, nch

            def p5_mainV(h, i, chunks, dlo, dw, nch):
                ip = i % 2
                base = p5_base(i)
                if i <= 7:
                    mcol = 12 + 2 * ip
                else:
                    mcol = 12
                mxi = cols[:, 0 + ip:1 + ip]
                bsi = cols[:, 18 + ip:19 + ip]
                p.op("V", (lambda i=i, base=base: lambda v: v.tensor_add(
                    s_ps[:, base + i * 128:base + (i + 1) * 128],
                    s_ps[:, base + i * 128:base + (i + 1) * 128], mask_s[:]))(),
                    reads=["mask", ("spb", (base + dlo) // 512)],
                    writes=[("spb", (base + dlo) // 512)])
                p.op("V", (lambda dlo=dlo, dw=dw, base=base, mcol=mcol:
                           lambda v: v.reduce_max(
                    cols[:, mcol:mcol + 1], s_ps[:, base + dlo:base + dlo + dw],
                    axis=AX.X))(),
                    reads=[("spb", (base + dlo) // 512)], writes=[("mxcol", mcol)])
                for (c0, w, k) in chunks:
                    p.op("V", (lambda c0=c0, w=w, k=k, base=base, mcol=mcol:
                               lambda v: v.reduce_max(
                        cols[:, mcol + 1 + k:mcol + 2 + k],
                        s_ps[:, base + c0:base + c0 + w], axis=AX.X))(),
                        reads=[("spb", (base + c0) // 512)],
                        writes=[("mxcol", mcol + 1 + k)])
                mx_keys = [("mxcol", mcol + k) for k in range(nch + 1)]
                p.op("V", (lambda nch=nch, mcol=mcol, mxi=mxi: lambda v: v.reduce_max(
                    mxi, cols[:, mcol:mcol + 1 + nch], axis=AX.X))(),
                    reads=mx_keys, writes=[("c_mx", ip)])
                p.op("V", (lambda mxi=mxi, bsi=bsi: lambda v: v.tensor_scalar(
                    bsi, mxi, -SCALE, None, ALU.mult))(),
                    reads=[("c_mx", ip)], writes=[("c_bias", ip)])

            def p5_mainS(h, i):
                ip = i % 2
                rng = (i + 1) * 128
                base = p5_base(i)
                bsi = cols[:, 18 + ip:19 + ip]
                sps_keys = [("spb", (base // 512) + k)
                            for k in range((rng + 511) // 512)]
                p.op("S", (lambda rng=rng, ip=ip, base=base, bsi=bsi:
                           lambda s: s.activation(
                    wbfs(ip)[:, 0:rng], s_ps[:, base:base + rng], AF.Exp,
                    bias=bsi, scale=SCALE, accum_out=lc5(ip)))(),
                    reads=sps_keys + [("c_bias", ip)],
                    writes=[("wbf", ip), ("c_l", ip)] + sps_keys)
                p.op("V", (lambda ip=ip: lambda v: v.reciprocal(linv5(ip), lc5(ip)))(),
                     reads=[("c_l", ip)], writes=[("c_linv", ip)])

            def p5_partA(h, i):
                ip = i % 2
                pairs = [(wbfs(ip)[:, mb * 128:(mb + 1) * 128], ("wbf", ip),
                          pTp(ip)[:, mb * 128:(mb + 1) * 128], ("pT", ip, mb))
                         for mb in range(i + 1)]
                transpose_batch(pairs, id_bf[:], "id_bf")

            def p5_partB(h, i):
                hp = h % 2
                ip = i % 2
                vh_keys = [("vh", hp, tt) for tt in range(NT)]
                ops_t = (o_psA if ip == 0 else o_psB)[:, 0:128]

                def mk_o(i=i, opst=ops_t, ip=ip, hp=hp):
                    def f(t):
                        ins = None
                        for mb in range(i + 1):
                            ins = t.matmul(opst,
                                           pTp(ip)[:, mb * 128:(mb + 1) * 128],
                                           vhp(hp)[:, mb * 128:(mb + 1) * 128],
                                           start=(mb == 0), stop=(mb == i))
                        return ins
                    return f
                p.op("T", mk_o(), reads=[("pT", ip, mb) for mb in range(i + 1)] + vh_keys,
                     writes=[("qps", ip)])
                p.op("V", (lambda opst=ops_t, ip=ip: lambda v: v.tensor_scalar(
                    rsc[:, ip * 128:(ip + 1) * 128], opst, linv5(ip), None, ALU.mult))(),
                    reads=[("qps", ip), ("c_linv", ip)], writes=[("o_tok", ip)])
                p.op("T", (lambda ip=ip: lambda t: t.transpose(
                    tr_ps[:, 0:128], rsc[:, ip * 128:(ip + 1) * 128], id_bf[:]))(),
                    reads=[("o_tok", ip), "id_bf"], writes=["tr_ps"])
                p.op("V", (lambda h=h, i=i: lambda v: v.tensor_copy(
                    oT(h)[:, i * 128:(i + 1) * 128], tr_ps[:, 0:128]))(),
                    reads=["tr_ps"], writes=[("oT", h)])

            k_build(0)
            pend5 = None
            for h in range(H):
                for i in range(NT):
                    info = p5_mainT(h, i)
                    p5_mainV(h, i, *info)
                    if pend5 is not None:
                        p5_partA(*pend5)
                        p5_partB(*pend5)
                    p5_mainS(h, i)
                    pend5 = (h, i)
                    if i == 8 and h + 1 < H:
                        k_build(h + 1)
            p5_partA(*pend5)
            p5_partB(*pend5)

        def p6_out_proj(b):
            p.barrier()
            for hdb in range(EB):
                par = hdb % 2
                p.op("Y", (lambda hdb=hdb, par=par: lambda q: q.dma_start(
                    out=xslot(par), in_=wout_h[hdb]))(),
                    writes=[("wof", par), ("xsl", par)], dma=True)
                p.op("V", (lambda hdb=hdb, par=par: lambda v: v.tensor_copy(
                    woutb(hdb), xslot(par)))(),
                    reads=[("xsl", par)], writes=[("wob", hdb)])
            wob_keys = [("wob", hdb) for hdb in range(EB)]
            oT_keys = [("oT", h) for h in range(H)]
            wbf_f32_p6 = wbf[:].bitcast(F32)
            aof = lambda par: wbf_f32_p6[:, par * 2048:(par + 1) * 2048]
            sq6 = ovl1[:, 0:2048]
            ln1b6 = lambda par: ovl1[:, 2048 + par * 2048:4096 + par * 2048]
            stg6 = lambda par: ovl1[:, 6144 + par * 2048:8192 + par * 2048]

            def p6_mm(tt):
                par = tt % 2
                p.op("Y", (lambda b=b, tt=tt, par=par: lambda q: q.dma_start(
                    out=xslot(par), in_=x_h[b, tt]))(),
                    writes=[("xsl", par)], dma=True)
                def mk_ao(tt=tt):
                    def f(t):
                        ins = None
                        for hdb in range(EB):
                            for ec in range(4):
                                ins = t.matmul(
                                    s_ps[:, ec * 512:(ec + 1) * 512],
                                    oT(hdb)[:, tt * 128:(tt + 1) * 128],
                                    woutb(hdb)[:, ec * 512:(ec + 1) * 512],
                                    start=(hdb == 0), stop=(hdb == EB - 1))
                        return ins
                    return f
                p.op("T", mk_ao(), reads=wob_keys + oT_keys, writes=["s_ps"])
                p.op("S", (lambda par=par: lambda s: s.activation(
                    aof(par), s_ps[:, :], AF.Copy,
                    accum_out=cols[:, 16 + par:17 + par]))(),
                    reads=["s_ps"], writes=[("aof", par), ("c_m1p", par)])
                dummy_mm(4)

            def p6_postV(tt):
                par = tt % 2
                p.op("S", (lambda par=par: lambda s: s.activation(
                    sq6, aof(par), AF.Square, accum_out=m2c))(),
                    reads=[("aof", par)], writes=["sq_s", "c_m2"])
                p.op("V", (lambda par=par: lambda v: v.tensor_scalar(
                    m1c, cols[:, 16 + par:17 + par], 1.0 / E, None, ALU.mult))(),
                    reads=[("c_m1p", par)], writes=["c_m1"])
                p.op("V", lambda v: v.tensor_scalar(
                    m2c, m2c, 1.0 / E, None, ALU.mult), reads=["c_m2"], writes=["c_m2"])
                p.op("V", lambda v: v.tensor_mul(t0c, m1c, m1c),
                     reads=["c_m1"], writes=["c_t0"])
                p.op("V", lambda v: v.tensor_sub(vrc, m2c, t0c),
                     reads=["c_m2", "c_t0"], writes=["c_vr"])
                p.op("V", lambda v: v.tensor_scalar(
                    vrc, vrc, LN_EPS, None, ALU.add), reads=["c_vr"], writes=["c_vr"])
                p.op("S", lambda s: s.activation(rsdc, vrc, AF.Ln),
                     reads=["c_vr"], writes=["c_rsd"])
                p.op("S", lambda s: s.activation(rsdc, rsdc, AF.Exp, scale=-0.5),
                     reads=["c_rsd"], writes=["c_rsd"])
                ln_apply([("aof", par)], aof(par), ("aof", par), aof(par))
                p.op("V", (lambda par=par: lambda v: v.tensor_add(
                    xslot(par), xslot(par), aof(par)))(),
                    reads=[("xsl", par), ("aof", par)], writes=[("xsl", par)])
                p.op("G", (lambda b=b, tt=tt, par=par: lambda q: q.dma_start(
                    out=x1_scr[b, tt], in_=xslot(par)))(),
                    reads=[("xsl", par)], writes=[("x1w", tt)], dma=True)
                ln_cols(("xsl", par), xslot(par), E, sq6, ["sq_s"], LN_EPS)
                ln_apply([("ln1b", par)], ln1b6(par), ("xsl", par), xslot(par))

            def p6_postT(tt):
                par = tt % 2
                pairs = [(ln1b6(par)[:, eb * 128:(eb + 1) * 128], ("ln1b", par),
                          stg6(par)[:, eb * 128:(eb + 1) * 128], ("l1st", par))
                         for eb in range(EB)]
                transpose_batch(pairs, id_bf[:], "id_bf")
                p.op("G", (lambda b=b, tt=tt, par=par: lambda q: q.dma_start(
                    out=ln1T_scr[b].rearrange("e p t -> p e t")[:, :, tt * 128:(tt + 1) * 128],
                    in_=stg6(par).rearrange("p (e t) -> p e t", t=128)))(),
                    reads=[("l1st", par)],
                    writes=[("l1w", tt)], dma=True)

            for tt in range(NT):
                p6_mm(tt)
                if tt >= 2:
                    p6_postT(tt - 2)
                if tt >= 1:
                    p6_postV(tt - 1)
            p6_postT(NT - 2)
            p6_postV(NT - 1)
            p6_postT(NT - 1)

        def f1_gateup_both():
            p.barrier()
            p.op("V", lambda v: v.memset(ssq_sb[:], 0.0), writes=["ssq"])
            for b2 in range(B):
                for eb in range(EB):
                    dst = (bigA if b2 == 0 else bigB)[:, eb * N:(eb + 1) * N]
                    p.op("Y", (lambda b2=b2, eb=eb, dst=dst: lambda q: q.dma_start(
                        out=dst, in_=ln1T_scr[b2, eb]))(),
                        writes=[("ln1T", b2, eb)], dma=True)
            l1_keys = [("ln1T", b2, eb) for b2 in range(B) for eb in range(EB)]
            wbf_f32 = wbf[:].bitcast(F32)
            gate_f = wbf_f32[:, 0:2048]
            upf_f = wbf_f32[:, 2048:4096]

            def gT_ap(par, eb):
                if par == 0:
                    return ovl1[:, 6144 + eb * 128:6144 + (eb + 1) * 128]
                return ovl1[:, 8192 + eb * 128:8192 + (eb + 1) * 128]

            def uT_ap(par, eb):
                if par == 0:
                    return ovl1[:, 10240 + eb * 128:10240 + (eb + 1) * 128]
                if eb < 8:
                    return cos_s[:, eb * 128:(eb + 1) * 128]
                return sin_s[:, (eb - 8) * 128:(eb - 7) * 128]

            def qchain2(src_ap, src_key, Bq, kb):
                th = []
                th.append((lambda src_ap=src_ap, src_key=src_key: p.op(
                    "S", (lambda a=src_ap: lambda s: s.copy(qa, a))(),
                    reads=[src_key], writes=["q_a"])))
                th.append(lambda: p.op("V", lambda v: v.tensor_reduce(
                    qred, qa[:].rearrange("p (g s) -> p g s", s=64), axis=AX.X,
                    op=ALU.add, apply_absolute_value=True),
                    reads=["q_a"], writes=["q_red"]))
                th.append(lambda: p.op("V", lambda v: v.tensor_scalar(
                    qsc_bf, qred, 1.0 / 64, 1e-8, ALU.mult, ALU.max),
                    reads=["q_red"], writes=["q_sc"]))
                th.append(lambda: p.op("V", lambda v: v.tensor_scalar(
                    qthr, qsc_bf, QTHR, None, ALU.mult),
                    reads=["q_sc"], writes=["q_thr"]))
                th.append((lambda Bq=Bq, kb=kb: p.op(
                    "S", lambda s: s.sign(Bq, qa), reads=["q_a"], writes=[kb])))
                th.append(lambda: p.op("S", lambda s: s.activation(
                    qa, qa, AF.Abs), reads=["q_a"], writes=["q_a"]))
                th.append(lambda: p.op("V", lambda v: v.tensor_tensor(
                    qa[:].rearrange("p (g s) -> p g s", s=64),
                    qa[:].rearrange("p (g s) -> p g s", s=64),
                    qthr[:].rearrange("p (g o) -> p g o", o=1)
                    .broadcast_to([128, 32, 64]),
                    ALU.is_gt), reads=["q_a", "q_thr"], writes=["q_a"]))
                th.append((lambda Bq=Bq, kb=kb: p.op(
                    "V", lambda v: v.tensor_mul(Bq, Bq, qa),
                    reads=["q_a", kb], writes=[kb])))
                th.append((lambda Bq=Bq, kb=kb: p.op(
                    "V", lambda v: v.tensor_tensor(
                        Bq.rearrange("p (g s) -> p g s", s=64),
                        Bq.rearrange("p (g s) -> p g s", s=64),
                        qsc_bf[:].rearrange("p (g o) -> p g o", o=1)
                        .broadcast_to([128, 32, 64]),
                        ALU.mult), reads=[kb, "q_sc"], writes=[kb])))
                return th

            def f1_quant_thunks(j):
                par = j % 2
                th = []
                th.append(lambda: p.op("Y", (lambda j=j: lambda q: q.dma_start(
                    out=gate_f, in_=wgu_h[j * 128:(j + 1) * 128, :]))(),
                    writes=[("gf",)], dma=True))
                th.append(lambda: p.op("Y", (lambda j=j: lambda q: q.dma_start(
                    out=upf_f, in_=wgu_h[HF + j * 128:HF + (j + 1) * 128, :]))(),
                    writes=[("uf",)], dma=True))
                th += qchain2(gate_f, ("gf",), qsg[:], "q_bg")
                th += qchain2(upf_f, ("uf",), qm[:], "q_bu")
                pairs = [(qsg[:, eb * 128:(eb + 1) * 128], "q_bg",
                          gT_ap(par, eb), ("gT", par)) for eb in range(EB)]
                th += transpose_thunks(pairs, id_bf[:], "id_bf")
                pairs = [(qm[:, eb * 128:(eb + 1) * 128], "q_bu",
                          uT_ap(par, eb), ("uT", par)) for eb in range(EB)]
                th += transpose_thunks(pairs, id_bf[:], "id_bf")
                return th

            pendW = []
            prevT = None
            for (vth, tth) in wp_chunks():
                pendW += vth
                if prevT:
                    pendW += prevT
                prevT = tth
            pendW += prevT
            pending = f1_quant_thunks(0)
            while pending:
                pending.pop(0)()
            for j in range(64):
                par = j % 2
                pending = f1_quant_thunks(j + 1) if j + 1 < 64 else []
                for b2 in range(B):
                    ln1Tb = (bigA if b2 == 0 else bigB)
                    for tc in range(4):
                        g3 = (j * 8 + b2 * 4 + tc) % 3
                        gps = [o_psA[:, 0:512], s_ps[:, 0:512],
                               s_ps[:, 1024:1536]][g3]
                        ups = [o_psB[:, 0:512], s_ps[:, 512:1024],
                               dmy_ps[:, 0:512]][g3]
                        gk = ("gups", g3)
                        uk = ("uups", g3)

                        def mk_mm(dst, wap_fn, tc=tc, b2=b2, ln1Tb=ln1Tb, par=par):
                            def f(t):
                                ins = None
                                for eb2 in range(EB):
                                    ins = t.matmul(
                                        dst, wap_fn(par, eb2),
                                        ln1Tb[:, eb2 * N + tc * 512:eb2 * N + (tc + 1) * 512],
                                        start=(eb2 == 0), stop=(eb2 == EB - 1))
                                return ins
                            return f
                        p.op("T", mk_mm(gps, gT_ap), reads=[("gT", par)] + l1_keys,
                             writes=[gk])
                        p.op("T", mk_mm(ups, uT_ap), reads=[("uT", par)] + l1_keys,
                             writes=[uk])
                        p.op("S", (lambda gps=gps: lambda s: s.activation(
                            wf32[:, 0:512], gps, AF.Silu))(),
                            reads=[gk], writes=["sgf"])
                        p.op("V", (lambda ups=ups, tc=tc, b2=b2: lambda v: v.tensor_mul(
                            hst(b2)[:, tc * 512:(tc + 1) * 512], wf32[:, 0:512], ups))(),
                            reads=["sgf", uk], writes=[("hb", b2, tc)])
                        p.op("S", (lambda tc=tc, b2=b2: lambda s: s.activation(
                            h2v[:, tc * 512:(tc + 1) * 512],
                            hst(b2)[:, tc * 512:(tc + 1) * 512], AF.Square))(),
                            reads=[("hb", b2, tc)], writes=[("h2", tc)])
                        for _ in range(3):
                            if pending:
                                pending.pop(0)()
                        for _ in range(4):
                            if pendW:
                                pendW.pop(0)()
                    def mk_ssq(b2=b2, j=j):
                        def f(t):
                            ins = None
                            for ttl in range(NT):
                                ins = t.matmul(s_ps[:, 1536 + ttl:1536 + ttl + 1],
                                               h2v[:, ttl * 128:(ttl + 1) * 128],
                                               ones_s[:], start=True, stop=True)
                            return ins
                        return f

                    def emit_ssq(b2=b2, j=j):
                        p.op("T", mk_ssq(b2, j),
                             reads=[("h2", tc2) for tc2 in range(4)] + ["ones"],
                             writes=["ssq_ps"])
                        p.op("V", (lambda b2=b2: lambda v: v.tensor_add(
                            ssq_sb[:, b2 * 16:(b2 + 1) * 16],
                            ssq_sb[:, b2 * 16:(b2 + 1) * 16],
                            s_ps[:, 1536:1552]))(),
                            reads=["ssq_ps", "ssq"], writes=["ssq"])
                    emit_ssq()
                    p.op("G", (lambda b2=b2, j=j, hpar=b2: lambda q: q.dma_start(
                        out=h_scr[b2, :, j].rearrange("q p t -> p q t"),
                        in_=hst(b2).rearrange("p (q t) -> p q t", t=128)))(),
                        reads=[("hb", b2, tc2) for tc2 in range(4)],
                        writes=[("hw", b2, j)], dma=True)
                while pending:
                    pending.pop(0)()
            while pendW:
                pendW.pop(0)()

        out_keys = []

        def f2_proj():
            p.barrier()
            p.op("V", lambda v: v.tensor_scalar(
                inv_sb[:], ssq_sb[:], 1.0 / HF, RMS_EPS, ALU.mult, ALU.add),
                reads=["ssq"], writes=["inv1"])
            p.op("S", lambda s: s.activation(inv_sb[:], inv_sb[:], AF.Ln),
                 reads=["inv1"], writes=["inv2"])
            p.op("S", lambda s: s.activation(inv_sb[:], inv_sb[:], AF.Exp, scale=-0.5),
                 reads=["inv2"], writes=["inv"])
            ht_keys_o = ["q_a", "q_sg", "q_m", "q_hat", ("gT", 0)]
            finsq = ovl1[:, 8192:10240]
            finln = ovl1[:, 10240:12288]

            def fin_tile(gt):
                bf, tf = gt // NT, gt % NT
                p.op("Y", (lambda bf=bf, tf=tf: lambda q: q.dma_start(
                    out=xslot(0), in_=ffn_scr[bf, tf]))(),
                    reads=[("fw", gt, 0), ("fw", gt, 1)],
                    writes=["ffs", ("xsl", 0)], dma=True)
                p.op("Y", (lambda bf=bf, tf=tf: lambda q: q.dma_start(
                    out=xslot(1), in_=x1_scr[bf, tf]))(),
                    writes=["x1s", ("xsl", 1)], dma=True)
                ln_cols("ffs", xslot(0), E, finsq, ["finsq"], LN_EPS)
                ln_apply(["finln"], finln, "ffs", xslot(0))
                p.op("V", lambda v: v.tensor_add(xslot(1), xslot(1), finln),
                     reads=["x1s", "finln"], writes=["x1s", ("xsl", 1)])
                k = ("ow", gt)
                p.op("G", (lambda bf=bf, tf=tf: lambda q: q.dma_start(
                    out=out_h[bf, tf], in_=xslot(1)))(),
                    reads=["x1s"], writes=[k], dma=True)
                out_keys.append(k)

            for dh in range(2):
                for dq2 in range(2):
                    X = wptq(dq2)
                    for grp in range(8):
                        p.op("Y", (lambda dh=dh, dq2=dq2, grp=grp, X=X:
                                   lambda qe: qe.dma_start(
                            out=X[:, grp * 4096:(grp + 1) * 4096]
                            .rearrange("p (k t) -> p k t", t=512),
                            in_=whq_scr[2 * dh + dq2, grp * 8:(grp + 1) * 8]
                            .rearrange("k p t -> p k t")))(),
                            writes=[("wq", dq2, grp)], dma=True)
                for gt in range(32):
                    b, tt = gt // NT, gt % NT
                    odd = gt % 2 == 1
                    htbuf = ovl1[:, 0:8192] if odd else wbf[:]
                    htk = ht_keys_o if odd else [("wbf", i4) for i4 in range(4)]
                    p.op("Y", (lambda b=b, tt=tt, htbuf=htbuf: lambda q: q.dma_start(
                        out=htbuf.rearrange("p (q t) -> p q t", t=128),
                        in_=h_scr[b, tt].rearrange("q p t -> p q t")))(),
                        writes=["htile%d" % (gt % 2)] + htk, dma=True)
                    for dq2 in range(2):
                        X = wptq(dq2)
                        yps = o_psA if dq2 == 0 else o_psB
                        for grp in range(8):
                            def mk_y(X=X, yps=yps, htbuf=htbuf, grp=grp):
                                def f(t):
                                    ins = None
                                    for j in range(grp * 8, (grp + 1) * 8):
                                        ins = t.matmul(
                                            yps[:, 0:512],
                                            htbuf[:, j * 128:(j + 1) * 128],
                                            X[:, j * 512:(j + 1) * 512],
                                            start=(j == 0), stop=(j == 63))
                                    return ins
                                return f
                            p.op("T", mk_y(),
                                 reads=["htile%d" % (gt % 2)] + htk +
                                 [("wq", dq2, grp)],
                                 writes=[("yps", dq2)])
                        p.op("V", (lambda gt=gt, dq2=dq2, yps=yps: lambda v: v.tensor_scalar(
                            ovl2[:, 0:4096].bitcast(F32)[:, dq2 * 512:(dq2 + 1) * 512],
                            yps[:, 0:512], inv_sb[:, gt:gt + 1], None, ALU.mult))(),
                            reads=[("yps", dq2), "inv"], writes=[("yf", dq2)])
                    p.op("G", (lambda b=b, tt=tt, dh=dh: lambda q: q.dma_start(
                        out=ffn_scr[b, tt][:, dh * 1024:(dh + 1) * 1024],
                        in_=ovl2[:, 0:4096].bitcast(F32)[:, 0:1024]))(),
                        reads=[("yf", 0), ("yf", 1)], writes=[("fw", gt, dh)], dma=True)
                    if dh == 1 and gt >= 2:
                        fin_tile(gt - 2)
                if dh == 1:
                    fin_tile(30)
                    fin_tile(31)

        def final_ln():
            p.op("Y", None, reads=out_keys)

        stage = 0
        for b in range(B):
            if upto >= 1 + b * 3:
                attention(b)
                if debug and b == 0:
                    p.barrier()
                    p.op("G", lambda q: q.dma_start(out=tap_ctok, in_=ctok[:]), reads=["ctok"], dma=True)
                    p.op("G", lambda q: q.dma_start(out=tap_cum, in_=cum[:, :]), dma=True)
                    p.op("G", lambda q: q.dma_start(out=tap_rkp, in_=rkp), dma=True)
                    p.op("G", lambda q: q.dma_start(out=tap_vp, in_=vp), dma=True)
                    p.op("G", lambda q: q.dma_start(out=tap_rqT, in_=bigB[:]), dma=True)
                    p.barrier()
            if upto >= 2 + b * 3:
                p5_attention_core(b)
                if debug and b == 0:
                    p.barrier()
                    p.op("G", lambda q: q.dma_start(out=tap_oT, in_=bigA[:]), dma=True)
                    p.barrier()
            if upto >= 3 + b * 3:
                p6_out_proj(b)
        if upto >= 7:
            f1_gateup_both()
        if upto >= 8:
            f2_proj()
        if upto >= 9:
            final_ln()

        p.emit(block, sems)
    return nc


_NC_CACHE = {}


def _get_nc(debug=False):
    if debug not in _NC_CACHE:
        _NC_CACHE[debug] = build_nc(debug)
    return _NC_CACHE[debug]


def _host_prep(d):
    Wq = np.asarray(d["L0_attn_Wq"], np.float32)
    Wkv = np.asarray(d["L0_attn_Wkv"], np.float32)
    Wkvh = np.asarray(d["L0_attn_Wkvh"], np.float32)
    Wg = np.asarray(d["L0_attn_Wg"], np.float32)
    Wout = np.asarray(d["L0_attn_Wout"], np.float32)
    wq = np.ascontiguousarray(Wq.T.reshape(EB, 128, 4, 512).transpose(2, 0, 1, 3))
    wkv = np.ascontiguousarray(Wkv.T.reshape(EB, 128, 256))
    wkvh = np.ascontiguousarray(Wkvh.T.reshape(EB, 128, 32))
    wg = np.ascontiguousarray(Wg.T.reshape(EB, 128, 32))
    wout = np.ascontiguousarray(Wout.T.reshape(EB, 128, E))
    ident = np.eye(128, dtype=BFNP)
    mask = np.where(np.arange(128)[None, :] <= np.arange(128)[:, None],
                    np.float32(0.0), np.float32(-1e30)).astype(np.float32)
    pos = np.arange(N, dtype=np.float64)
    inv = 10000.0 ** (-np.arange(0, D, 2, dtype=np.float64) / D)
    ang = pos[:, None] * inv[None, :]
    cos_tok = np.ascontiguousarray(np.cos(ang).astype(np.float32).astype(BFNP)
                                   .reshape(NT, 128, 64).transpose(1, 0, 2)
                                   .reshape(128, NT * 64))
    sin_tok = np.ascontiguousarray(np.sin(ang).astype(np.float32).astype(BFNP)
                                   .reshape(NT, 128, 64).transpose(1, 0, 2)
                                   .reshape(128, NT * 64))
    ones_col = np.ones((128, 1), dtype=BFNP)
    com = dict(wq=wq, wkv=wkv, wkvh=wkvh, wg=wg, wout=wout,
               wgu=np.ascontiguousarray(np.asarray(d["L0_ffn_w_gate_up"], np.float32)),
               wp=np.ascontiguousarray(np.asarray(d["L0_ffn_w_proj"], np.float32)),
               ident=ident, mask=mask, cos_tok=cos_tok, sin_tok=sin_tok,
               ones_col=ones_col)
    return com


def kernel(**inputs):
    x = np.asarray(inputs["L0_attn_x"], np.float32)
    com = _host_prep(inputs)
    nc = _get_nc(False)
    in_maps = []
    for c in range(8):
        m = dict(com)
        m["x"] = np.ascontiguousarray(
            x[c * B:(c + 1) * B].reshape(B, NT, 128, E))
        in_maps.append(m)
    res = run_bass_kernel_spmd(nc, in_maps, core_ids=list(range(8)))
    outs = [r["out"].reshape(B, N, E) for r in res.results]
    return np.concatenate(outs, axis=0)


# version 32
# speedup vs baseline: 1.0063x; 1.0063x over previous
import numpy as np
import ml_dtypes
import concourse.bass as bass
import concourse.mybir as mybir
from concourse.bass_utils import run_bass_kernel_spmd
from contextlib import ExitStack

F32 = mybir.dt.float32
BF16 = mybir.dt.bfloat16
AF = mybir.ActivationFunctionType
ALU = mybir.AluOpType
AX = mybir.AxisListType
BFNP = ml_dtypes.bfloat16

B = 2
N = 2048
E = 2048
H = 16
D = 128
HF = 8192
NT = N // 128
EB = E // 128
SCALE = 1.0 / float(np.sqrt(np.float64(D)))
LN_EPS = 1e-5
RMS_EPS = 1.1920928955078125e-07
QTHR = 0.5009765625

ENGS = ("T", "V", "S", "G", "Y")
DMA_SLOTS = {"Y": 8, "G": 6}


class Prog:
    def __init__(self):
        self.ops = []
        self.sem_ids = list(ENGS)
        for e, k in DMA_SLOTS.items():
            self.sem_ids += [(e, s) for s in range(k)]
        self.cnt = {sid: 0 for sid in self.sem_ids}
        self.dmac = {e: 0 for e in DMA_SLOTS}
        self.buf = {}

    def op(self, eng, fn, reads=(), writes=(), dma=False):
        waits = {}

        def add(tok):
            if tok is None:
                return
            sid, c2 = tok
            if waits.get(sid, 0) < c2:
                waits[sid] = c2

        for k in reads:
            b = self.buf.get(k)
            if b:
                add(b[0])
        for k in writes:
            b = self.buf.get(k)
            if b:
                add(b[0])
                for t in b[1]:
                    add(t)
        if dma:
            slot = (eng, self.dmac[eng] % DMA_SLOTS[eng])
            self.dmac[eng] += 1
            self.cnt[slot] += 16
            tok = (slot, self.cnt[slot])
            sid = slot
            inc = 16
        else:
            if fn is None:
                inc = 0
            else:
                inc = 1
            self.cnt[eng] += inc
            tok = (eng, self.cnt[eng])
            sid = eng
        for k in reads:
            self.buf.setdefault(k, [None, []])[1].append(tok)
        for k in writes:
            self.buf[k] = [tok, []]
        self.ops.append((eng, fn, waits, inc, sid))
        return tok

    def barrier(self):
        snap = dict(self.cnt)
        for e in ENGS:
            waits = {sid: c for sid, c in snap.items() if c > 0 and sid != e}
            self.ops.append((e, None, waits, 0, e))

    def emit(self, block, sems):
        per = {e: [] for e in ENGS}
        for o in self.ops:
            per[o[0]].append(o)

        def mk_body(name):
            def f(eng):
                water = {}
                for (_, fn, waits, inc, sid) in per[name]:
                    for s in sorted(waits, key=repr):
                        v = waits[s]
                        if water.get(s, 0) < v:
                            eng.wait_ge(sems[s], v)
                            water[s] = v
                    if fn is not None:
                        ins = fn(eng)
                        if inc:
                            ins.then_inc(sems[sid], inc)
            return f

        block.tensor(mk_body("T"))
        block.vector(mk_body("V"))
        block.scalar(mk_body("S"))
        block.gpsimd(mk_body("G"))
        block.sync(mk_body("Y"))


def build_nc(debug=False, upto=9):
    nc = bass.Bass("TRN2", target_bir_lowering=False, debug=False)
    ap = lambda t: t.ap()
    scr_kind = "ExternalOutput" if debug else "Internal"

    x_h = ap(nc.dram_tensor("x", [B, NT, 128, E], F32, kind="ExternalInput"))
    wq_h = ap(nc.dram_tensor("wq", [4, EB, 128, 512], F32, kind="ExternalInput"))
    wkv_h = ap(nc.dram_tensor("wkv", [EB, 128, 256], F32, kind="ExternalInput"))
    wkvh_h = ap(nc.dram_tensor("wkvh", [EB, 128, 32], F32, kind="ExternalInput"))
    wg_h = ap(nc.dram_tensor("wg", [EB, 128, 32], F32, kind="ExternalInput"))
    wout_h = ap(nc.dram_tensor("wout", [EB, 128, E], F32, kind="ExternalInput"))
    wgu_h = ap(nc.dram_tensor("wgu", [2 * HF, E], F32, kind="ExternalInput"))
    wp_h = ap(nc.dram_tensor("wp", [E, HF], F32, kind="ExternalInput"))
    id_h = ap(nc.dram_tensor("ident", [128, 128], BF16, kind="ExternalInput"))
    mask_h = ap(nc.dram_tensor("mask", [128, 128], F32, kind="ExternalInput"))
    cos_h = ap(nc.dram_tensor("cos_tok", [128, NT * 64], BF16, kind="ExternalInput"))
    sin_h = ap(nc.dram_tensor("sin_tok", [128, NT * 64], BF16, kind="ExternalInput"))
    ones_h = ap(nc.dram_tensor("ones_col", [128, 1], BF16, kind="ExternalInput"))

    out_h = ap(nc.dram_tensor("out", [B, NT, 128, E], F32, kind="ExternalOutput"))

    x1_scr = ap(nc.dram_tensor("x1_scr", [B, NT, 128, E], F32, kind=scr_kind))
    whq_scr = ap(nc.dram_tensor("whq_scr", [4, 64, 128, 512], BF16, kind=scr_kind))
    ln1T_scr = ap(nc.dram_tensor("ln1T_scr", [B, EB, 128, N], BF16, kind=scr_kind))
    h_scr = ap(nc.dram_tensor("h_scr", [B, NT, 64, 128, 128], BF16, kind=scr_kind))
    ffn_scr = ap(nc.dram_tensor("ffn_scr", [B, NT, 128, E], F32, kind=scr_kind))
    if debug:
        tap_cum = ap(nc.dram_tensor("tap_cum", [32, 6144], F32, kind="ExternalOutput"))
        tap_ctok = ap(nc.dram_tensor("tap_ctok", [128, 512], F32, kind="ExternalOutput"))
        tap_rkp = ap(nc.dram_tensor("tap_rkp", [128, 2048], BF16, kind="ExternalOutput"))
        tap_vp = ap(nc.dram_tensor("tap_vp", [128, 2048], BF16, kind="ExternalOutput"))
        tap_rqT = ap(nc.dram_tensor("tap_rqT", [128, 32768], BF16, kind="ExternalOutput"))
        tap_oT = ap(nc.dram_tensor("tap_oT", [128, 32768], BF16, kind="ExternalOutput"))

    st_ = ExitStack()
    with st_:
        SB = lambda nm, sh, dt: st_.enter_context(nc.sbuf_tensor(nm, sh, dt))
        bigA = SB("bigA", [128, 32768], BF16)
        bigB = SB("bigB", [128, 32768], BF16)
        wf32 = SB("wf32", [128, 4096], F32)
        wbf = SB("wbf", [128, 8192], BF16)
        ovl1 = SB("ovl1", [128, 12288], BF16)
        ovl2 = SB("ovl2", [128, 6144], BF16)
        ctok = SB("ctok", [128, 512], F32)
        id_bf = SB("id_bf", [128, 128], BF16)
        mask_s = SB("mask_s", [128, 128], F32)
        cos_s = SB("cos_s", [128, NT * 64], BF16)
        sin_s = SB("sin_s", [128, NT * 64], BF16)
        ones_s = SB("ones_s", [128, 1], BF16)
        cols = SB("cols", [128, 24], F32)
        qcols = SB("qcols", [128, 256], F32)
        ssq_sb = SB("ssq_sb", [128, 32], F32)
        inv_sb = SB("inv_sb", [128, 32], F32)
        qscb_t = SB("qscb_t", [128, 32], BF16)
        wscb_t = SB("wscb_t", [128, 32], BF16)

        s_ps = st_.enter_context(nc.psum_tensor("s_ps", [128, 2048], F32))
        dmy_ps = st_.enter_context(nc.psum_tensor("dmy_ps", [128, 512], F32))
        o_psA = st_.enter_context(nc.psum_tensor("o_psA", [128, 512], F32))
        o_psB = st_.enter_context(nc.psum_tensor("o_psB", [128, 512], F32))
        tr_ps = st_.enter_context(nc.psum_tensor("tr_ps", [128, 1024], BF16))

        sems = {}
        for e in ENGS:
            sems[e] = st_.enter_context(nc.semaphore("sem" + e))
        for e, k in DMA_SLOTS.items():
            for s2 in range(k):
                sems[(e, s2)] = st_.enter_context(
                    nc.semaphore("sem%s%d" % (e, s2)))
        block = st_.enter_context(nc.Block())
        p = Prog()

        cum = ovl1[:32, :].bitcast(F32)
        cA = cum[:, 0:2048]
        cB = cum[:, 2048:4096]
        cC = cum[:, 4096:6144]
        khT = ovl1[:, 0:2048]
        vh_sb = ovl1[:, 2048:4096]
        pT_sb = ovl1[:, 4096:6144]
        kstage = ovl1[:, 6144:8192]
        qa = ovl1[:, 0:2048]
        qsg = ovl1[:, 2048:4096]
        qm = ovl1[:, 4096:6144]
        rkp = ovl2[:, 0:2048]
        vp = ovl2[:, 2048:4096]
        rsc = ovl2[:, 4096:6144]
        ovl2f = ovl2[:, 0:4096].bitcast(F32)
        hst = lambda par: ovl2[:, par * 2048:(par + 1) * 2048]
        h2v = ovl2[:, 4096:6144]
        xslot = lambda i: wf32[:, i * 2048:(i + 1) * 2048]
        wbfs = lambda i: wbf[:, i * 2048:(i + 1) * 2048]
        bBf32 = bigB[:].bitcast(F32)
        gup_f = lambda par: bBf32[:, par * 2048:(par + 1) * 2048]
        up_f = lambda par: bBf32[:, (2 + par) * 2048:(3 + par) * 2048]
        gT = lambda par: bigB[:, 16384 + par * 2048:16384 + (par + 1) * 2048]
        uT = lambda par: bigB[:, 20480 + par * 2048:20480 + (par + 1) * 2048]

        mxc = cols[:, 0:1]
        biasc = cols[:, 1:2]
        lc = cols[:, 2:3]
        linvc = cols[:, 3:4]
        m1c = cols[:, 4:5]
        m2c = cols[:, 5:6]
        vrc = cols[:, 6:7]
        rsdc = cols[:, 7:8]
        t0c = cols[:, 8:9]
        qred = qcols[:, 0:32]
        qthr = qcols[:, 32:64]
        qsc_bf = qscb_t[:]

        ln0T = lambda eb: bigA[:, eb * N:(eb + 1) * N]
        oT = lambda h: bigA[:, h * N:(h + 1) * N]
        rqT = lambda h: bigB[:, h * N:(h + 1) * N]
        rq_tile = lambda h, i: bigB[:, ((h // 4) * 16 + i) * 512 + (h % 4) * 128:
                                    ((h // 4) * 16 + i) * 512 + (h % 4) * 128 + 128]
        woutb = lambda hdb: bigB[:, hdb * N:(hdb + 1) * N]
        ln1T = lambda eb: bigA[:, eb * N:(eb + 1) * N]
        wptq = lambda half: (bigA if half == 0 else bigB)

        for dst, src, key in [(id_bf[:], id_h, "id_bf"), (mask_s[:], mask_h, "mask"),
                              (cos_s[:], cos_h, "cos"), (sin_s[:], sin_h, "sin"),
                              (ones_s[:], ones_h, "ones")]:
            p.op("Y", (lambda d, s: lambda q: q.dma_start(out=d, in_=s))(dst, src),
                 writes=[key], dma=True)

        def dummy_mm(n):
            def mk(n=n):
                def f(t):
                    ins = None
                    for _ in range(n):
                        ins = t.matmul(dmy_ps[:, 0:512], id_bf[:],
                                       cos_s[:, 0:512], start=True, stop=True)
                    return ins
                return f
            p.op("T", mk(), reads=["id_bf", "cos"], writes=["dmy"])

        dummy_mm(30)

        def ln_cols(src_key, src_ap, n_free, sq_ap, sq_key, eps, junk_ap=None):
            if not isinstance(sq_key, list):
                sq_key = [sq_key]
            if junk_ap is None:
                junk_ap = sq_ap
            p.op("S", (lambda a, d: lambda s: s.activation(
                d, a, AF.Copy, accum_out=m1c))(src_ap, junk_ap),
                reads=[src_key], writes=sq_key + ["c_m1"])
            p.op("S", (lambda a, d: lambda s: s.activation(
                d, a, AF.Square, accum_out=m2c))(src_ap, sq_ap),
                reads=[src_key], writes=sq_key + ["c_m2"])
            p.op("V", (lambda n: lambda v: v.tensor_scalar(
                m1c, m1c, 1.0 / n, None, ALU.mult))(n_free),
                reads=["c_m1"], writes=["c_m1"])
            p.op("V", (lambda n: lambda v: v.tensor_scalar(
                m2c, m2c, 1.0 / n, None, ALU.mult))(n_free),
                reads=["c_m2"], writes=["c_m2"])
            p.op("V", lambda v: v.tensor_mul(t0c, m1c, m1c),
                 reads=["c_m1"], writes=["c_t0"])
            p.op("V", lambda v: v.tensor_sub(vrc, m2c, t0c),
                 reads=["c_m2", "c_t0"], writes=["c_vr"])
            p.op("V", (lambda e: lambda v: v.tensor_scalar(
                vrc, vrc, e, None, ALU.add))(eps),
                reads=["c_vr"], writes=["c_vr"])
            p.op("S", lambda s: s.activation(rsdc, vrc, AF.Ln),
                 reads=["c_vr"], writes=["c_rsd"])
            p.op("S", lambda s: s.activation(rsdc, rsdc, AF.Exp, scale=-0.5),
                 reads=["c_rsd"], writes=["c_rsd"])

        def ln_apply(dst_key, dst_ap, src_key, src_ap):
            if not isinstance(dst_key, list):
                dst_key = [dst_key]
            p.op("V", (lambda d, a: lambda v: v.tensor_scalar(
                d, a, m1c, rsdc, ALU.subtract, ALU.mult))(dst_ap, src_ap),
                reads=[src_key, "c_m1", "c_rsd"], writes=dst_key)

        def transpose_thunks(pairs, ident, ident_key):
            ths = []
            for base in range(0, len(pairs), 8):
                grp = pairs[base:base + 8]
                ths.append((lambda grp=grp: transpose_batch(grp, ident, ident_key)))
            return ths

        def transpose_batch(pairs, ident, ident_key, eng="V"):
            for base in range(0, len(pairs), 8):
                grp = pairs[base:base + 8]

                def mk_t(grp=grp):
                    def f(t):
                        ins = None
                        for gi, (sa, _, _, _) in enumerate(grp):
                            ins = t.transpose(
                                tr_ps[:, gi * 128:(gi + 1) * 128], sa, ident)
                        return ins
                    return f
                rk = sorted({g[1] for g in grp})
                p.op("T", mk_t(), reads=rk + [ident_key], writes=["tr_ps"])

                contig = len(grp) > 1
                for gi in range(1, len(grp)):
                    a, bq = grp[gi - 1][2], grp[gi][2]
                    if not (a.tensor is bq.tensor and
                            bq.offset == a.offset + 128):
                        contig = False
                        break
                if contig:
                    d0 = grp[0][2]
                    full = d0.tensor.ap()[:, d0.offset:d0.offset + 128 * len(grp)]
                    def mk_e1(grp=grp, full=full, eng=eng):
                        def f(v):
                            if eng == "V":
                                return v.tensor_copy(full, tr_ps[:, 0:128 * len(grp)])
                            return v.copy(full, tr_ps[:, 0:128 * len(grp)])
                        return f
                    wk = sorted({g[3] for g in grp})
                    p.op(eng, mk_e1(), reads=["tr_ps"], writes=wk)
                else:
                    def mk_e(grp=grp, eng=eng):
                        def f(v):
                            ins = None
                            for gi, (_, _, da, _) in enumerate(grp):
                                if eng == "V":
                                    ins = v.tensor_copy(da, tr_ps[:, gi * 128:(gi + 1) * 128])
                                else:
                                    ins = v.copy(da, tr_ps[:, gi * 128:(gi + 1) * 128])
                            return ins
                        return f
                    wk = sorted({g[3] for g in grp})
                    p.op(eng, mk_e(), reads=["tr_ps"], writes=wk)

        def rope_tile(src_key, tt, out_lo, out_hi, okl, okh, par=0):
            ct = cos_s[:, tt * 64:(tt + 1) * 64]
            st2 = sin_s[:, tt * 64:(tt + 1) * 64]
            t1 = rsc[:, par * 512:par * 512 + 64]
            t2 = rsc[:, par * 512 + 64:par * 512 + 128]
            r1 = qcols[:, par * 128:par * 128 + 64]
            r2 = qcols[:, par * 128 + 64:par * 128 + 128]
            p.op("V", lambda v: v.tensor_mul(r1, t1, ct),
                 reads=[src_key, "cos"], writes=["ro_r1"])
            p.op("V", lambda v: v.tensor_mul(r2, t2, st2),
                 reads=[src_key, "sin"], writes=["ro_r2"])
            p.op("V", (lambda o: lambda v: v.tensor_sub(o, r1, r2))(out_lo),
                 reads=["ro_r1", "ro_r2"], writes=[okl])
            p.op("V", lambda v: v.tensor_mul(r1, t2, ct),
                 reads=[src_key, "cos"], writes=["ro_r1"])
            p.op("V", lambda v: v.tensor_mul(r2, t1, st2),
                 reads=[src_key, "sin"], writes=["ro_r2"])
            p.op("V", (lambda o: lambda v: v.tensor_add(o, r1, r2))(out_hi),
                 reads=["ro_r1", "ro_r2"], writes=[okh])

        def quant_thunks(src_f32_ap, src_key, dst_bf_ap, dst_key):
            th = []
            th.append(lambda: p.op("S", (lambda a=src_f32_ap: lambda s: s.copy(qa, a))(),
                                   reads=[src_key], writes=["q_a"]))
            th.append(lambda: p.op("V", lambda v: v.tensor_reduce(
                qred, qa[:].rearrange("p (g s) -> p g s", s=64), axis=AX.X,
                op=ALU.add, apply_absolute_value=True),
                reads=["q_a"], writes=["q_red"]))
            th.append(lambda: p.op("V", lambda v: v.tensor_scalar(
                qsc_bf, qred, 1.0 / 64, 1e-8, ALU.mult, ALU.max),
                reads=["q_red"], writes=["q_sc"]))
            th.append(lambda: p.op("V", lambda v: v.tensor_scalar(
                qthr, qsc_bf, QTHR, None, ALU.mult),
                reads=["q_sc"], writes=["q_thr"]))
            th.append(lambda: p.op("S", lambda s: s.sign(qsg, qa),
                                   reads=["q_a"], writes=["q_sg"]))
            th.append(lambda: p.op("S", lambda s: s.activation(qm, qa, AF.Abs),
                                   reads=["q_a"], writes=["q_m", "q_hat"]))
            th.append(lambda: p.op("V", lambda v: v.tensor_tensor(
                qa[:].rearrange("p (g s) -> p g s", s=64),
                qm[:].rearrange("p (g s) -> p g s", s=64),
                qthr[:].rearrange("p (g o) -> p g o", o=1).broadcast_to([128, 32, 64]),
                ALU.is_gt), reads=["q_m", "q_thr"], writes=["q_a"]))
            th.append(lambda: p.op("V", lambda v: v.tensor_mul(qm, qa, qsg),
                                   reads=["q_a", "q_sg"], writes=["q_m", "q_hat"]))
            th.append(lambda: p.op("V", (lambda d=dst_bf_ap: lambda v: v.tensor_tensor(
                d.rearrange("p (g s) -> p g s", s=64),
                qm[:].rearrange("p (g s) -> p g s", s=64),
                qsc_bf[:].rearrange("p (g o) -> p g o", o=1).broadcast_to([128, 32, 64]),
                ALU.mult))(), reads=["q_m", "q_sc"],
                writes=[dst_key, "q_m"] if dst_key == "q_hat" else [dst_key]))
            return th

        def quant_chunk(src_f32_ap, src_key, dst_bf_ap, dst_key):
            p.op("S", (lambda a: lambda s: s.copy(qa, a))(src_f32_ap),
                 reads=[src_key], writes=["q_a"])
            p.op("V", lambda v: v.tensor_reduce(
                qred, qa[:].rearrange("p (g s) -> p g s", s=64), axis=AX.X,
                op=ALU.add, apply_absolute_value=True),
                reads=["q_a"], writes=["q_red"])
            p.op("V", lambda v: v.tensor_scalar(
                qsc_bf, qred, 1.0 / 64, 1e-8, ALU.mult, ALU.max),
                reads=["q_red"], writes=["q_sc"])
            p.op("V", lambda v: v.tensor_scalar(qthr, qsc_bf, QTHR, None, ALU.mult),
                 reads=["q_sc"], writes=["q_thr"])
            p.op("S", lambda s: s.sign(qsg, qa), reads=["q_a"], writes=["q_sg"])
            p.op("S", lambda s: s.activation(qm, qa, AF.Abs),
                 reads=["q_a"], writes=["q_m", "q_hat"])
            p.op("V", lambda v: v.tensor_tensor(
                qa[:].rearrange("p (g s) -> p g s", s=64),
                qm[:].rearrange("p (g s) -> p g s", s=64),
                qthr[:].rearrange("p (g o) -> p g o", o=1).broadcast_to([128, 32, 64]),
                ALU.is_gt), reads=["q_m", "q_thr"], writes=["q_a"])
            p.op("V", lambda v: v.tensor_mul(qm, qa, qsg),
                 reads=["q_a", "q_sg"], writes=["q_m", "q_hat"])
            p.op("V", (lambda d: lambda v: v.tensor_tensor(
                d.rearrange("p (g s) -> p g s", s=64),
                qm[:].rearrange("p (g s) -> p g s", s=64),
                qsc_bf[:].rearrange("p (g o) -> p g o", o=1).broadcast_to([128, 32, 64]),
                ALU.mult))(dst_bf_ap), reads=["q_m", "q_sc"],
                writes=[dst_key, "q_m"] if dst_key == "q_hat" else [dst_key])

        wstg = wf32[:, 512:1536]
        wfbf = wf32[:].bitcast(BF16)
        ctok_bf = ctok[:].bitcast(BF16)
        wqa_c = [wfbf[:, 3072:4096], ctok_bf[:, 0:1024]]
        wqb_c = [wfbf[:, 5120:6144], wfbf[:, 7168:8192]]
        wred_c = [qcols[:, 64:80], qcols[:, 80:96]]
        wthr_c = [qcols[:, 96:112], qcols[:, 112:128]]
        wscb_c = [wscb_t[:, 0:16], wscb_t[:, 16:32]]

        def wp_chunks():
            out = []
            cidx = 0
            for q in range(4):
                for db2 in range(4):
                    for hs2 in range(8):
                        pc = cidx % 2
                        A = wqa_c[pc]
                        Bq = wqb_c[pc]
                        red = wred_c[pc]
                        thr = wthr_c[pc]
                        scb = wscb_c[pc]
                        ka, kb, kt = "w_a%d" % pc, "w_b%d" % pc, "w_t%d" % pc
                        kr, ks, kh = ("w_red%d" % pc, "w_sc%d" % pc,
                                      "w_thr%d" % pc)
                        vth = []
                        vth.append((lambda q=q, db2=db2, hs2=hs2: p.op(
                            "Y", (lambda q=q, db2=db2, hs2=hs2:
                                  lambda qe: qe.dma_start(
                                out=wstg,
                                in_=wp_h[q * 512 + db2 * 128:
                                         q * 512 + (db2 + 1) * 128,
                                         hs2 * 1024:(hs2 + 1) * 1024]))(),
                            writes=["w_stg"], dma=True)))
                        vth.append((lambda A=A, ka=ka, kt=kt: p.op(
                            "S", lambda s: s.copy(A, wstg),
                            reads=["w_stg"], writes=[ka, kt])))
                        vth.append((lambda A=A, red=red, ka=ka, kr=kr: p.op(
                            "V", lambda v: v.tensor_reduce(
                                red, A.rearrange("p (g s) -> p g s", s=64),
                                axis=AX.X, op=ALU.add,
                                apply_absolute_value=True),
                            reads=[ka], writes=[kr])))
                        vth.append((lambda red=red, scb=scb, kr=kr, ks=ks: p.op(
                            "V", lambda v: v.tensor_scalar(
                                scb, red, 1.0 / 64, 1e-8, ALU.mult, ALU.max),
                            reads=[kr], writes=[ks])))
                        vth.append((lambda thr=thr, scb=scb, ks=ks, kh=kh: p.op(
                            "V", lambda v: v.tensor_scalar(
                                thr, scb, QTHR, None, ALU.mult),
                            reads=[ks], writes=[kh])))
                        vth.append((lambda A=A, Bq=Bq, ka=ka, kb=kb: p.op(
                            "S", lambda s: s.sign(Bq, A),
                            reads=[ka], writes=[kb])))
                        vth.append((lambda A=A, ka=ka: p.op(
                            "S", lambda s: s.activation(A, A, AF.Abs),
                            reads=[ka], writes=[ka])))
                        vth.append((lambda A=A, thr=thr, ka=ka, kh=kh: p.op(
                            "V", lambda v: v.tensor_tensor(
                                A.rearrange("p (g s) -> p g s", s=64),
                                A.rearrange("p (g s) -> p g s", s=64),
                                thr.rearrange("p (g o) -> p g o", o=1)
                                .broadcast_to([128, 16, 64]),
                                ALU.is_gt), reads=[ka, kh], writes=[ka])))
                        vth.append((lambda A=A, Bq=Bq, ka=ka, kb=kb: p.op(
                            "V", lambda v: v.tensor_mul(Bq, Bq, A),
                            reads=[ka, kb], writes=[kb])))
                        vth.append((lambda Bq=Bq, scb=scb, kb=kb, ks=ks: p.op(
                            "V", lambda v: v.tensor_tensor(
                                Bq.rearrange("p (g s) -> p g s", s=64),
                                Bq.rearrange("p (g s) -> p g s", s=64),
                                scb.rearrange("p (g o) -> p g o", o=1)
                                .broadcast_to([128, 16, 64]),
                                ALU.mult), reads=[kb, ks], writes=[kb])))
                        pairs = [(Bq[:, k * 128:(k + 1) * 128], kb,
                                  A[:, k * 128:(k + 1) * 128], kt)
                                 for k in [0, 2, 4, 6, 1, 3, 5, 7]]
                        tth = transpose_thunks(pairs, id_bf[:], "id_bf")
                        tth.append((lambda q=q, db2=db2, hs2=hs2, A=A, kt=kt:
                                    p.op("G", (lambda q=q, db2=db2, hs2=hs2,
                                               A=A: lambda qe: qe.dma_start(
                            out=whq_scr[q, hs2 * 8:(hs2 + 1) * 8, :,
                                        db2 * 128:(db2 + 1) * 128]
                            .rearrange("k p t -> p k t"),
                            in_=A.rearrange("p (k t) -> p k t", t=128)))(),
                            reads=[kt], writes=[("whqw", q, db2, hs2)],
                            dma=True)))
                        out.append((vth, tth))
                        cidx += 1
            return out

        def attention(b):
            p.barrier()
            for tt in range(NT):
                par = tt % 2
                xs = xslot(par)
                p.op("Y", (lambda b=b, tt=tt, d=xs: lambda q: q.dma_start(
                    out=d, in_=x_h[b, tt]))(), writes=[("xsl", par)], dma=True)
                ln_cols(("xsl", par), xs, E, wbfs(2), ["sq_s", ("wbf", 2)], LN_EPS)
                dummy_mm(3)
                ln_apply(["lnbf", ("wbf", 3)], wbfs(3), ("xsl", par), xs)
                dummy_mm(3)
                pairs = [(wbfs(3)[:, eb * 128:(eb + 1) * 128], ("wbf", 3),
                          ln0T(eb)[:, tt * 128:(tt + 1) * 128], ("ln0T", eb))
                         for eb in range(EB)]
                transpose_batch(pairs, id_bf[:], "id_bf")

            ln0_keys = [("ln0T", eb) for eb in range(EB)]

            def stream_w128(src_ap, par, wkey_f, wkey_b):
                p.op("Y", (lambda s=src_ap, d=xslot(par): lambda q: q.dma_start(
                    out=d.rearrange("p (e d) -> p e d", d=128),
                    in_=s.rearrange("e p d -> p e d")))(),
                    writes=[wkey_f, ("xsl", par)], dma=True)
                p.op("V", (lambda par=par: lambda v: v.tensor_copy(
                    wbfs(par), xslot(par)))(), reads=[("xsl", par)], writes=[wkey_b, ("wbf", par)])

            q4s = ovl1[:, 0:8192]
            kvs = ovl1[:, 8192:12288]

            def stream_q4(hg):
                for c in range(4):
                    par = c % 2
                    p.op("Y", (lambda hg=hg, c=c, par=par: lambda q: q.dma_start(
                        out=xslot(par).rearrange("p (e d) -> p e d", d=512),
                        in_=wq_h[hg, c * 4:(c + 1) * 4].rearrange("e p d -> p e d")))(),
                        writes=[("xsl", par)], dma=True)
                    p.op("V", (lambda c=c, par=par: lambda v: v.tensor_copy(
                        q4s[:, c * 2048:(c + 1) * 2048], xslot(par)))(),
                        reads=[("xsl", par)], writes=[("q4b", c)])

            def stream_kv():
                for c in range(2):
                    par = c % 2
                    p.op("Y", (lambda c=c, par=par: lambda q: q.dma_start(
                        out=xslot(par).rearrange("p (e d) -> p e d", d=256),
                        in_=wkv_h[c * 8:(c + 1) * 8].rearrange("e p d -> p e d")))(),
                        writes=[("xsl", par)], dma=True)
                    p.op("V", (lambda c=c, par=par: lambda v: v.tensor_copy(
                        kvs[:, c * 2048:(c + 1) * 2048], xslot(par)))(),
                        reads=[("xsl", par)], writes=[("kvb",)])

            def q4_mm(hg, tt, width, rhs_ap, rkeys):
                qpar = tt % 2
                qps = (o_psA if qpar == 0 else o_psB)[:, 0:width]

                def mk_q(tt=tt, qps=qps, rhs_ap=rhs_ap, width=width):
                    def f(t):
                        ins = None
                        for eb in range(EB):
                            ins = t.matmul(qps,
                                           ln0T(eb)[:, tt * 128:(tt + 1) * 128],
                                           rhs_ap[:, eb * width:(eb + 1) * width],
                                           start=(eb == 0), stop=(eb == EB - 1))
                        return ins
                    return f
                p.op("T", mk_q(), reads=rkeys + ln0_keys, writes=[("qps", qpar)])
                return qps, qpar

            def head_post(h, tt, qps, qpar, hsub, kind):
                if kind == "v":
                    p.op("S", (lambda qps=qps, tt=tt, hsub=hsub: lambda s: s.copy(
                        vp[:, tt * 128:(tt + 1) * 128],
                        qps[:, hsub * 128:(hsub + 1) * 128]))(),
                        reads=[("qps", qpar)], writes=[("vp", tt)])
                    return
                if kind == "q":
                    if hsub != 0:
                        return
                    base = qpar * 512
                    ob = 1024 + qpar * 512
                    p.op("S", (lambda qps=qps, base=base: lambda s: s.copy(
                        rsc[:, base:base + 512], qps[:, 0:512]))(),
                        reads=[("qps", qpar)], writes=[("qbf", qpar)])
                    t3 = rsc[:, base:base + 512].rearrange(
                        "p (h c) -> p c h", c=128)
                    o3 = rsc[:, ob:ob + 512].rearrange(
                        "p (h c) -> p c h", c=128)
                    ct = cos_s[:, tt * 64:(tt + 1) * 64].rearrange(
                        "p (c o) -> p c o", o=1).broadcast_to([128, 64, 4])
                    st = sin_s[:, tt * 64:(tt + 1) * 64].rearrange(
                        "p (c o) -> p c o", o=1).broadcast_to([128, 64, 4])
                    r1 = ctok[:, 0:256].rearrange("p (c h) -> p c h", h=4)
                    r2 = ctok[:, 256:512].rearrange("p (c h) -> p c h", h=4)
                    t1, t2 = t3[:, 0:64, :], t3[:, 64:128, :]
                    olo, ohi = o3[:, 0:64, :], o3[:, 64:128, :]
                    p.op("V", lambda v: v.tensor_mul(r1, t1, ct),
                         reads=[("qbf", qpar), "cos"], writes=[("ro1",)])
                    p.op("V", lambda v: v.tensor_mul(r2, t2, st),
                         reads=[("qbf", qpar), "sin"], writes=[("ro2",)])
                    p.op("V", lambda v: v.tensor_sub(olo, r1, r2),
                         reads=[("ro1",), ("ro2",)], writes=[("o1", qpar)])
                    p.op("V", lambda v: v.tensor_mul(r1, t2, ct),
                         reads=[("qbf", qpar), "cos", ("o1", qpar)],
                         writes=[("ro1",)])
                    p.op("V", lambda v: v.tensor_mul(r2, t1, st),
                         reads=[("qbf", qpar), "sin"], writes=[("ro2",)])
                    p.op("V", lambda v: v.tensor_add(ohi, r1, r2),
                         reads=[("ro1",), ("ro2",)], writes=[("o2", qpar)])

                    def mk_tq(ob=ob):
                        def f(t):
                            ins = None
                            for k2 in range(4):
                                ins = t.transpose(
                                    tr_ps[:, k2 * 128:(k2 + 1) * 128],
                                    rsc[:, ob + k2 * 128:ob + (k2 + 1) * 128],
                                    id_bf[:])
                            return ins
                        return f
                    p.op("T", mk_tq(), reads=[("o1", qpar), ("o2", qpar),
                                              "id_bf"], writes=["tr_ps"])
                    p.op("V", (lambda h=h, tt=tt: lambda v: v.tensor_copy(
                        bigB[:, ((h // 4) * 16 + tt) * 512:
                             ((h // 4) * 16 + tt) * 512 + 512],
                        tr_ps[:, 0:512]))(),
                        reads=["tr_ps"],
                        writes=[("rqT", h), ("rqT", h + 1),
                                ("rqT", h + 2), ("rqT", h + 3)] + ["tr_ps"])
                else:
                    p.op("S", (lambda qps=qps, qpar=qpar, hsub=hsub:
                               lambda s: s.copy(
                        rsc[:, qpar * 512:qpar * 512 + 128],
                        qps[:, hsub * 128:(hsub + 1) * 128]))(),
                        reads=[("qps", qpar)], writes=[("qbf", qpar)])
                    ol = rkp[:, tt * 128:tt * 128 + 64]
                    oh = rkp[:, tt * 128 + 64:(tt + 1) * 128]
                    rope_tile(("qbf", qpar), tt, ol, oh, ("rkp", tt), ("rkp", tt),
                              par=qpar)

            q4keys = [("q4b", c) for c in range(4)]
            pend = []
            for hg in range(4):
                stream_q4(hg)
                for tt in range(NT):
                    qps, qpar = q4_mm(hg, tt, 512, q4s, q4keys)
                    while pend:
                        head_post(*pend.pop(0))
                    for hsub in range(4):
                        pend.append((hg * 4 + hsub, tt, qps, qpar, hsub, "q"))
            stream_kv()
            for tt in range(NT):
                qps, qpar = q4_mm(0, tt, 256, kvs, [("kvb",)])
                while pend:
                    head_post(*pend.pop(0))
                pend.append((0, tt, qps, qpar, 0, "k"))
                pend.append((0, tt, qps, qpar, 1, "v"))
            while pend:
                head_post(*pend.pop(0))

            def proj32(wh, dstf, dkey):
                p.op("Y", (lambda s=wh: lambda q: q.dma_start(
                    out=xslot(0)[:, 0:512].rearrange("p (e c) -> p e c", c=32),
                    in_=s.rearrange("e p c -> p e c")))(),
                    writes=["wsmf", ("xsl", 0)], dma=True)
                p.op("V", lambda v: v.tensor_copy(wbfs(2)[:, 0:512], xslot(0)[:, 0:512]),
                     reads=[("xsl", 0)], writes=["wsmb", ("wbf", 2)])
                for tc in range(4):
                    def mk_g(tc=tc):
                        def f(t):
                            ins = None
                            for eb in range(EB):
                                ins = t.matmul(
                                    s_ps[:32, tc * 512:(tc + 1) * 512],
                                    wbfs(2)[:, eb * 32:(eb + 1) * 32],
                                    ln0T(eb)[:, tc * 512:(tc + 1) * 512],
                                    start=(eb == 0), stop=(eb == EB - 1))
                            return ins
                        return f
                    p.op("T", mk_g(), reads=[("wbf", 2)] + ln0_keys, writes=[("gps", tc)])
                def mk_ev(dstf=dstf):
                    def f(v):
                        return v.tensor_copy(dstf, s_ps[:32, :])
                    return f
                p.op("V", mk_ev(), reads=[("gps", tc2) for tc2 in range(4)] +
                     [("q4b", c) for c in range(4)] + [("kvb",)],
                     writes=[dkey])

            proj32(wg_h, cB, "cB")


            p.op("S", lambda s: s.activation(cC, cB, AF.Abs), reads=["cB"], writes=["cC"])
            p.op("S", lambda s: s.activation(cC, cC, AF.Exp, scale=-1.0),
                 reads=["cC"], writes=["cC"])
            p.op("S", lambda s: s.activation(cC, cC, AF.Ln, bias=1.0),
                 reads=["cC"], writes=["cC"])
            p.op("V", lambda v: v.tensor_scalar(cB, cB, 0.0, None, ALU.min),
                 reads=["cB"], writes=["cB"])
            p.op("V", lambda v: v.tensor_sub(cB, cB, cC), reads=["cC", "cB"],
                 writes=["cB"])
            p.op("V", lambda v: v.memset(cA, 1.0), writes=["cA"])
            p.op("V", lambda v: v.tensor_tensor_scan(cC, cA, cB, 0.0, ALU.mult, ALU.add),
                 reads=["cA", "cB"], writes=["cC"])
            p.op("V", lambda v: v.tensor_sub(cC, cC, cB), reads=["cB", "cC"],
                 writes=["cC"])
            p.op("S", lambda s: s.activation(cC, cC, AF.Exp), reads=["cC"], writes=["cC"])
            proj32(wkvh_h, cA, "cA")
            p.op("V", lambda v: v.tensor_mul(cA, cA, cC), reads=["cA", "cC"],
                 writes=["cA"])
            p.op("V", lambda v: v.memset(cB, 1.0), writes=["cB"])
            p.op("V", lambda v: v.tensor_tensor_scan(cC, cB, cA, 0.0, ALU.mult, ALU.add),
                 reads=["cA", "cB"], writes=["cC"])
            cCbf = ovl1[:32, 4096:6144]
            p.op("V", lambda v: v.tensor_copy(cCbf, cC), reads=["cC"], writes=["cCbf"])
            def mk_ct(t):
                ins = None
                for k in range(NT):
                    ins = t.transpose(tr_ps[:, k * 32:(k + 1) * 32],
                                      cCbf[:, k * 128:(k + 1) * 128], id_bf[0:32, 0:32])
                return ins
            p.op("T", mk_ct, reads=["cCbf", "id_bf"], writes=["tr_ps"])
            p.op("V", lambda v: v.tensor_copy(ctok[:], tr_ps[:, 0:512]),
                 reads=["tr_ps"], writes=["ctok"])
            return ln0_keys

        def p5_attention_core(b):
            p.barrier()
            khTp = lambda hp: ovl1[:, (0 if hp == 0 else 6144):(2048 if hp == 0 else 8192)]
            vhp = lambda hp: ovl1[:, (2048 if hp == 0 else 8192):(4096 if hp == 0 else 10240)]
            pTp = lambda ip: ovl1[:, (4096 if ip == 0 else 10240):(6144 if ip == 0 else 12288)]
            kst5 = wbfs(2)
            lc5 = lambda ip: cols[:, 2 + ip:3 + ip]
            linv5 = lambda ip: cols[:, 10 + ip:11 + ip]

            def k_build(h):
                hp = h % 2
                for tt in range(NT):
                    kc = ctok[:, tt * 32 + h:tt * 32 + h + 1]
                    p.op("V", (lambda tt=tt, kc=kc: lambda v: v.tensor_scalar(
                        kst5[:, tt * 128:(tt + 1) * 128],
                        rkp[:, tt * 128:(tt + 1) * 128],
                        kc, None, ALU.mult))(),
                        reads=[("rkp", tt), "ctok"], writes=[("kst", tt)])
                pairs = [(kst5[:, tt * 128:(tt + 1) * 128], ("kst", tt),
                          khTp(hp)[:, tt * 128:(tt + 1) * 128], ("khT", hp, tt))
                         for tt in range(NT)]
                transpose_batch(pairs, id_bf[:], "id_bf")
                for tt in range(NT):
                    vc = ctok[:, tt * 32 + 16 + h:tt * 32 + 17 + h]
                    p.op("V", (lambda tt=tt, vc=vc, hp=hp: lambda v: v.tensor_scalar(
                        vhp(hp)[:, tt * 128:(tt + 1) * 128],
                        vp[:, tt * 128:(tt + 1) * 128],
                        vc, None, ALU.mult))(),
                        reads=[("vp", tt), "ctok"], writes=[("vh", hp, tt)])

            def p5_base(i):
                return 1024 * (i % 2) if i <= 7 else 0

            def p5_mainT(h, i):
                hp = h % 2
                rng = (i + 1) * 128
                base = p5_base(i)
                kh_keys = [("khT", hp, tt) for tt in range(NT)]
                dlo = (i * 128 // 512) * 512
                dw = min(512, rng - dlo)
                p.op("T", (lambda i=i, h=h, hp=hp, dlo=dlo, dw=dw, base=base:
                           lambda t: t.matmul(
                    s_ps[:, base + dlo:base + dlo + dw],
                    rq_tile(h, i),
                    khTp(hp)[:, dlo:dlo + dw], start=True, stop=True))(),
                    reads=kh_keys + [("rqT", h)],
                    writes=[("spb", (base + dlo) // 512)])
                c0 = 0
                nch = 0
                chunks = []
                while c0 < rng:
                    w = min(512, rng - c0)
                    if c0 != dlo:
                        p.op("T", (lambda i=i, h=h, hp=hp, c0=c0, w=w, base=base:
                                   lambda t: t.matmul(
                            s_ps[:, base + c0:base + c0 + w],
                            rq_tile(h, i),
                            khTp(hp)[:, c0:c0 + w], start=True, stop=True))(),
                            reads=kh_keys + [("rqT", h)],
                            writes=[("spb", (base + c0) // 512)])
                        chunks.append((c0, w, nch))
                        nch += 1
                    c0 += w
                return chunks, dlo, dw, nch

            def p5_mainV(h, i, chunks, dlo, dw, nch):
                ip = i % 2
                base = p5_base(i)
                if i <= 7:
                    mcol = 12 + 2 * ip
                else:
                    mcol = 12
                mxi = cols[:, 0 + ip:1 + ip]
                bsi = cols[:, 18 + ip:19 + ip]
                p.op("V", (lambda i=i, base=base: lambda v: v.tensor_add(
                    s_ps[:, base + i * 128:base + (i + 1) * 128],
                    s_ps[:, base + i * 128:base + (i + 1) * 128], mask_s[:]))(),
                    reads=["mask", ("spb", (base + dlo) // 512)],
                    writes=[("spb", (base + dlo) // 512)])
                p.op("V", (lambda dlo=dlo, dw=dw, base=base, mcol=mcol:
                           lambda v: v.reduce_max(
                    cols[:, mcol:mcol + 1], s_ps[:, base + dlo:base + dlo + dw],
                    axis=AX.X))(),
                    reads=[("spb", (base + dlo) // 512)], writes=[("mxcol", mcol)])
                for (c0, w, k) in chunks:
                    p.op("V", (lambda c0=c0, w=w, k=k, base=base, mcol=mcol:
                               lambda v: v.reduce_max(
                        cols[:, mcol + 1 + k:mcol + 2 + k],
                        s_ps[:, base + c0:base + c0 + w], axis=AX.X))(),
                        reads=[("spb", (base + c0) // 512)],
                        writes=[("mxcol", mcol + 1 + k)])
                mx_keys = [("mxcol", mcol + k) for k in range(nch + 1)]
                p.op("V", (lambda nch=nch, mcol=mcol, mxi=mxi: lambda v: v.reduce_max(
                    mxi, cols[:, mcol:mcol + 1 + nch], axis=AX.X))(),
                    reads=mx_keys, writes=[("c_mx", ip)])
                p.op("V", (lambda mxi=mxi, bsi=bsi: lambda v: v.tensor_scalar(
                    bsi, mxi, -SCALE, None, ALU.mult))(),
                    reads=[("c_mx", ip)], writes=[("c_bias", ip)])

            def p5_mainS(h, i):
                ip = i % 2
                rng = (i + 1) * 128
                base = p5_base(i)
                bsi = cols[:, 18 + ip:19 + ip]
                sps_keys = [("spb", (base // 512) + k)
                            for k in range((rng + 511) // 512)]
                p.op("S", (lambda rng=rng, ip=ip, base=base, bsi=bsi:
                           lambda s: s.activation(
                    wbfs(ip)[:, 0:rng], s_ps[:, base:base + rng], AF.Exp,
                    bias=bsi, scale=SCALE, accum_out=lc5(ip)))(),
                    reads=sps_keys + [("c_bias", ip)],
                    writes=[("wbf", ip), ("c_l", ip)] + sps_keys)
                p.op("V", (lambda ip=ip: lambda v: v.reciprocal(linv5(ip), lc5(ip)))(),
                     reads=[("c_l", ip)], writes=[("c_linv", ip)])

            def p5_partA(h, i):
                ip = i % 2
                pairs = [(wbfs(ip)[:, mb * 128:(mb + 1) * 128], ("wbf", ip),
                          pTp(ip)[:, mb * 128:(mb + 1) * 128], ("pT", ip, mb))
                         for mb in range(i + 1)]
                transpose_batch(pairs, id_bf[:], "id_bf")

            def p5_partB(h, i):
                hp = h % 2
                ip = i % 2
                vh_keys = [("vh", hp, tt) for tt in range(NT)]
                ops_t = (o_psA if ip == 0 else o_psB)[:, 0:128]

                def mk_o(i=i, opst=ops_t, ip=ip, hp=hp):
                    def f(t):
                        ins = None
                        for mb in range(i + 1):
                            ins = t.matmul(opst,
                                           pTp(ip)[:, mb * 128:(mb + 1) * 128],
                                           vhp(hp)[:, mb * 128:(mb + 1) * 128],
                                           start=(mb == 0), stop=(mb == i))
                        return ins
                    return f
                p.op("T", mk_o(), reads=[("pT", ip, mb) for mb in range(i + 1)] + vh_keys,
                     writes=[("qps", ip)])
                p.op("V", (lambda opst=ops_t, ip=ip: lambda v: v.tensor_scalar(
                    rsc[:, ip * 128:(ip + 1) * 128], opst, linv5(ip), None, ALU.mult))(),
                    reads=[("qps", ip), ("c_linv", ip)], writes=[("o_tok", ip)])
                p.op("T", (lambda ip=ip: lambda t: t.transpose(
                    tr_ps[:, 0:128], rsc[:, ip * 128:(ip + 1) * 128], id_bf[:]))(),
                    reads=[("o_tok", ip), "id_bf"], writes=["tr_ps"])
                p.op("V", (lambda h=h, i=i: lambda v: v.tensor_copy(
                    oT(h)[:, i * 128:(i + 1) * 128], tr_ps[:, 0:128]))(),
                    reads=["tr_ps"], writes=[("oT", h)])

            k_build(0)
            pend5 = None
            for h in range(H):
                for i in range(NT):
                    info = p5_mainT(h, i)
                    dummy_mm(3)
                    p5_mainV(h, i, *info)
                    if pend5 is not None:
                        p5_partA(*pend5)
                        p5_partB(*pend5)
                    p5_mainS(h, i)
                    pend5 = (h, i)
                    if i == 8 and h + 1 < H:
                        k_build(h + 1)
            p5_partA(*pend5)
            p5_partB(*pend5)

        def p6_out_proj(b):
            p.barrier()
            for hdb in range(EB):
                par = hdb % 2
                p.op("Y", (lambda hdb=hdb, par=par: lambda q: q.dma_start(
                    out=xslot(par), in_=wout_h[hdb]))(),
                    writes=[("wof", par), ("xsl", par)], dma=True)
                p.op("V", (lambda hdb=hdb, par=par: lambda v: v.tensor_copy(
                    woutb(hdb), xslot(par)))(),
                    reads=[("xsl", par)], writes=[("wob", hdb)])
            wob_keys = [("wob", hdb) for hdb in range(EB)]
            oT_keys = [("oT", h) for h in range(H)]
            wbf_f32_p6 = wbf[:].bitcast(F32)
            aof = lambda par: wbf_f32_p6[:, par * 2048:(par + 1) * 2048]
            sq6 = ovl1[:, 0:2048]
            ln1b6 = lambda par: ovl1[:, 2048 + par * 2048:4096 + par * 2048]
            stg6 = lambda par: ovl1[:, 6144 + par * 2048:8192 + par * 2048]

            def p6_mm(tt):
                par = tt % 2
                p.op("Y", (lambda b=b, tt=tt, par=par: lambda q: q.dma_start(
                    out=xslot(par), in_=x_h[b, tt]))(),
                    writes=[("xsl", par)], dma=True)
                def mk_ao(tt=tt):
                    def f(t):
                        ins = None
                        for hdb in range(EB):
                            for ec in range(4):
                                ins = t.matmul(
                                    s_ps[:, ec * 512:(ec + 1) * 512],
                                    oT(hdb)[:, tt * 128:(tt + 1) * 128],
                                    woutb(hdb)[:, ec * 512:(ec + 1) * 512],
                                    start=(hdb == 0), stop=(hdb == EB - 1))
                        return ins
                    return f
                p.op("T", mk_ao(), reads=wob_keys + oT_keys, writes=["s_ps"])
                p.op("S", (lambda par=par: lambda s: s.activation(
                    aof(par), s_ps[:, :], AF.Copy,
                    accum_out=cols[:, 16 + par:17 + par]))(),
                    reads=["s_ps"], writes=[("aof", par), ("c_m1p", par)])
                dummy_mm(4)

            def p6_postV(tt):
                par = tt % 2
                p.op("S", (lambda par=par: lambda s: s.activation(
                    sq6, aof(par), AF.Square, accum_out=m2c))(),
                    reads=[("aof", par)], writes=["sq_s", "c_m2"])
                p.op("V", (lambda par=par: lambda v: v.tensor_scalar(
                    m1c, cols[:, 16 + par:17 + par], 1.0 / E, None, ALU.mult))(),
                    reads=[("c_m1p", par)], writes=["c_m1"])
                p.op("V", lambda v: v.tensor_scalar(
                    m2c, m2c, 1.0 / E, None, ALU.mult), reads=["c_m2"], writes=["c_m2"])
                p.op("V", lambda v: v.tensor_mul(t0c, m1c, m1c),
                     reads=["c_m1"], writes=["c_t0"])
                p.op("V", lambda v: v.tensor_sub(vrc, m2c, t0c),
                     reads=["c_m2", "c_t0"], writes=["c_vr"])
                p.op("V", lambda v: v.tensor_scalar(
                    vrc, vrc, LN_EPS, None, ALU.add), reads=["c_vr"], writes=["c_vr"])
                p.op("S", lambda s: s.activation(rsdc, vrc, AF.Ln),
                     reads=["c_vr"], writes=["c_rsd"])
                p.op("S", lambda s: s.activation(rsdc, rsdc, AF.Exp, scale=-0.5),
                     reads=["c_rsd"], writes=["c_rsd"])
                ln_apply([("aof", par)], aof(par), ("aof", par), aof(par))
                p.op("V", (lambda par=par: lambda v: v.tensor_add(
                    xslot(par), xslot(par), aof(par)))(),
                    reads=[("xsl", par), ("aof", par)], writes=[("xsl", par)])
                p.op("G", (lambda b=b, tt=tt, par=par: lambda q: q.dma_start(
                    out=x1_scr[b, tt], in_=xslot(par)))(),
                    reads=[("xsl", par)], writes=[("x1w", tt)], dma=True)
                ln_cols(("xsl", par), xslot(par), E, sq6, ["sq_s"], LN_EPS)
                ln_apply([("ln1b", par)], ln1b6(par), ("xsl", par), xslot(par))

            def p6_postT(tt):
                par = tt % 2
                pairs = [(ln1b6(par)[:, eb * 128:(eb + 1) * 128], ("ln1b", par),
                          stg6(par)[:, eb * 128:(eb + 1) * 128], ("l1st", par))
                         for eb in range(EB)]
                transpose_batch(pairs, id_bf[:], "id_bf")
                p.op("G", (lambda b=b, tt=tt, par=par: lambda q: q.dma_start(
                    out=ln1T_scr[b].rearrange("e p t -> p e t")[:, :, tt * 128:(tt + 1) * 128],
                    in_=stg6(par).rearrange("p (e t) -> p e t", t=128)))(),
                    reads=[("l1st", par)],
                    writes=[("l1w", tt)], dma=True)

            for tt in range(NT):
                p6_mm(tt)
                if tt >= 2:
                    p6_postT(tt - 2)
                if tt >= 1:
                    p6_postV(tt - 1)
            p6_postT(NT - 2)
            p6_postV(NT - 1)
            p6_postT(NT - 1)

        def f1_gateup_both():
            p.barrier()
            p.op("V", lambda v: v.memset(ssq_sb[:], 0.0), writes=["ssq"])
            for b2 in range(B):
                for eb in range(EB):
                    dst = (bigA if b2 == 0 else bigB)[:, eb * N:(eb + 1) * N]
                    p.op("Y", (lambda b2=b2, eb=eb, dst=dst: lambda q: q.dma_start(
                        out=dst, in_=ln1T_scr[b2, eb]))(),
                        writes=[("ln1T", b2, eb)], dma=True)
            l1_keys = [("ln1T", b2, eb) for b2 in range(B) for eb in range(EB)]
            wbf_f32 = wbf[:].bitcast(F32)
            gate_f = wbf_f32[:, 0:2048]
            upf_f = wbf_f32[:, 2048:4096]

            def gT_ap(par, eb):
                if par == 0:
                    return ovl1[:, 6144 + eb * 128:6144 + (eb + 1) * 128]
                return ovl1[:, 8192 + eb * 128:8192 + (eb + 1) * 128]

            def uT_ap(par, eb):
                if par == 0:
                    return ovl1[:, 10240 + eb * 128:10240 + (eb + 1) * 128]
                if eb < 8:
                    return cos_s[:, eb * 128:(eb + 1) * 128]
                return sin_s[:, (eb - 8) * 128:(eb - 7) * 128]

            def qchain2(src_ap, src_key, Bq, kb):
                th = []
                th.append((lambda src_ap=src_ap, src_key=src_key: p.op(
                    "S", (lambda a=src_ap: lambda s: s.copy(qa, a))(),
                    reads=[src_key], writes=["q_a"])))
                th.append(lambda: p.op("V", lambda v: v.tensor_reduce(
                    qred, qa[:].rearrange("p (g s) -> p g s", s=64), axis=AX.X,
                    op=ALU.add, apply_absolute_value=True),
                    reads=["q_a"], writes=["q_red"]))
                th.append(lambda: p.op("V", lambda v: v.tensor_scalar(
                    qsc_bf, qred, 1.0 / 64, 1e-8, ALU.mult, ALU.max),
                    reads=["q_red"], writes=["q_sc"]))
                th.append(lambda: p.op("V", lambda v: v.tensor_scalar(
                    qthr, qsc_bf, QTHR, None, ALU.mult),
                    reads=["q_sc"], writes=["q_thr"]))
                th.append((lambda Bq=Bq, kb=kb: p.op(
                    "S", lambda s: s.sign(Bq, qa), reads=["q_a"], writes=[kb])))
                th.append(lambda: p.op("S", lambda s: s.activation(
                    qa, qa, AF.Abs), reads=["q_a"], writes=["q_a"]))
                th.append(lambda: p.op("V", lambda v: v.tensor_tensor(
                    qa[:].rearrange("p (g s) -> p g s", s=64),
                    qa[:].rearrange("p (g s) -> p g s", s=64),
                    qthr[:].rearrange("p (g o) -> p g o", o=1)
                    .broadcast_to([128, 32, 64]),
                    ALU.is_gt), reads=["q_a", "q_thr"], writes=["q_a"]))
                th.append((lambda Bq=Bq, kb=kb: p.op(
                    "V", lambda v: v.tensor_mul(Bq, Bq, qa),
                    reads=["q_a", kb], writes=[kb])))
                th.append((lambda Bq=Bq, kb=kb: p.op(
                    "V", lambda v: v.tensor_tensor(
                        Bq.rearrange("p (g s) -> p g s", s=64),
                        Bq.rearrange("p (g s) -> p g s", s=64),
                        qsc_bf[:].rearrange("p (g o) -> p g o", o=1)
                        .broadcast_to([128, 32, 64]),
                        ALU.mult), reads=[kb, "q_sc"], writes=[kb])))
                return th

            def f1_quant_thunks(j):
                par = j % 2
                th = []
                th.append(lambda: p.op("Y", (lambda j=j: lambda q: q.dma_start(
                    out=gate_f, in_=wgu_h[j * 128:(j + 1) * 128, :]))(),
                    writes=[("gf",)], dma=True))
                th.append(lambda: p.op("Y", (lambda j=j: lambda q: q.dma_start(
                    out=upf_f, in_=wgu_h[HF + j * 128:HF + (j + 1) * 128, :]))(),
                    writes=[("uf",)], dma=True))
                th += qchain2(gate_f, ("gf",), qsg[:], "q_bg")
                th += qchain2(upf_f, ("uf",), qm[:], "q_bu")
                pairs = [(qsg[:, eb * 128:(eb + 1) * 128], "q_bg",
                          gT_ap(par, eb), ("gT", par)) for eb in range(EB)]
                th += transpose_thunks(pairs, id_bf[:], "id_bf")
                pairs = [(qm[:, eb * 128:(eb + 1) * 128], "q_bu",
                          uT_ap(par, eb), ("uT", par)) for eb in range(EB)]
                th += transpose_thunks(pairs, id_bf[:], "id_bf")
                return th

            pendW = []
            prevT = None
            for (vth, tth) in wp_chunks():
                pendW += vth
                if prevT:
                    pendW += prevT
                prevT = tth
            pendW += prevT
            pending = f1_quant_thunks(0)
            while pending:
                pending.pop(0)()
            for j in range(64):
                par = j % 2
                pending = f1_quant_thunks(j + 1) if j + 1 < 64 else []
                for b2 in range(B):
                    ln1Tb = (bigA if b2 == 0 else bigB)
                    for tc in range(4):
                        g3 = (j * 8 + b2 * 4 + tc) % 3
                        gps = [o_psA[:, 0:512], s_ps[:, 0:512],
                               s_ps[:, 1024:1536]][g3]
                        ups = [o_psB[:, 0:512], s_ps[:, 512:1024],
                               dmy_ps[:, 0:512]][g3]
                        gk = ("gups", g3)
                        uk = ("uups", g3)

                        def mk_mm(dst, wap_fn, tc=tc, b2=b2, ln1Tb=ln1Tb, par=par):
                            def f(t):
                                ins = None
                                for eb2 in range(EB):
                                    ins = t.matmul(
                                        dst, wap_fn(par, eb2),
                                        ln1Tb[:, eb2 * N + tc * 512:eb2 * N + (tc + 1) * 512],
                                        start=(eb2 == 0), stop=(eb2 == EB - 1))
                                return ins
                            return f
                        p.op("T", mk_mm(gps, gT_ap), reads=[("gT", par)] + l1_keys,
                             writes=[gk])
                        p.op("T", mk_mm(ups, uT_ap), reads=[("uT", par)] + l1_keys,
                             writes=[uk])
                        p.op("S", (lambda gps=gps: lambda s: s.activation(
                            wf32[:, 0:512], gps, AF.Silu))(),
                            reads=[gk], writes=["sgf"])
                        p.op("V", (lambda ups=ups, tc=tc, b2=b2: lambda v: v.tensor_mul(
                            hst(b2)[:, tc * 512:(tc + 1) * 512], wf32[:, 0:512], ups))(),
                            reads=["sgf", uk], writes=[("hb", b2, tc)])
                        p.op("S", (lambda tc=tc, b2=b2: lambda s: s.activation(
                            h2v[:, tc * 512:(tc + 1) * 512],
                            hst(b2)[:, tc * 512:(tc + 1) * 512], AF.Square))(),
                            reads=[("hb", b2, tc)], writes=[("h2", tc)])
                        for _ in range(3):
                            if pending:
                                pending.pop(0)()
                        for _ in range(4):
                            if pendW:
                                pendW.pop(0)()
                    def mk_ssq(b2=b2, j=j):
                        def f(t):
                            ins = None
                            for ttl in range(NT):
                                ins = t.matmul(s_ps[:, 1536 + ttl:1536 + ttl + 1],
                                               h2v[:, ttl * 128:(ttl + 1) * 128],
                                               ones_s[:], start=True, stop=True)
                            return ins
                        return f

                    def emit_ssq(b2=b2, j=j):
                        p.op("T", mk_ssq(b2, j),
                             reads=[("h2", tc2) for tc2 in range(4)] + ["ones"],
                             writes=["ssq_ps"])
                        p.op("V", (lambda b2=b2: lambda v: v.tensor_add(
                            ssq_sb[:, b2 * 16:(b2 + 1) * 16],
                            ssq_sb[:, b2 * 16:(b2 + 1) * 16],
                            s_ps[:, 1536:1552]))(),
                            reads=["ssq_ps", "ssq"], writes=["ssq"])
                    emit_ssq()
                    p.op("G", (lambda b2=b2, j=j, hpar=b2: lambda q: q.dma_start(
                        out=h_scr[b2, :, j].rearrange("q p t -> p q t"),
                        in_=hst(b2).rearrange("p (q t) -> p q t", t=128)))(),
                        reads=[("hb", b2, tc2) for tc2 in range(4)],
                        writes=[("hw", b2, j)], dma=True)
                while pending:
                    pending.pop(0)()
            while pendW:
                pendW.pop(0)()

        out_keys = []

        def f2_proj():
            p.barrier()
            p.op("V", lambda v: v.tensor_scalar(
                inv_sb[:], ssq_sb[:], 1.0 / HF, RMS_EPS, ALU.mult, ALU.add),
                reads=["ssq"], writes=["inv1"])
            p.op("S", lambda s: s.activation(inv_sb[:], inv_sb[:], AF.Ln),
                 reads=["inv1"], writes=["inv2"])
            p.op("S", lambda s: s.activation(inv_sb[:], inv_sb[:], AF.Exp, scale=-0.5),
                 reads=["inv2"], writes=["inv"])
            ht_keys_o = ["q_a", "q_sg", "q_m", "q_hat", ("gT", 0)]
            finsq = ovl1[:, 8192:10240]
            finln = ovl1[:, 10240:12288]

            def fin_tile(gt):
                bf, tf = gt // NT, gt % NT
                p.op("Y", (lambda bf=bf, tf=tf: lambda q: q.dma_start(
                    out=xslot(0), in_=ffn_scr[bf, tf]))(),
                    reads=[("fw", gt, 0), ("fw", gt, 1)],
                    writes=["ffs", ("xsl", 0)], dma=True)
                p.op("Y", (lambda bf=bf, tf=tf: lambda q: q.dma_start(
                    out=xslot(1), in_=x1_scr[bf, tf]))(),
                    writes=["x1s", ("xsl", 1)], dma=True)
                ln_cols("ffs", xslot(0), E, finsq, ["finsq"], LN_EPS)
                ln_apply(["finln"], finln, "ffs", xslot(0))
                p.op("V", lambda v: v.tensor_add(xslot(1), xslot(1), finln),
                     reads=["x1s", "finln"], writes=["x1s", ("xsl", 1)])
                k = ("ow", gt)
                p.op("G", (lambda bf=bf, tf=tf: lambda q: q.dma_start(
                    out=out_h[bf, tf], in_=xslot(1)))(),
                    reads=["x1s"], writes=[k], dma=True)
                out_keys.append(k)

            for dh in range(2):
                for dq2 in range(2):
                    X = wptq(dq2)
                    for grp in range(8):
                        p.op("Y", (lambda dh=dh, dq2=dq2, grp=grp, X=X:
                                   lambda qe: qe.dma_start(
                            out=X[:, grp * 4096:(grp + 1) * 4096]
                            .rearrange("p (k t) -> p k t", t=512),
                            in_=whq_scr[2 * dh + dq2, grp * 8:(grp + 1) * 8]
                            .rearrange("k p t -> p k t")))(),
                            writes=[("wq", dq2, grp)], dma=True)
                for gt in range(32):
                    b, tt = gt // NT, gt % NT
                    odd = gt % 2 == 1
                    htbuf = ovl1[:, 0:8192] if odd else wbf[:]
                    htk = ht_keys_o if odd else [("wbf", i4) for i4 in range(4)]
                    p.op("Y", (lambda b=b, tt=tt, htbuf=htbuf: lambda q: q.dma_start(
                        out=htbuf.rearrange("p (q t) -> p q t", t=128),
                        in_=h_scr[b, tt].rearrange("q p t -> p q t")))(),
                        writes=["htile%d" % (gt % 2)] + htk, dma=True)
                    for dq2 in range(2):
                        X = wptq(dq2)
                        yps = o_psA if dq2 == 0 else o_psB
                        for grp in range(8):
                            def mk_y(X=X, yps=yps, htbuf=htbuf, grp=grp):
                                def f(t):
                                    ins = None
                                    for j in range(grp * 8, (grp + 1) * 8):
                                        ins = t.matmul(
                                            yps[:, 0:512],
                                            htbuf[:, j * 128:(j + 1) * 128],
                                            X[:, j * 512:(j + 1) * 512],
                                            start=(j == 0), stop=(j == 63))
                                    return ins
                                return f
                            p.op("T", mk_y(),
                                 reads=["htile%d" % (gt % 2)] + htk +
                                 [("wq", dq2, grp)],
                                 writes=[("yps", dq2)])
                        p.op("V", (lambda gt=gt, dq2=dq2, yps=yps: lambda v: v.tensor_scalar(
                            ovl2[:, 0:4096].bitcast(F32)[:, dq2 * 512:(dq2 + 1) * 512],
                            yps[:, 0:512], inv_sb[:, gt:gt + 1], None, ALU.mult))(),
                            reads=[("yps", dq2), "inv"], writes=[("yf", dq2)])
                    p.op("G", (lambda b=b, tt=tt, dh=dh: lambda q: q.dma_start(
                        out=ffn_scr[b, tt][:, dh * 1024:(dh + 1) * 1024],
                        in_=ovl2[:, 0:4096].bitcast(F32)[:, 0:1024]))(),
                        reads=[("yf", 0), ("yf", 1)], writes=[("fw", gt, dh)], dma=True)
                    if dh == 1 and gt >= 2:
                        fin_tile(gt - 2)
                if dh == 1:
                    fin_tile(30)
                    fin_tile(31)

        def final_ln():
            p.op("Y", None, reads=out_keys)

        stage = 0
        for b in range(B):
            if upto >= 1 + b * 3:
                attention(b)
                if debug and b == 0:
                    p.barrier()
                    p.op("G", lambda q: q.dma_start(out=tap_ctok, in_=ctok[:]), reads=["ctok"], dma=True)
                    p.op("G", lambda q: q.dma_start(out=tap_cum, in_=cum[:, :]), dma=True)
                    p.op("G", lambda q: q.dma_start(out=tap_rkp, in_=rkp), dma=True)
                    p.op("G", lambda q: q.dma_start(out=tap_vp, in_=vp), dma=True)
                    p.op("G", lambda q: q.dma_start(out=tap_rqT, in_=bigB[:]), dma=True)
                    p.barrier()
            if upto >= 2 + b * 3:
                p5_attention_core(b)
                if debug and b == 0:
                    p.barrier()
                    p.op("G", lambda q: q.dma_start(out=tap_oT, in_=bigA[:]), dma=True)
                    p.barrier()
            if upto >= 3 + b * 3:
                p6_out_proj(b)
        if upto >= 7:
            f1_gateup_both()
        if upto >= 8:
            f2_proj()
        if upto >= 9:
            final_ln()

        p.emit(block, sems)
    return nc


_NC_CACHE = {}


def _get_nc(debug=False):
    if debug not in _NC_CACHE:
        _NC_CACHE[debug] = build_nc(debug)
    return _NC_CACHE[debug]


def _host_prep(d):
    Wq = np.asarray(d["L0_attn_Wq"], np.float32)
    Wkv = np.asarray(d["L0_attn_Wkv"], np.float32)
    Wkvh = np.asarray(d["L0_attn_Wkvh"], np.float32)
    Wg = np.asarray(d["L0_attn_Wg"], np.float32)
    Wout = np.asarray(d["L0_attn_Wout"], np.float32)
    wq = np.ascontiguousarray(Wq.T.reshape(EB, 128, 4, 512).transpose(2, 0, 1, 3))
    wkv = np.ascontiguousarray(Wkv.T.reshape(EB, 128, 256))
    wkvh = np.ascontiguousarray(Wkvh.T.reshape(EB, 128, 32))
    wg = np.ascontiguousarray(Wg.T.reshape(EB, 128, 32))
    wout = np.ascontiguousarray(Wout.T.reshape(EB, 128, E))
    ident = np.eye(128, dtype=BFNP)
    mask = np.where(np.arange(128)[None, :] <= np.arange(128)[:, None],
                    np.float32(0.0), np.float32(-1e30)).astype(np.float32)
    pos = np.arange(N, dtype=np.float64)
    inv = 10000.0 ** (-np.arange(0, D, 2, dtype=np.float64) / D)
    ang = pos[:, None] * inv[None, :]
    cos_tok = np.ascontiguousarray(np.cos(ang).astype(np.float32).astype(BFNP)
                                   .reshape(NT, 128, 64).transpose(1, 0, 2)
                                   .reshape(128, NT * 64))
    sin_tok = np.ascontiguousarray(np.sin(ang).astype(np.float32).astype(BFNP)
                                   .reshape(NT, 128, 64).transpose(1, 0, 2)
                                   .reshape(128, NT * 64))
    ones_col = np.ones((128, 1), dtype=BFNP)
    com = dict(wq=wq, wkv=wkv, wkvh=wkvh, wg=wg, wout=wout,
               wgu=np.ascontiguousarray(np.asarray(d["L0_ffn_w_gate_up"], np.float32)),
               wp=np.ascontiguousarray(np.asarray(d["L0_ffn_w_proj"], np.float32)),
               ident=ident, mask=mask, cos_tok=cos_tok, sin_tok=sin_tok,
               ones_col=ones_col)
    return com


def kernel(**inputs):
    x = np.asarray(inputs["L0_attn_x"], np.float32)
    com = _host_prep(inputs)
    nc = _get_nc(False)
    in_maps = []
    for c in range(8):
        m = dict(com)
        m["x"] = np.ascontiguousarray(
            x[c * B:(c + 1) * B].reshape(B, NT, 128, E))
        in_maps.append(m)
    res = run_bass_kernel_spmd(nc, in_maps, core_ids=list(range(8)))
    outs = [r["out"].reshape(B, N, E) for r in res.results]
    return np.concatenate(outs, axis=0)


# version 33 (final)
# speedup vs baseline: 1.0096x; 1.0033x over previous
import numpy as np
import ml_dtypes
import concourse.bass as bass
import concourse.mybir as mybir
from concourse.bass_utils import run_bass_kernel_spmd
from contextlib import ExitStack

F32 = mybir.dt.float32
BF16 = mybir.dt.bfloat16
AF = mybir.ActivationFunctionType
ALU = mybir.AluOpType
AX = mybir.AxisListType
BFNP = ml_dtypes.bfloat16

B = 2
N = 2048
E = 2048
H = 16
D = 128
HF = 8192
NT = N // 128
EB = E // 128
SCALE = 1.0 / float(np.sqrt(np.float64(D)))
LN_EPS = 1e-5
RMS_EPS = 1.1920928955078125e-07
QTHR = 0.5009765625

ENGS = ("T", "V", "S", "G", "Y")
DMA_SLOTS = {"Y": 8, "G": 6}


class Prog:
    def __init__(self):
        self.ops = []
        self.sem_ids = list(ENGS)
        for e, k in DMA_SLOTS.items():
            self.sem_ids += [(e, s) for s in range(k)]
        self.cnt = {sid: 0 for sid in self.sem_ids}
        self.dmac = {e: 0 for e in DMA_SLOTS}
        self.buf = {}

    def op(self, eng, fn, reads=(), writes=(), dma=False):
        waits = {}

        def add(tok):
            if tok is None:
                return
            sid, c2 = tok
            if waits.get(sid, 0) < c2:
                waits[sid] = c2

        for k in reads:
            b = self.buf.get(k)
            if b:
                add(b[0])
        for k in writes:
            b = self.buf.get(k)
            if b:
                add(b[0])
                for t in b[1]:
                    add(t)
        if dma:
            slot = (eng, self.dmac[eng] % DMA_SLOTS[eng])
            self.dmac[eng] += 1
            self.cnt[slot] += 16
            tok = (slot, self.cnt[slot])
            sid = slot
            inc = 16
        else:
            if fn is None:
                inc = 0
            else:
                inc = 1
            self.cnt[eng] += inc
            tok = (eng, self.cnt[eng])
            sid = eng
        for k in reads:
            self.buf.setdefault(k, [None, []])[1].append(tok)
        for k in writes:
            self.buf[k] = [tok, []]
        self.ops.append((eng, fn, waits, inc, sid))
        return tok

    def barrier(self):
        snap = dict(self.cnt)
        for e in ENGS:
            waits = {sid: c for sid, c in snap.items() if c > 0 and sid != e}
            self.ops.append((e, None, waits, 0, e))

    def emit(self, block, sems):
        per = {e: [] for e in ENGS}
        for o in self.ops:
            per[o[0]].append(o)

        def mk_body(name):
            def f(eng):
                water = {}
                for (_, fn, waits, inc, sid) in per[name]:
                    for s in sorted(waits, key=repr):
                        v = waits[s]
                        if water.get(s, 0) < v:
                            eng.wait_ge(sems[s], v)
                            water[s] = v
                    if fn is not None:
                        ins = fn(eng)
                        if inc:
                            ins.then_inc(sems[sid], inc)
            return f

        block.tensor(mk_body("T"))
        block.vector(mk_body("V"))
        block.scalar(mk_body("S"))
        block.gpsimd(mk_body("G"))
        block.sync(mk_body("Y"))


def build_nc(debug=False, upto=9):
    nc = bass.Bass("TRN2", target_bir_lowering=False, debug=False)
    ap = lambda t: t.ap()
    scr_kind = "ExternalOutput" if debug else "Internal"

    x_h = ap(nc.dram_tensor("x", [B, NT, 128, E], F32, kind="ExternalInput"))
    wq_h = ap(nc.dram_tensor("wq", [4, EB, 128, 512], F32, kind="ExternalInput"))
    wkv_h = ap(nc.dram_tensor("wkv", [EB, 128, 256], F32, kind="ExternalInput"))
    wkvh_h = ap(nc.dram_tensor("wkvh", [EB, 128, 32], F32, kind="ExternalInput"))
    wg_h = ap(nc.dram_tensor("wg", [EB, 128, 32], F32, kind="ExternalInput"))
    wout_h = ap(nc.dram_tensor("wout", [EB, 128, E], F32, kind="ExternalInput"))
    wgu_h = ap(nc.dram_tensor("wgu", [2 * HF, E], F32, kind="ExternalInput"))
    wp_h = ap(nc.dram_tensor("wp", [E, HF], F32, kind="ExternalInput"))
    id_h = ap(nc.dram_tensor("ident", [128, 128], BF16, kind="ExternalInput"))
    mask_h = ap(nc.dram_tensor("mask", [128, 128], F32, kind="ExternalInput"))
    cos_h = ap(nc.dram_tensor("cos_tok", [128, NT * 64], BF16, kind="ExternalInput"))
    sin_h = ap(nc.dram_tensor("sin_tok", [128, NT * 64], BF16, kind="ExternalInput"))
    ones_h = ap(nc.dram_tensor("ones_col", [128, 1], BF16, kind="ExternalInput"))

    out_h = ap(nc.dram_tensor("out", [B, NT, 128, E], F32, kind="ExternalOutput"))

    x1_scr = ap(nc.dram_tensor("x1_scr", [B, NT, 128, E], F32, kind=scr_kind))
    whq_scr = ap(nc.dram_tensor("whq_scr", [4, 64, 128, 512], BF16, kind=scr_kind))
    ln1T_scr = ap(nc.dram_tensor("ln1T_scr", [B, EB, 128, N], BF16, kind=scr_kind))
    h_scr = ap(nc.dram_tensor("h_scr", [B, NT, 64, 128, 128], BF16, kind=scr_kind))
    ffn_scr = ap(nc.dram_tensor("ffn_scr", [B, NT, 128, E], F32, kind=scr_kind))
    if debug:
        tap_cum = ap(nc.dram_tensor("tap_cum", [32, 6144], F32, kind="ExternalOutput"))
        tap_ctok = ap(nc.dram_tensor("tap_ctok", [128, 512], F32, kind="ExternalOutput"))
        tap_rkp = ap(nc.dram_tensor("tap_rkp", [128, 2048], BF16, kind="ExternalOutput"))
        tap_vp = ap(nc.dram_tensor("tap_vp", [128, 2048], BF16, kind="ExternalOutput"))
        tap_rqT = ap(nc.dram_tensor("tap_rqT", [128, 32768], BF16, kind="ExternalOutput"))
        tap_oT = ap(nc.dram_tensor("tap_oT", [128, 32768], BF16, kind="ExternalOutput"))

    st_ = ExitStack()
    with st_:
        SB = lambda nm, sh, dt: st_.enter_context(nc.sbuf_tensor(nm, sh, dt))
        bigA = SB("bigA", [128, 32768], BF16)
        bigB = SB("bigB", [128, 32768], BF16)
        wf32 = SB("wf32", [128, 4096], F32)
        wbf = SB("wbf", [128, 8192], BF16)
        ovl1 = SB("ovl1", [128, 12288], BF16)
        ovl2 = SB("ovl2", [128, 6144], BF16)
        ctok = SB("ctok", [128, 512], F32)
        id_bf = SB("id_bf", [128, 128], BF16)
        mask_s = SB("mask_s", [128, 128], F32)
        cos_s = SB("cos_s", [128, NT * 64], BF16)
        sin_s = SB("sin_s", [128, NT * 64], BF16)
        ones_s = SB("ones_s", [128, 1], BF16)
        cols = SB("cols", [128, 24], F32)
        qcols = SB("qcols", [128, 256], F32)
        ssq_sb = SB("ssq_sb", [128, 32], F32)
        inv_sb = SB("inv_sb", [128, 32], F32)
        qscb_t = SB("qscb_t", [128, 32], BF16)
        wscb_t = SB("wscb_t", [128, 32], BF16)

        s_ps = st_.enter_context(nc.psum_tensor("s_ps", [128, 2048], F32))
        dmy_ps = st_.enter_context(nc.psum_tensor("dmy_ps", [128, 512], F32))
        o_psA = st_.enter_context(nc.psum_tensor("o_psA", [128, 512], F32))
        o_psB = st_.enter_context(nc.psum_tensor("o_psB", [128, 512], F32))
        tr_ps = st_.enter_context(nc.psum_tensor("tr_ps", [128, 1024], BF16))

        sems = {}
        for e in ENGS:
            sems[e] = st_.enter_context(nc.semaphore("sem" + e))
        for e, k in DMA_SLOTS.items():
            for s2 in range(k):
                sems[(e, s2)] = st_.enter_context(
                    nc.semaphore("sem%s%d" % (e, s2)))
        block = st_.enter_context(nc.Block())
        p = Prog()

        cum = ovl1[:32, :].bitcast(F32)
        cA = cum[:, 0:2048]
        cB = cum[:, 2048:4096]
        cC = cum[:, 4096:6144]
        khT = ovl1[:, 0:2048]
        vh_sb = ovl1[:, 2048:4096]
        pT_sb = ovl1[:, 4096:6144]
        kstage = ovl1[:, 6144:8192]
        qa = ovl1[:, 0:2048]
        qsg = ovl1[:, 2048:4096]
        qm = ovl1[:, 4096:6144]
        rkp = ovl2[:, 0:2048]
        vp = ovl2[:, 2048:4096]
        rsc = ovl2[:, 4096:6144]
        ovl2f = ovl2[:, 0:4096].bitcast(F32)
        hst = lambda par: ovl2[:, par * 2048:(par + 1) * 2048]
        h2v = ovl2[:, 4096:6144]
        xslot = lambda i: wf32[:, i * 2048:(i + 1) * 2048]
        wbfs = lambda i: wbf[:, i * 2048:(i + 1) * 2048]
        bBf32 = bigB[:].bitcast(F32)
        gup_f = lambda par: bBf32[:, par * 2048:(par + 1) * 2048]
        up_f = lambda par: bBf32[:, (2 + par) * 2048:(3 + par) * 2048]
        gT = lambda par: bigB[:, 16384 + par * 2048:16384 + (par + 1) * 2048]
        uT = lambda par: bigB[:, 20480 + par * 2048:20480 + (par + 1) * 2048]

        mxc = cols[:, 0:1]
        biasc = cols[:, 1:2]
        lc = cols[:, 2:3]
        linvc = cols[:, 3:4]
        m1c = cols[:, 4:5]
        m2c = cols[:, 5:6]
        vrc = cols[:, 6:7]
        rsdc = cols[:, 7:8]
        t0c = cols[:, 8:9]
        qred = qcols[:, 0:32]
        qthr = qcols[:, 32:64]
        qsc_bf = qscb_t[:]

        ln0T = lambda eb: bigA[:, eb * N:(eb + 1) * N]
        oT = lambda h: bigA[:, h * N:(h + 1) * N]
        rqT = lambda h: bigB[:, h * N:(h + 1) * N]
        rq_tile = lambda h, i: bigB[:, ((h // 4) * 16 + i) * 512 + (h % 4) * 128:
                                    ((h // 4) * 16 + i) * 512 + (h % 4) * 128 + 128]
        woutb = lambda hdb: bigB[:, hdb * N:(hdb + 1) * N]
        ln1T = lambda eb: bigA[:, eb * N:(eb + 1) * N]
        wptq = lambda half: (bigA if half == 0 else bigB)

        for dst, src, key in [(id_bf[:], id_h, "id_bf"), (mask_s[:], mask_h, "mask"),
                              (cos_s[:], cos_h, "cos"), (sin_s[:], sin_h, "sin"),
                              (ones_s[:], ones_h, "ones")]:
            p.op("Y", (lambda d, s: lambda q: q.dma_start(out=d, in_=s))(dst, src),
                 writes=[key], dma=True)

        def dummy_mm(n):
            def mk(n=n):
                def f(t):
                    ins = None
                    for _ in range(n):
                        ins = t.matmul(dmy_ps[:, 0:512], id_bf[:],
                                       cos_s[:, 0:512], start=True, stop=True)
                    return ins
                return f
            p.op("T", mk(), reads=["id_bf", "cos"], writes=["dmy"])

        dummy_mm(30)

        def ln_cols(src_key, src_ap, n_free, sq_ap, sq_key, eps, junk_ap=None):
            if not isinstance(sq_key, list):
                sq_key = [sq_key]
            if junk_ap is None:
                junk_ap = sq_ap
            p.op("S", (lambda a, d: lambda s: s.activation(
                d, a, AF.Copy, accum_out=m1c))(src_ap, junk_ap),
                reads=[src_key], writes=sq_key + ["c_m1"])
            p.op("S", (lambda a, d: lambda s: s.activation(
                d, a, AF.Square, accum_out=m2c))(src_ap, sq_ap),
                reads=[src_key], writes=sq_key + ["c_m2"])
            p.op("V", (lambda n: lambda v: v.tensor_scalar(
                m1c, m1c, 1.0 / n, None, ALU.mult))(n_free),
                reads=["c_m1"], writes=["c_m1"])
            p.op("V", (lambda n: lambda v: v.tensor_scalar(
                m2c, m2c, 1.0 / n, None, ALU.mult))(n_free),
                reads=["c_m2"], writes=["c_m2"])
            p.op("V", lambda v: v.tensor_mul(t0c, m1c, m1c),
                 reads=["c_m1"], writes=["c_t0"])
            p.op("V", lambda v: v.tensor_sub(vrc, m2c, t0c),
                 reads=["c_m2", "c_t0"], writes=["c_vr"])
            p.op("V", (lambda e: lambda v: v.tensor_scalar(
                vrc, vrc, e, None, ALU.add))(eps),
                reads=["c_vr"], writes=["c_vr"])
            p.op("S", lambda s: s.activation(rsdc, vrc, AF.Ln),
                 reads=["c_vr"], writes=["c_rsd"])
            p.op("S", lambda s: s.activation(rsdc, rsdc, AF.Exp, scale=-0.5),
                 reads=["c_rsd"], writes=["c_rsd"])

        def ln_apply(dst_key, dst_ap, src_key, src_ap):
            if not isinstance(dst_key, list):
                dst_key = [dst_key]
            p.op("V", (lambda d, a: lambda v: v.tensor_scalar(
                d, a, m1c, rsdc, ALU.subtract, ALU.mult))(dst_ap, src_ap),
                reads=[src_key, "c_m1", "c_rsd"], writes=dst_key)

        def transpose_thunks(pairs, ident, ident_key):
            ths = []
            for base in range(0, len(pairs), 8):
                grp = pairs[base:base + 8]
                ths.append((lambda grp=grp: transpose_batch(grp, ident, ident_key)))
            return ths

        def transpose_batch(pairs, ident, ident_key, eng="V"):
            for base in range(0, len(pairs), 8):
                grp = pairs[base:base + 8]

                def mk_t(grp=grp):
                    def f(t):
                        ins = None
                        for gi, (sa, _, _, _) in enumerate(grp):
                            ins = t.transpose(
                                tr_ps[:, gi * 128:(gi + 1) * 128], sa, ident)
                        return ins
                    return f
                rk = sorted({g[1] for g in grp})
                p.op("T", mk_t(), reads=rk + [ident_key], writes=["tr_ps"])

                contig = len(grp) > 1
                for gi in range(1, len(grp)):
                    a, bq = grp[gi - 1][2], grp[gi][2]
                    if not (a.tensor is bq.tensor and
                            bq.offset == a.offset + 128):
                        contig = False
                        break
                if contig:
                    d0 = grp[0][2]
                    full = d0.tensor.ap()[:, d0.offset:d0.offset + 128 * len(grp)]
                    def mk_e1(grp=grp, full=full, eng=eng):
                        def f(v):
                            if eng == "V":
                                return v.tensor_copy(full, tr_ps[:, 0:128 * len(grp)])
                            return v.copy(full, tr_ps[:, 0:128 * len(grp)])
                        return f
                    wk = sorted({g[3] for g in grp})
                    p.op(eng, mk_e1(), reads=["tr_ps"], writes=wk)
                else:
                    def mk_e(grp=grp, eng=eng):
                        def f(v):
                            ins = None
                            for gi, (_, _, da, _) in enumerate(grp):
                                if eng == "V":
                                    ins = v.tensor_copy(da, tr_ps[:, gi * 128:(gi + 1) * 128])
                                else:
                                    ins = v.copy(da, tr_ps[:, gi * 128:(gi + 1) * 128])
                            return ins
                        return f
                    wk = sorted({g[3] for g in grp})
                    p.op(eng, mk_e(), reads=["tr_ps"], writes=wk)

        def rope_tile(src_key, tt, out_lo, out_hi, okl, okh, par=0):
            ct = cos_s[:, tt * 64:(tt + 1) * 64]
            st2 = sin_s[:, tt * 64:(tt + 1) * 64]
            t1 = rsc[:, par * 512:par * 512 + 64]
            t2 = rsc[:, par * 512 + 64:par * 512 + 128]
            r1 = qcols[:, par * 128:par * 128 + 64]
            r2 = qcols[:, par * 128 + 64:par * 128 + 128]
            p.op("V", lambda v: v.tensor_mul(r1, t1, ct),
                 reads=[src_key, "cos"], writes=["ro_r1"])
            p.op("V", lambda v: v.tensor_mul(r2, t2, st2),
                 reads=[src_key, "sin"], writes=["ro_r2"])
            p.op("V", (lambda o: lambda v: v.tensor_sub(o, r1, r2))(out_lo),
                 reads=["ro_r1", "ro_r2"], writes=[okl])
            p.op("V", lambda v: v.tensor_mul(r1, t2, ct),
                 reads=[src_key, "cos"], writes=["ro_r1"])
            p.op("V", lambda v: v.tensor_mul(r2, t1, st2),
                 reads=[src_key, "sin"], writes=["ro_r2"])
            p.op("V", (lambda o: lambda v: v.tensor_add(o, r1, r2))(out_hi),
                 reads=["ro_r1", "ro_r2"], writes=[okh])

        def quant_thunks(src_f32_ap, src_key, dst_bf_ap, dst_key):
            th = []
            th.append(lambda: p.op("S", (lambda a=src_f32_ap: lambda s: s.copy(qa, a))(),
                                   reads=[src_key], writes=["q_a"]))
            th.append(lambda: p.op("V", lambda v: v.tensor_reduce(
                qred, qa[:].rearrange("p (g s) -> p g s", s=64), axis=AX.X,
                op=ALU.add, apply_absolute_value=True),
                reads=["q_a"], writes=["q_red"]))
            th.append(lambda: p.op("V", lambda v: v.tensor_scalar(
                qsc_bf, qred, 1.0 / 64, 1e-8, ALU.mult, ALU.max),
                reads=["q_red"], writes=["q_sc"]))
            th.append(lambda: p.op("V", lambda v: v.tensor_scalar(
                qthr, qsc_bf, QTHR, None, ALU.mult),
                reads=["q_sc"], writes=["q_thr"]))
            th.append(lambda: p.op("S", lambda s: s.sign(qsg, qa),
                                   reads=["q_a"], writes=["q_sg"]))
            th.append(lambda: p.op("S", lambda s: s.activation(qm, qa, AF.Abs),
                                   reads=["q_a"], writes=["q_m", "q_hat"]))
            th.append(lambda: p.op("V", lambda v: v.tensor_tensor(
                qa[:].rearrange("p (g s) -> p g s", s=64),
                qm[:].rearrange("p (g s) -> p g s", s=64),
                qthr[:].rearrange("p (g o) -> p g o", o=1).broadcast_to([128, 32, 64]),
                ALU.is_gt), reads=["q_m", "q_thr"], writes=["q_a"]))
            th.append(lambda: p.op("V", lambda v: v.tensor_mul(qm, qa, qsg),
                                   reads=["q_a", "q_sg"], writes=["q_m", "q_hat"]))
            th.append(lambda: p.op("V", (lambda d=dst_bf_ap: lambda v: v.tensor_tensor(
                d.rearrange("p (g s) -> p g s", s=64),
                qm[:].rearrange("p (g s) -> p g s", s=64),
                qsc_bf[:].rearrange("p (g o) -> p g o", o=1).broadcast_to([128, 32, 64]),
                ALU.mult))(), reads=["q_m", "q_sc"],
                writes=[dst_key, "q_m"] if dst_key == "q_hat" else [dst_key]))
            return th

        def quant_chunk(src_f32_ap, src_key, dst_bf_ap, dst_key):
            p.op("S", (lambda a: lambda s: s.copy(qa, a))(src_f32_ap),
                 reads=[src_key], writes=["q_a"])
            p.op("V", lambda v: v.tensor_reduce(
                qred, qa[:].rearrange("p (g s) -> p g s", s=64), axis=AX.X,
                op=ALU.add, apply_absolute_value=True),
                reads=["q_a"], writes=["q_red"])
            p.op("V", lambda v: v.tensor_scalar(
                qsc_bf, qred, 1.0 / 64, 1e-8, ALU.mult, ALU.max),
                reads=["q_red"], writes=["q_sc"])
            p.op("V", lambda v: v.tensor_scalar(qthr, qsc_bf, QTHR, None, ALU.mult),
                 reads=["q_sc"], writes=["q_thr"])
            p.op("S", lambda s: s.sign(qsg, qa), reads=["q_a"], writes=["q_sg"])
            p.op("S", lambda s: s.activation(qm, qa, AF.Abs),
                 reads=["q_a"], writes=["q_m", "q_hat"])
            p.op("V", lambda v: v.tensor_tensor(
                qa[:].rearrange("p (g s) -> p g s", s=64),
                qm[:].rearrange("p (g s) -> p g s", s=64),
                qthr[:].rearrange("p (g o) -> p g o", o=1).broadcast_to([128, 32, 64]),
                ALU.is_gt), reads=["q_m", "q_thr"], writes=["q_a"])
            p.op("V", lambda v: v.tensor_mul(qm, qa, qsg),
                 reads=["q_a", "q_sg"], writes=["q_m", "q_hat"])
            p.op("V", (lambda d: lambda v: v.tensor_tensor(
                d.rearrange("p (g s) -> p g s", s=64),
                qm[:].rearrange("p (g s) -> p g s", s=64),
                qsc_bf[:].rearrange("p (g o) -> p g o", o=1).broadcast_to([128, 32, 64]),
                ALU.mult))(dst_bf_ap), reads=["q_m", "q_sc"],
                writes=[dst_key, "q_m"] if dst_key == "q_hat" else [dst_key])

        wstg = wf32[:, 512:1536]
        wfbf = wf32[:].bitcast(BF16)
        ctok_bf = ctok[:].bitcast(BF16)
        wqa_c = [wfbf[:, 3072:4096], ctok_bf[:, 0:1024]]
        wqb_c = [wfbf[:, 5120:6144], wfbf[:, 7168:8192]]
        wred_c = [qcols[:, 64:80], qcols[:, 80:96]]
        wthr_c = [qcols[:, 96:112], qcols[:, 112:128]]
        wscb_c = [wscb_t[:, 0:16], wscb_t[:, 16:32]]

        def wp_chunks():
            out = []
            cidx = 0
            for q in range(4):
                for db2 in range(4):
                    for hs2 in range(8):
                        pc = cidx % 2
                        A = wqa_c[pc]
                        Bq = wqb_c[pc]
                        red = wred_c[pc]
                        thr = wthr_c[pc]
                        scb = wscb_c[pc]
                        ka, kb, kt = "w_a%d" % pc, "w_b%d" % pc, "w_t%d" % pc
                        kr, ks, kh = ("w_red%d" % pc, "w_sc%d" % pc,
                                      "w_thr%d" % pc)
                        vth = []
                        vth.append((lambda q=q, db2=db2, hs2=hs2: p.op(
                            "Y", (lambda q=q, db2=db2, hs2=hs2:
                                  lambda qe: qe.dma_start(
                                out=wstg,
                                in_=wp_h[q * 512 + db2 * 128:
                                         q * 512 + (db2 + 1) * 128,
                                         hs2 * 1024:(hs2 + 1) * 1024]))(),
                            writes=["w_stg"], dma=True)))
                        vth.append((lambda A=A, ka=ka, kt=kt: p.op(
                            "S", lambda s: s.copy(A, wstg),
                            reads=["w_stg"], writes=[ka, kt])))
                        vth.append((lambda A=A, red=red, ka=ka, kr=kr: p.op(
                            "V", lambda v: v.tensor_reduce(
                                red, A.rearrange("p (g s) -> p g s", s=64),
                                axis=AX.X, op=ALU.add,
                                apply_absolute_value=True),
                            reads=[ka], writes=[kr])))
                        vth.append((lambda red=red, scb=scb, kr=kr, ks=ks: p.op(
                            "V", lambda v: v.tensor_scalar(
                                scb, red, 1.0 / 64, 1e-8, ALU.mult, ALU.max),
                            reads=[kr], writes=[ks])))
                        vth.append((lambda thr=thr, scb=scb, ks=ks, kh=kh: p.op(
                            "V", lambda v: v.tensor_scalar(
                                thr, scb, QTHR, None, ALU.mult),
                            reads=[ks], writes=[kh])))
                        vth.append((lambda A=A, Bq=Bq, ka=ka, kb=kb: p.op(
                            "S", lambda s: s.sign(Bq, A),
                            reads=[ka], writes=[kb])))
                        vth.append((lambda A=A, ka=ka: p.op(
                            "S", lambda s: s.activation(A, A, AF.Abs),
                            reads=[ka], writes=[ka])))
                        vth.append((lambda A=A, thr=thr, ka=ka, kh=kh: p.op(
                            "V", lambda v: v.tensor_tensor(
                                A.rearrange("p (g s) -> p g s", s=64),
                                A.rearrange("p (g s) -> p g s", s=64),
                                thr.rearrange("p (g o) -> p g o", o=1)
                                .broadcast_to([128, 16, 64]),
                                ALU.is_gt), reads=[ka, kh], writes=[ka])))
                        vth.append((lambda A=A, Bq=Bq, ka=ka, kb=kb: p.op(
                            "V", lambda v: v.tensor_mul(Bq, Bq, A),
                            reads=[ka, kb], writes=[kb])))
                        vth.append((lambda Bq=Bq, scb=scb, kb=kb, ks=ks: p.op(
                            "V", lambda v: v.tensor_tensor(
                                Bq.rearrange("p (g s) -> p g s", s=64),
                                Bq.rearrange("p (g s) -> p g s", s=64),
                                scb.rearrange("p (g o) -> p g o", o=1)
                                .broadcast_to([128, 16, 64]),
                                ALU.mult), reads=[kb, ks], writes=[kb])))
                        pairs = [(Bq[:, k * 128:(k + 1) * 128], kb,
                                  A[:, k * 128:(k + 1) * 128], kt)
                                 for k in [0, 2, 4, 6, 1, 3, 5, 7]]
                        tth = transpose_thunks(pairs, id_bf[:], "id_bf")
                        tth.append((lambda q=q, db2=db2, hs2=hs2, A=A, kt=kt:
                                    p.op("G", (lambda q=q, db2=db2, hs2=hs2,
                                               A=A: lambda qe: qe.dma_start(
                            out=whq_scr[q, hs2 * 8:(hs2 + 1) * 8, :,
                                        db2 * 128:(db2 + 1) * 128]
                            .rearrange("k p t -> p k t"),
                            in_=A.rearrange("p (k t) -> p k t", t=128)))(),
                            reads=[kt], writes=[("whqw", q, db2, hs2)],
                            dma=True)))
                        out.append((vth, tth))
                        cidx += 1
            return out

        def attention(b):
            p.barrier()
            for tt in range(NT):
                par = tt % 2
                xs = xslot(par)
                p.op("Y", (lambda b=b, tt=tt, d=xs: lambda q: q.dma_start(
                    out=d, in_=x_h[b, tt]))(), writes=[("xsl", par)], dma=True)
                ln_cols(("xsl", par), xs, E, wbfs(2), ["sq_s", ("wbf", 2)], LN_EPS)
                dummy_mm(3)
                ln_apply(["lnbf", ("wbf", 3)], wbfs(3), ("xsl", par), xs)
                dummy_mm(3)
                pairs = [(wbfs(3)[:, eb * 128:(eb + 1) * 128], ("wbf", 3),
                          ln0T(eb)[:, tt * 128:(tt + 1) * 128], ("ln0T", eb))
                         for eb in range(EB)]
                transpose_batch(pairs, id_bf[:], "id_bf")

            ln0_keys = [("ln0T", eb) for eb in range(EB)]

            def stream_w128(src_ap, par, wkey_f, wkey_b):
                p.op("Y", (lambda s=src_ap, d=xslot(par): lambda q: q.dma_start(
                    out=d.rearrange("p (e d) -> p e d", d=128),
                    in_=s.rearrange("e p d -> p e d")))(),
                    writes=[wkey_f, ("xsl", par)], dma=True)
                p.op("V", (lambda par=par: lambda v: v.tensor_copy(
                    wbfs(par), xslot(par)))(), reads=[("xsl", par)], writes=[wkey_b, ("wbf", par)])

            q4s = ovl1[:, 0:8192]
            kvs = ovl1[:, 8192:12288]

            def stream_q4(hg):
                for c in range(4):
                    par = c % 2
                    p.op("Y", (lambda hg=hg, c=c, par=par: lambda q: q.dma_start(
                        out=xslot(par).rearrange("p (e d) -> p e d", d=512),
                        in_=wq_h[hg, c * 4:(c + 1) * 4].rearrange("e p d -> p e d")))(),
                        writes=[("xsl", par)], dma=True)
                    p.op("V", (lambda c=c, par=par: lambda v: v.tensor_copy(
                        q4s[:, c * 2048:(c + 1) * 2048], xslot(par)))(),
                        reads=[("xsl", par)], writes=[("q4b", c)])

            def stream_kv():
                for c in range(2):
                    par = c % 2
                    p.op("Y", (lambda c=c, par=par: lambda q: q.dma_start(
                        out=xslot(par).rearrange("p (e d) -> p e d", d=256),
                        in_=wkv_h[c * 8:(c + 1) * 8].rearrange("e p d -> p e d")))(),
                        writes=[("xsl", par)], dma=True)
                    p.op("V", (lambda c=c, par=par: lambda v: v.tensor_copy(
                        kvs[:, c * 2048:(c + 1) * 2048], xslot(par)))(),
                        reads=[("xsl", par)], writes=[("kvb",)])

            def q4_mm(hg, tt, width, rhs_ap, rkeys):
                qpar = tt % 2
                qps = (o_psA if qpar == 0 else o_psB)[:, 0:width]

                def mk_q(tt=tt, qps=qps, rhs_ap=rhs_ap, width=width):
                    def f(t):
                        ins = None
                        for eb in range(EB):
                            ins = t.matmul(qps,
                                           ln0T(eb)[:, tt * 128:(tt + 1) * 128],
                                           rhs_ap[:, eb * width:(eb + 1) * width],
                                           start=(eb == 0), stop=(eb == EB - 1))
                        return ins
                    return f
                p.op("T", mk_q(), reads=rkeys + ln0_keys, writes=[("qps", qpar)])
                return qps, qpar

            def head_post(h, tt, qps, qpar, hsub, kind):
                if kind == "v":
                    p.op("S", (lambda qps=qps, tt=tt, hsub=hsub: lambda s: s.copy(
                        vp[:, tt * 128:(tt + 1) * 128],
                        qps[:, hsub * 128:(hsub + 1) * 128]))(),
                        reads=[("qps", qpar)], writes=[("vp", tt)])
                    return
                if kind == "q":
                    if hsub != 0:
                        return
                    base = qpar * 512
                    ob = 1024 + qpar * 512
                    p.op("S", (lambda qps=qps, base=base: lambda s: s.copy(
                        rsc[:, base:base + 512], qps[:, 0:512]))(),
                        reads=[("qps", qpar)], writes=[("qbf", qpar)])
                    t3 = rsc[:, base:base + 512].rearrange(
                        "p (h c) -> p c h", c=128)
                    o3 = rsc[:, ob:ob + 512].rearrange(
                        "p (h c) -> p c h", c=128)
                    ct = cos_s[:, tt * 64:(tt + 1) * 64].rearrange(
                        "p (c o) -> p c o", o=1).broadcast_to([128, 64, 4])
                    st = sin_s[:, tt * 64:(tt + 1) * 64].rearrange(
                        "p (c o) -> p c o", o=1).broadcast_to([128, 64, 4])
                    r1 = ctok[:, 0:256].rearrange("p (c h) -> p c h", h=4)
                    r2 = ctok[:, 256:512].rearrange("p (c h) -> p c h", h=4)
                    t1, t2 = t3[:, 0:64, :], t3[:, 64:128, :]
                    olo, ohi = o3[:, 0:64, :], o3[:, 64:128, :]
                    p.op("V", lambda v: v.tensor_mul(r1, t1, ct),
                         reads=[("qbf", qpar), "cos"], writes=[("ro1",)])
                    p.op("V", lambda v: v.tensor_mul(r2, t2, st),
                         reads=[("qbf", qpar), "sin"], writes=[("ro2",)])
                    p.op("V", lambda v: v.tensor_sub(olo, r1, r2),
                         reads=[("ro1",), ("ro2",)], writes=[("o1", qpar)])
                    p.op("V", lambda v: v.tensor_mul(r1, t2, ct),
                         reads=[("qbf", qpar), "cos", ("o1", qpar)],
                         writes=[("ro1",)])
                    p.op("V", lambda v: v.tensor_mul(r2, t1, st),
                         reads=[("qbf", qpar), "sin"], writes=[("ro2",)])
                    p.op("V", lambda v: v.tensor_add(ohi, r1, r2),
                         reads=[("ro1",), ("ro2",)], writes=[("o2", qpar)])

                    def mk_tq(ob=ob):
                        def f(t):
                            ins = None
                            for k2 in range(4):
                                ins = t.transpose(
                                    tr_ps[:, k2 * 128:(k2 + 1) * 128],
                                    rsc[:, ob + k2 * 128:ob + (k2 + 1) * 128],
                                    id_bf[:])
                            return ins
                        return f
                    p.op("T", mk_tq(), reads=[("o1", qpar), ("o2", qpar),
                                              "id_bf"], writes=["tr_ps"])
                    p.op("V", (lambda h=h, tt=tt: lambda v: v.tensor_copy(
                        bigB[:, ((h // 4) * 16 + tt) * 512:
                             ((h // 4) * 16 + tt) * 512 + 512],
                        tr_ps[:, 0:512]))(),
                        reads=["tr_ps"],
                        writes=[("rqT", h), ("rqT", h + 1),
                                ("rqT", h + 2), ("rqT", h + 3)] + ["tr_ps"])
                else:
                    p.op("S", (lambda qps=qps, qpar=qpar, hsub=hsub:
                               lambda s: s.copy(
                        rsc[:, qpar * 512:qpar * 512 + 128],
                        qps[:, hsub * 128:(hsub + 1) * 128]))(),
                        reads=[("qps", qpar)], writes=[("qbf", qpar)])
                    ol = rkp[:, tt * 128:tt * 128 + 64]
                    oh = rkp[:, tt * 128 + 64:(tt + 1) * 128]
                    rope_tile(("qbf", qpar), tt, ol, oh, ("rkp", tt), ("rkp", tt),
                              par=qpar)

            q4keys = [("q4b", c) for c in range(4)]
            pend = []
            for hg in range(4):
                stream_q4(hg)
                for tt in range(NT):
                    qps, qpar = q4_mm(hg, tt, 512, q4s, q4keys)
                    while pend:
                        head_post(*pend.pop(0))
                    for hsub in range(4):
                        pend.append((hg * 4 + hsub, tt, qps, qpar, hsub, "q"))
            stream_kv()
            for tt in range(NT):
                qps, qpar = q4_mm(0, tt, 256, kvs, [("kvb",)])
                while pend:
                    head_post(*pend.pop(0))
                pend.append((0, tt, qps, qpar, 0, "k"))
                pend.append((0, tt, qps, qpar, 1, "v"))
            while pend:
                head_post(*pend.pop(0))

            def proj32(wh, dstf, dkey):
                p.op("Y", (lambda s=wh: lambda q: q.dma_start(
                    out=xslot(0)[:, 0:512].rearrange("p (e c) -> p e c", c=32),
                    in_=s.rearrange("e p c -> p e c")))(),
                    writes=["wsmf", ("xsl", 0)], dma=True)
                p.op("V", lambda v: v.tensor_copy(wbfs(2)[:, 0:512], xslot(0)[:, 0:512]),
                     reads=[("xsl", 0)], writes=["wsmb", ("wbf", 2)])
                for tc in range(4):
                    def mk_g(tc=tc):
                        def f(t):
                            ins = None
                            for eb in range(EB):
                                ins = t.matmul(
                                    s_ps[:32, tc * 512:(tc + 1) * 512],
                                    wbfs(2)[:, eb * 32:(eb + 1) * 32],
                                    ln0T(eb)[:, tc * 512:(tc + 1) * 512],
                                    start=(eb == 0), stop=(eb == EB - 1))
                            return ins
                        return f
                    p.op("T", mk_g(), reads=[("wbf", 2)] + ln0_keys, writes=[("gps", tc)])
                def mk_ev(dstf=dstf):
                    def f(v):
                        return v.tensor_copy(dstf, s_ps[:32, :])
                    return f
                p.op("V", mk_ev(), reads=[("gps", tc2) for tc2 in range(4)] +
                     [("q4b", c) for c in range(4)] + [("kvb",)],
                     writes=[dkey])

            proj32(wg_h, cB, "cB")


            p.op("S", lambda s: s.activation(cC, cB, AF.Abs), reads=["cB"], writes=["cC"])
            p.op("S", lambda s: s.activation(cC, cC, AF.Exp, scale=-1.0),
                 reads=["cC"], writes=["cC"])
            p.op("S", lambda s: s.activation(cC, cC, AF.Ln, bias=1.0),
                 reads=["cC"], writes=["cC"])
            p.op("V", lambda v: v.tensor_scalar(cB, cB, 0.0, None, ALU.min),
                 reads=["cB"], writes=["cB"])
            p.op("V", lambda v: v.tensor_sub(cB, cB, cC), reads=["cC", "cB"],
                 writes=["cB"])
            p.op("V", lambda v: v.memset(cA, 1.0), writes=["cA"])
            p.op("V", lambda v: v.tensor_tensor_scan(cC, cA, cB, 0.0, ALU.mult, ALU.add),
                 reads=["cA", "cB"], writes=["cC"])
            p.op("V", lambda v: v.tensor_sub(cC, cC, cB), reads=["cB", "cC"],
                 writes=["cC"])
            p.op("S", lambda s: s.activation(cC, cC, AF.Exp), reads=["cC"], writes=["cC"])
            proj32(wkvh_h, cA, "cA")
            p.op("V", lambda v: v.tensor_mul(cA, cA, cC), reads=["cA", "cC"],
                 writes=["cA"])
            p.op("V", lambda v: v.memset(cB, 1.0), writes=["cB"])
            p.op("V", lambda v: v.tensor_tensor_scan(cC, cB, cA, 0.0, ALU.mult, ALU.add),
                 reads=["cA", "cB"], writes=["cC"])
            cCbf = ovl1[:32, 4096:6144]
            p.op("V", lambda v: v.tensor_copy(cCbf, cC), reads=["cC"], writes=["cCbf"])
            def mk_ct(t):
                ins = None
                for k in range(NT):
                    ins = t.transpose(tr_ps[:, k * 32:(k + 1) * 32],
                                      cCbf[:, k * 128:(k + 1) * 128], id_bf[0:32, 0:32])
                return ins
            p.op("T", mk_ct, reads=["cCbf", "id_bf"], writes=["tr_ps"])
            p.op("V", lambda v: v.tensor_copy(ctok[:], tr_ps[:, 0:512]),
                 reads=["tr_ps"], writes=["ctok"])
            return ln0_keys

        def p5_attention_core(b):
            p.barrier()
            khTp = lambda hp: ovl1[:, (0 if hp == 0 else 6144):(2048 if hp == 0 else 8192)]
            vhp = lambda hp: ovl1[:, (2048 if hp == 0 else 8192):(4096 if hp == 0 else 10240)]
            pTp = lambda ip: ovl1[:, (4096 if ip == 0 else 10240):(6144 if ip == 0 else 12288)]
            kst5 = wbfs(2)
            lc5 = lambda ip: cols[:, 2 + ip:3 + ip]
            linv5 = lambda ip: cols[:, 10 + ip:11 + ip]

            def k_build(h):
                hp = h % 2
                ct3 = ctok[:].rearrange("p (t g) -> p t g", g=32)
                kc3 = ct3[:, :, h:h + 1].broadcast_to([128, NT, 128])
                vc3 = ct3[:, :, 16 + h:17 + h].broadcast_to([128, NT, 128])
                p.op("V", (lambda kc3=kc3: lambda v: v.tensor_tensor(
                    kst5[:].rearrange("p (t c) -> p t c", c=128),
                    rkp.rearrange("p (t c) -> p t c", c=128),
                    kc3, ALU.mult))(),
                    reads=[("rkp", tt) for tt in range(NT)] + ["ctok"],
                    writes=[("kst", tt) for tt in range(NT)])
                pairs = [(kst5[:, tt * 128:(tt + 1) * 128], ("kst", tt),
                          khTp(hp)[:, tt * 128:(tt + 1) * 128], ("khT", hp, tt))
                         for tt in range(NT)]
                transpose_batch(pairs, id_bf[:], "id_bf")
                p.op("V", (lambda vc3=vc3, hp=hp: lambda v: v.tensor_tensor(
                    vhp(hp)[:].rearrange("p (t c) -> p t c", c=128),
                    vp.rearrange("p (t c) -> p t c", c=128),
                    vc3, ALU.mult))(),
                    reads=[("vp", tt) for tt in range(NT)] + ["ctok"],
                    writes=[("vh", hp, tt) for tt in range(NT)])

            def p5_base(i):
                return 1024 * (i % 2) if i <= 7 else 0

            def p5_mainT(h, i):
                hp = h % 2
                rng = (i + 1) * 128
                base = p5_base(i)
                kh_keys = [("khT", hp, tt) for tt in range(NT)]
                dlo = (i * 128 // 512) * 512
                dw = min(512, rng - dlo)
                p.op("T", (lambda i=i, h=h, hp=hp, dlo=dlo, dw=dw, base=base:
                           lambda t: t.matmul(
                    s_ps[:, base + dlo:base + dlo + dw],
                    rq_tile(h, i),
                    khTp(hp)[:, dlo:dlo + dw], start=True, stop=True))(),
                    reads=kh_keys + [("rqT", h)],
                    writes=[("spb", (base + dlo) // 512)])
                c0 = 0
                nch = 0
                chunks = []
                while c0 < rng:
                    w = min(512, rng - c0)
                    if c0 != dlo:
                        p.op("T", (lambda i=i, h=h, hp=hp, c0=c0, w=w, base=base:
                                   lambda t: t.matmul(
                            s_ps[:, base + c0:base + c0 + w],
                            rq_tile(h, i),
                            khTp(hp)[:, c0:c0 + w], start=True, stop=True))(),
                            reads=kh_keys + [("rqT", h)],
                            writes=[("spb", (base + c0) // 512)])
                        chunks.append((c0, w, nch))
                        nch += 1
                    c0 += w
                return chunks, dlo, dw, nch

            def p5_mainV(h, i, chunks, dlo, dw, nch):
                ip = i % 2
                base = p5_base(i)
                if i <= 7:
                    mcol = 12 + 2 * ip
                else:
                    mcol = 12
                mxi = cols[:, 0 + ip:1 + ip]
                bsi = cols[:, 18 + ip:19 + ip]
                p.op("V", (lambda i=i, base=base: lambda v: v.tensor_add(
                    s_ps[:, base + i * 128:base + (i + 1) * 128],
                    s_ps[:, base + i * 128:base + (i + 1) * 128], mask_s[:]))(),
                    reads=["mask", ("spb", (base + dlo) // 512)],
                    writes=[("spb", (base + dlo) // 512)])
                p.op("V", (lambda dlo=dlo, dw=dw, base=base, mcol=mcol:
                           lambda v: v.reduce_max(
                    cols[:, mcol:mcol + 1], s_ps[:, base + dlo:base + dlo + dw],
                    axis=AX.X))(),
                    reads=[("spb", (base + dlo) // 512)], writes=[("mxcol", mcol)])
                for (c0, w, k) in chunks:
                    p.op("V", (lambda c0=c0, w=w, k=k, base=base, mcol=mcol:
                               lambda v: v.reduce_max(
                        cols[:, mcol + 1 + k:mcol + 2 + k],
                        s_ps[:, base + c0:base + c0 + w], axis=AX.X))(),
                        reads=[("spb", (base + c0) // 512)],
                        writes=[("mxcol", mcol + 1 + k)])
                mx_keys = [("mxcol", mcol + k) for k in range(nch + 1)]
                p.op("V", (lambda nch=nch, mcol=mcol, mxi=mxi: lambda v: v.reduce_max(
                    mxi, cols[:, mcol:mcol + 1 + nch], axis=AX.X))(),
                    reads=mx_keys, writes=[("c_mx", ip)])
                p.op("V", (lambda mxi=mxi, bsi=bsi: lambda v: v.tensor_scalar(
                    bsi, mxi, -SCALE, None, ALU.mult))(),
                    reads=[("c_mx", ip)], writes=[("c_bias", ip)])

            def p5_mainS(h, i):
                ip = i % 2
                rng = (i + 1) * 128
                base = p5_base(i)
                bsi = cols[:, 18 + ip:19 + ip]
                sps_keys = [("spb", (base // 512) + k)
                            for k in range((rng + 511) // 512)]
                p.op("S", (lambda rng=rng, ip=ip, base=base, bsi=bsi:
                           lambda s: s.activation(
                    wbfs(ip)[:, 0:rng], s_ps[:, base:base + rng], AF.Exp,
                    bias=bsi, scale=SCALE, accum_out=lc5(ip)))(),
                    reads=sps_keys + [("c_bias", ip)],
                    writes=[("wbf", ip), ("c_l", ip)] + sps_keys)
                p.op("V", (lambda ip=ip: lambda v: v.reciprocal(linv5(ip), lc5(ip)))(),
                     reads=[("c_l", ip)], writes=[("c_linv", ip)])

            def p5_partA(h, i):
                ip = i % 2
                pairs = [(wbfs(ip)[:, mb * 128:(mb + 1) * 128], ("wbf", ip),
                          pTp(ip)[:, mb * 128:(mb + 1) * 128], ("pT", ip, mb))
                         for mb in range(i + 1)]
                transpose_batch(pairs, id_bf[:], "id_bf")

            def p5_partB(h, i):
                hp = h % 2
                ip = i % 2
                vh_keys = [("vh", hp, tt) for tt in range(NT)]
                ops_t = (o_psA if ip == 0 else o_psB)[:, 0:128]

                def mk_o(i=i, opst=ops_t, ip=ip, hp=hp):
                    def f(t):
                        ins = None
                        for mb in range(i + 1):
                            ins = t.matmul(opst,
                                           pTp(ip)[:, mb * 128:(mb + 1) * 128],
                                           vhp(hp)[:, mb * 128:(mb + 1) * 128],
                                           start=(mb == 0), stop=(mb == i))
                        return ins
                    return f
                p.op("T", mk_o(), reads=[("pT", ip, mb) for mb in range(i + 1)] + vh_keys,
                     writes=[("qps", ip)])
                p.op("V", (lambda opst=ops_t, ip=ip: lambda v: v.tensor_scalar(
                    rsc[:, ip * 128:(ip + 1) * 128], opst, linv5(ip), None, ALU.mult))(),
                    reads=[("qps", ip), ("c_linv", ip)], writes=[("o_tok", ip)])
                p.op("T", (lambda ip=ip: lambda t: t.transpose(
                    tr_ps[:, 0:128], rsc[:, ip * 128:(ip + 1) * 128], id_bf[:]))(),
                    reads=[("o_tok", ip), "id_bf"], writes=["tr_ps"])
                p.op("V", (lambda h=h, i=i: lambda v: v.tensor_copy(
                    oT(h)[:, i * 128:(i + 1) * 128], tr_ps[:, 0:128]))(),
                    reads=["tr_ps"], writes=[("oT", h)])

            k_build(0)
            pend5 = None
            for h in range(H):
                for i in range(NT):
                    info = p5_mainT(h, i)
                    dummy_mm(3)
                    p5_mainV(h, i, *info)
                    if pend5 is not None:
                        p5_partA(*pend5)
                        p5_partB(*pend5)
                    p5_mainS(h, i)
                    pend5 = (h, i)
                    if i == 8 and h + 1 < H:
                        k_build(h + 1)
            p5_partA(*pend5)
            p5_partB(*pend5)

        def p6_out_proj(b):
            p.barrier()
            for hdb in range(EB):
                par = hdb % 2
                p.op("Y", (lambda hdb=hdb, par=par: lambda q: q.dma_start(
                    out=xslot(par), in_=wout_h[hdb]))(),
                    writes=[("wof", par), ("xsl", par)], dma=True)
                p.op("V", (lambda hdb=hdb, par=par: lambda v: v.tensor_copy(
                    woutb(hdb), xslot(par)))(),
                    reads=[("xsl", par)], writes=[("wob", hdb)])
            wob_keys = [("wob", hdb) for hdb in range(EB)]
            oT_keys = [("oT", h) for h in range(H)]
            wbf_f32_p6 = wbf[:].bitcast(F32)
            aof = lambda par: wbf_f32_p6[:, par * 2048:(par + 1) * 2048]
            sq6 = ovl1[:, 0:2048]
            ln1b6 = lambda par: ovl1[:, 2048 + par * 2048:4096 + par * 2048]
            stg6 = lambda par: ovl1[:, 6144 + par * 2048:8192 + par * 2048]

            def p6_mm(tt):
                par = tt % 2
                p.op("Y", (lambda b=b, tt=tt, par=par: lambda q: q.dma_start(
                    out=xslot(par), in_=x_h[b, tt]))(),
                    writes=[("xsl", par)], dma=True)
                def mk_ao(tt=tt):
                    def f(t):
                        ins = None
                        for hdb in range(EB):
                            for ec in range(4):
                                ins = t.matmul(
                                    s_ps[:, ec * 512:(ec + 1) * 512],
                                    oT(hdb)[:, tt * 128:(tt + 1) * 128],
                                    woutb(hdb)[:, ec * 512:(ec + 1) * 512],
                                    start=(hdb == 0), stop=(hdb == EB - 1))
                        return ins
                    return f
                p.op("T", mk_ao(), reads=wob_keys + oT_keys, writes=["s_ps"])
                p.op("S", (lambda par=par: lambda s: s.activation(
                    aof(par), s_ps[:, :], AF.Copy,
                    accum_out=cols[:, 16 + par:17 + par]))(),
                    reads=["s_ps"], writes=[("aof", par), ("c_m1p", par)])
                dummy_mm(4)

            def p6_postV(tt):
                par = tt % 2
                p.op("S", (lambda par=par: lambda s: s.activation(
                    sq6, aof(par), AF.Square, accum_out=m2c))(),
                    reads=[("aof", par)], writes=["sq_s", "c_m2"])
                p.op("V", (lambda par=par: lambda v: v.tensor_scalar(
                    m1c, cols[:, 16 + par:17 + par], 1.0 / E, None, ALU.mult))(),
                    reads=[("c_m1p", par)], writes=["c_m1"])
                p.op("V", lambda v: v.tensor_scalar(
                    m2c, m2c, 1.0 / E, None, ALU.mult), reads=["c_m2"], writes=["c_m2"])
                p.op("V", lambda v: v.tensor_mul(t0c, m1c, m1c),
                     reads=["c_m1"], writes=["c_t0"])
                p.op("V", lambda v: v.tensor_sub(vrc, m2c, t0c),
                     reads=["c_m2", "c_t0"], writes=["c_vr"])
                p.op("V", lambda v: v.tensor_scalar(
                    vrc, vrc, LN_EPS, None, ALU.add), reads=["c_vr"], writes=["c_vr"])
                p.op("S", lambda s: s.activation(rsdc, vrc, AF.Ln),
                     reads=["c_vr"], writes=["c_rsd"])
                p.op("S", lambda s: s.activation(rsdc, rsdc, AF.Exp, scale=-0.5),
                     reads=["c_rsd"], writes=["c_rsd"])
                ln_apply([("aof", par)], aof(par), ("aof", par), aof(par))
                p.op("V", (lambda par=par: lambda v: v.tensor_add(
                    xslot(par), xslot(par), aof(par)))(),
                    reads=[("xsl", par), ("aof", par)], writes=[("xsl", par)])
                p.op("G", (lambda b=b, tt=tt, par=par: lambda q: q.dma_start(
                    out=x1_scr[b, tt], in_=xslot(par)))(),
                    reads=[("xsl", par)], writes=[("x1w", tt)], dma=True)
                ln_cols(("xsl", par), xslot(par), E, sq6, ["sq_s"], LN_EPS)
                ln_apply([("ln1b", par)], ln1b6(par), ("xsl", par), xslot(par))

            def p6_postT(tt):
                par = tt % 2
                pairs = [(ln1b6(par)[:, eb * 128:(eb + 1) * 128], ("ln1b", par),
                          stg6(par)[:, eb * 128:(eb + 1) * 128], ("l1st", par))
                         for eb in range(EB)]
                transpose_batch(pairs, id_bf[:], "id_bf")
                p.op("G", (lambda b=b, tt=tt, par=par: lambda q: q.dma_start(
                    out=ln1T_scr[b].rearrange("e p t -> p e t")[:, :, tt * 128:(tt + 1) * 128],
                    in_=stg6(par).rearrange("p (e t) -> p e t", t=128)))(),
                    reads=[("l1st", par)],
                    writes=[("l1w", tt)], dma=True)

            for tt in range(NT):
                p6_mm(tt)
                if tt >= 2:
                    p6_postT(tt - 2)
                if tt >= 1:
                    p6_postV(tt - 1)
            p6_postT(NT - 2)
            p6_postV(NT - 1)
            p6_postT(NT - 1)

        def f1_gateup_both():
            p.barrier()
            p.op("V", lambda v: v.memset(ssq_sb[:], 0.0), writes=["ssq"])
            for b2 in range(B):
                for eb in range(EB):
                    dst = (bigA if b2 == 0 else bigB)[:, eb * N:(eb + 1) * N]
                    p.op("Y", (lambda b2=b2, eb=eb, dst=dst: lambda q: q.dma_start(
                        out=dst, in_=ln1T_scr[b2, eb]))(),
                        writes=[("ln1T", b2, eb)], dma=True)
            l1_keys = [("ln1T", b2, eb) for b2 in range(B) for eb in range(EB)]
            wbf_f32 = wbf[:].bitcast(F32)
            gate_f = wbf_f32[:, 0:2048]
            upf_f = wbf_f32[:, 2048:4096]

            def gT_ap(par, eb):
                if par == 0:
                    return ovl1[:, 6144 + eb * 128:6144 + (eb + 1) * 128]
                return ovl1[:, 8192 + eb * 128:8192 + (eb + 1) * 128]

            def uT_ap(par, eb):
                if par == 0:
                    return ovl1[:, 10240 + eb * 128:10240 + (eb + 1) * 128]
                if eb < 8:
                    return cos_s[:, eb * 128:(eb + 1) * 128]
                return sin_s[:, (eb - 8) * 128:(eb - 7) * 128]

            def qchain2(src_ap, src_key, Bq, kb):
                th = []
                th.append((lambda src_ap=src_ap, src_key=src_key: p.op(
                    "S", (lambda a=src_ap: lambda s: s.copy(qa, a))(),
                    reads=[src_key], writes=["q_a"])))
                th.append(lambda: p.op("V", lambda v: v.tensor_reduce(
                    qred, qa[:].rearrange("p (g s) -> p g s", s=64), axis=AX.X,
                    op=ALU.add, apply_absolute_value=True),
                    reads=["q_a"], writes=["q_red"]))
                th.append(lambda: p.op("V", lambda v: v.tensor_scalar(
                    qsc_bf, qred, 1.0 / 64, 1e-8, ALU.mult, ALU.max),
                    reads=["q_red"], writes=["q_sc"]))
                th.append(lambda: p.op("V", lambda v: v.tensor_scalar(
                    qthr, qsc_bf, QTHR, None, ALU.mult),
                    reads=["q_sc"], writes=["q_thr"]))
                th.append((lambda Bq=Bq, kb=kb: p.op(
                    "S", lambda s: s.sign(Bq, qa), reads=["q_a"], writes=[kb])))
                th.append(lambda: p.op("S", lambda s: s.activation(
                    qa, qa, AF.Abs), reads=["q_a"], writes=["q_a"]))
                th.append(lambda: p.op("V", lambda v: v.tensor_tensor(
                    qa[:].rearrange("p (g s) -> p g s", s=64),
                    qa[:].rearrange("p (g s) -> p g s", s=64),
                    qthr[:].rearrange("p (g o) -> p g o", o=1)
                    .broadcast_to([128, 32, 64]),
                    ALU.is_gt), reads=["q_a", "q_thr"], writes=["q_a"]))
                th.append((lambda Bq=Bq, kb=kb: p.op(
                    "V", lambda v: v.tensor_mul(Bq, Bq, qa),
                    reads=["q_a", kb], writes=[kb])))
                th.append((lambda Bq=Bq, kb=kb: p.op(
                    "V", lambda v: v.tensor_tensor(
                        Bq.rearrange("p (g s) -> p g s", s=64),
                        Bq.rearrange("p (g s) -> p g s", s=64),
                        qsc_bf[:].rearrange("p (g o) -> p g o", o=1)
                        .broadcast_to([128, 32, 64]),
                        ALU.mult), reads=[kb, "q_sc"], writes=[kb])))
                return th

            def f1_quant_thunks(j):
                par = j % 2
                th = []
                th.append(lambda: p.op("Y", (lambda j=j: lambda q: q.dma_start(
                    out=gate_f, in_=wgu_h[j * 128:(j + 1) * 128, :]))(),
                    writes=[("gf",)], dma=True))
                th.append(lambda: p.op("Y", (lambda j=j: lambda q: q.dma_start(
                    out=upf_f, in_=wgu_h[HF + j * 128:HF + (j + 1) * 128, :]))(),
                    writes=[("uf",)], dma=True))
                th += qchain2(gate_f, ("gf",), qsg[:], "q_bg")
                th += qchain2(upf_f, ("uf",), qm[:], "q_bu")
                pairs = [(qsg[:, eb * 128:(eb + 1) * 128], "q_bg",
                          gT_ap(par, eb), ("gT", par)) for eb in range(EB)]
                th += transpose_thunks(pairs, id_bf[:], "id_bf")
                pairs = [(qm[:, eb * 128:(eb + 1) * 128], "q_bu",
                          uT_ap(par, eb), ("uT", par)) for eb in range(EB)]
                th += transpose_thunks(pairs, id_bf[:], "id_bf")
                return th

            pendW = []
            prevT = None
            for (vth, tth) in wp_chunks():
                pendW += vth
                if prevT:
                    pendW += prevT
                prevT = tth
            pendW += prevT
            pending = f1_quant_thunks(0)
            while pending:
                pending.pop(0)()
            for j in range(64):
                par = j % 2
                pending = f1_quant_thunks(j + 1) if j + 1 < 64 else []
                for b2 in range(B):
                    ln1Tb = (bigA if b2 == 0 else bigB)
                    for tc in range(4):
                        g3 = (j * 8 + b2 * 4 + tc) % 3
                        gps = [o_psA[:, 0:512], s_ps[:, 0:512],
                               s_ps[:, 1024:1536]][g3]
                        ups = [o_psB[:, 0:512], s_ps[:, 512:1024],
                               dmy_ps[:, 0:512]][g3]
                        gk = ("gups", g3)
                        uk = ("uups", g3)

                        def mk_mm(dst, wap_fn, tc=tc, b2=b2, ln1Tb=ln1Tb, par=par):
                            def f(t):
                                ins = None
                                for eb2 in range(EB):
                                    ins = t.matmul(
                                        dst, wap_fn(par, eb2),
                                        ln1Tb[:, eb2 * N + tc * 512:eb2 * N + (tc + 1) * 512],
                                        start=(eb2 == 0), stop=(eb2 == EB - 1))
                                return ins
                            return f
                        p.op("T", mk_mm(gps, gT_ap), reads=[("gT", par)] + l1_keys,
                             writes=[gk])
                        p.op("T", mk_mm(ups, uT_ap), reads=[("uT", par)] + l1_keys,
                             writes=[uk])
                        p.op("S", (lambda gps=gps: lambda s: s.activation(
                            wf32[:, 0:512], gps, AF.Silu))(),
                            reads=[gk], writes=["sgf"])
                        p.op("V", (lambda ups=ups, tc=tc, b2=b2: lambda v: v.tensor_mul(
                            hst(b2)[:, tc * 512:(tc + 1) * 512], wf32[:, 0:512], ups))(),
                            reads=["sgf", uk], writes=[("hb", b2, tc)])
                        p.op("S", (lambda tc=tc, b2=b2: lambda s: s.activation(
                            h2v[:, tc * 512:(tc + 1) * 512],
                            hst(b2)[:, tc * 512:(tc + 1) * 512], AF.Square))(),
                            reads=[("hb", b2, tc)], writes=[("h2", tc)])
                        for _ in range(3):
                            if pending:
                                pending.pop(0)()
                        for _ in range(4):
                            if pendW:
                                pendW.pop(0)()
                    def mk_ssq(b2=b2, j=j):
                        def f(t):
                            ins = None
                            for ttl in range(NT):
                                ins = t.matmul(s_ps[:, 1536 + ttl:1536 + ttl + 1],
                                               h2v[:, ttl * 128:(ttl + 1) * 128],
                                               ones_s[:], start=True, stop=True)
                            return ins
                        return f

                    def emit_ssq(b2=b2, j=j):
                        p.op("T", mk_ssq(b2, j),
                             reads=[("h2", tc2) for tc2 in range(4)] + ["ones"],
                             writes=["ssq_ps"])
                        p.op("V", (lambda b2=b2: lambda v: v.tensor_add(
                            ssq_sb[:, b2 * 16:(b2 + 1) * 16],
                            ssq_sb[:, b2 * 16:(b2 + 1) * 16],
                            s_ps[:, 1536:1552]))(),
                            reads=["ssq_ps", "ssq"], writes=["ssq"])
                    emit_ssq()
                    p.op("G", (lambda b2=b2, j=j, hpar=b2: lambda q: q.dma_start(
                        out=h_scr[b2, :, j].rearrange("q p t -> p q t"),
                        in_=hst(b2).rearrange("p (q t) -> p q t", t=128)))(),
                        reads=[("hb", b2, tc2) for tc2 in range(4)],
                        writes=[("hw", b2, j)], dma=True)
                while pending:
                    pending.pop(0)()
            while pendW:
                pendW.pop(0)()

        out_keys = []

        def f2_proj():
            p.barrier()
            p.op("V", lambda v: v.tensor_scalar(
                inv_sb[:], ssq_sb[:], 1.0 / HF, RMS_EPS, ALU.mult, ALU.add),
                reads=["ssq"], writes=["inv1"])
            p.op("S", lambda s: s.activation(inv_sb[:], inv_sb[:], AF.Ln),
                 reads=["inv1"], writes=["inv2"])
            p.op("S", lambda s: s.activation(inv_sb[:], inv_sb[:], AF.Exp, scale=-0.5),
                 reads=["inv2"], writes=["inv"])
            ht_keys_o = ["q_a", "q_sg", "q_m", "q_hat", ("gT", 0)]
            finsq = ovl1[:, 8192:10240]
            finln = ovl1[:, 10240:12288]

            def fin_tile(gt):
                bf, tf = gt // NT, gt % NT
                p.op("Y", (lambda bf=bf, tf=tf: lambda q: q.dma_start(
                    out=xslot(0), in_=ffn_scr[bf, tf]))(),
                    reads=[("fw", gt, 0), ("fw", gt, 1)],
                    writes=["ffs", ("xsl", 0)], dma=True)
                p.op("Y", (lambda bf=bf, tf=tf: lambda q: q.dma_start(
                    out=xslot(1), in_=x1_scr[bf, tf]))(),
                    writes=["x1s", ("xsl", 1)], dma=True)
                ln_cols("ffs", xslot(0), E, finsq, ["finsq"], LN_EPS)
                ln_apply(["finln"], finln, "ffs", xslot(0))
                p.op("V", lambda v: v.tensor_add(xslot(1), xslot(1), finln),
                     reads=["x1s", "finln"], writes=["x1s", ("xsl", 1)])
                k = ("ow", gt)
                p.op("G", (lambda bf=bf, tf=tf: lambda q: q.dma_start(
                    out=out_h[bf, tf], in_=xslot(1)))(),
                    reads=["x1s"], writes=[k], dma=True)
                out_keys.append(k)

            for dh in range(2):
                for dq2 in range(2):
                    X = wptq(dq2)
                    for grp in range(8):
                        p.op("Y", (lambda dh=dh, dq2=dq2, grp=grp, X=X:
                                   lambda qe: qe.dma_start(
                            out=X[:, grp * 4096:(grp + 1) * 4096]
                            .rearrange("p (k t) -> p k t", t=512),
                            in_=whq_scr[2 * dh + dq2, grp * 8:(grp + 1) * 8]
                            .rearrange("k p t -> p k t")))(),
                            writes=[("wq", dq2, grp)], dma=True)
                for gt in range(32):
                    b, tt = gt // NT, gt % NT
                    odd = gt % 2 == 1
                    htbuf = ovl1[:, 0:8192] if odd else wbf[:]
                    htk = ht_keys_o if odd else [("wbf", i4) for i4 in range(4)]
                    p.op("Y", (lambda b=b, tt=tt, htbuf=htbuf: lambda q: q.dma_start(
                        out=htbuf.rearrange("p (q t) -> p q t", t=128),
                        in_=h_scr[b, tt].rearrange("q p t -> p q t")))(),
                        writes=["htile%d" % (gt % 2)] + htk, dma=True)
                    for dq2 in range(2):
                        X = wptq(dq2)
                        yps = o_psA if dq2 == 0 else o_psB
                        for grp in range(8):
                            def mk_y(X=X, yps=yps, htbuf=htbuf, grp=grp):
                                def f(t):
                                    ins = None
                                    for j in range(grp * 8, (grp + 1) * 8):
                                        ins = t.matmul(
                                            yps[:, 0:512],
                                            htbuf[:, j * 128:(j + 1) * 128],
                                            X[:, j * 512:(j + 1) * 512],
                                            start=(j == 0), stop=(j == 63))
                                    return ins
                                return f
                            p.op("T", mk_y(),
                                 reads=["htile%d" % (gt % 2)] + htk +
                                 [("wq", dq2, grp)],
                                 writes=[("yps", dq2)])
                        p.op("V", (lambda gt=gt, dq2=dq2, yps=yps: lambda v: v.tensor_scalar(
                            ovl2[:, 0:4096].bitcast(F32)[:, dq2 * 512:(dq2 + 1) * 512],
                            yps[:, 0:512], inv_sb[:, gt:gt + 1], None, ALU.mult))(),
                            reads=[("yps", dq2), "inv"], writes=[("yf", dq2)])
                    p.op("G", (lambda b=b, tt=tt, dh=dh: lambda q: q.dma_start(
                        out=ffn_scr[b, tt][:, dh * 1024:(dh + 1) * 1024],
                        in_=ovl2[:, 0:4096].bitcast(F32)[:, 0:1024]))(),
                        reads=[("yf", 0), ("yf", 1)], writes=[("fw", gt, dh)], dma=True)
                    if dh == 1 and gt >= 2:
                        fin_tile(gt - 2)
                if dh == 1:
                    fin_tile(30)
                    fin_tile(31)

        def final_ln():
            p.op("Y", None, reads=out_keys)

        stage = 0
        for b in range(B):
            if upto >= 1 + b * 3:
                attention(b)
                if debug and b == 0:
                    p.barrier()
                    p.op("G", lambda q: q.dma_start(out=tap_ctok, in_=ctok[:]), reads=["ctok"], dma=True)
                    p.op("G", lambda q: q.dma_start(out=tap_cum, in_=cum[:, :]), dma=True)
                    p.op("G", lambda q: q.dma_start(out=tap_rkp, in_=rkp), dma=True)
                    p.op("G", lambda q: q.dma_start(out=tap_vp, in_=vp), dma=True)
                    p.op("G", lambda q: q.dma_start(out=tap_rqT, in_=bigB[:]), dma=True)
                    p.barrier()
            if upto >= 2 + b * 3:
                p5_attention_core(b)
                if debug and b == 0:
                    p.barrier()
                    p.op("G", lambda q: q.dma_start(out=tap_oT, in_=bigA[:]), dma=True)
                    p.barrier()
            if upto >= 3 + b * 3:
                p6_out_proj(b)
        if upto >= 7:
            f1_gateup_both()
        if upto >= 8:
            f2_proj()
        if upto >= 9:
            final_ln()

        p.emit(block, sems)
    return nc


_NC_CACHE = {}


def _get_nc(debug=False):
    if debug not in _NC_CACHE:
        _NC_CACHE[debug] = build_nc(debug)
    return _NC_CACHE[debug]


def _host_prep(d):
    Wq = np.asarray(d["L0_attn_Wq"], np.float32)
    Wkv = np.asarray(d["L0_attn_Wkv"], np.float32)
    Wkvh = np.asarray(d["L0_attn_Wkvh"], np.float32)
    Wg = np.asarray(d["L0_attn_Wg"], np.float32)
    Wout = np.asarray(d["L0_attn_Wout"], np.float32)
    wq = np.ascontiguousarray(Wq.T.reshape(EB, 128, 4, 512).transpose(2, 0, 1, 3))
    wkv = np.ascontiguousarray(Wkv.T.reshape(EB, 128, 256))
    wkvh = np.ascontiguousarray(Wkvh.T.reshape(EB, 128, 32))
    wg = np.ascontiguousarray(Wg.T.reshape(EB, 128, 32))
    wout = np.ascontiguousarray(Wout.T.reshape(EB, 128, E))
    ident = np.eye(128, dtype=BFNP)
    mask = np.where(np.arange(128)[None, :] <= np.arange(128)[:, None],
                    np.float32(0.0), np.float32(-1e30)).astype(np.float32)
    pos = np.arange(N, dtype=np.float64)
    inv = 10000.0 ** (-np.arange(0, D, 2, dtype=np.float64) / D)
    ang = pos[:, None] * inv[None, :]
    cos_tok = np.ascontiguousarray(np.cos(ang).astype(np.float32).astype(BFNP)
                                   .reshape(NT, 128, 64).transpose(1, 0, 2)
                                   .reshape(128, NT * 64))
    sin_tok = np.ascontiguousarray(np.sin(ang).astype(np.float32).astype(BFNP)
                                   .reshape(NT, 128, 64).transpose(1, 0, 2)
                                   .reshape(128, NT * 64))
    ones_col = np.ones((128, 1), dtype=BFNP)
    com = dict(wq=wq, wkv=wkv, wkvh=wkvh, wg=wg, wout=wout,
               wgu=np.ascontiguousarray(np.asarray(d["L0_ffn_w_gate_up"], np.float32)),
               wp=np.ascontiguousarray(np.asarray(d["L0_ffn_w_proj"], np.float32)),
               ident=ident, mask=mask, cos_tok=cos_tok, sin_tok=sin_tok,
               ones_col=ones_col)
    return com


def kernel(**inputs):
    x = np.asarray(inputs["L0_attn_x"], np.float32)
    com = _host_prep(inputs)
    nc = _get_nc(False)
    in_maps = []
    for c in range(8):
        m = dict(com)
        m["x"] = np.ascontiguousarray(
            x[c * B:(c + 1) * B].reshape(B, NT, 128, E))
        in_maps.append(m)
    res = run_bass_kernel_spmd(nc, in_maps, core_ids=list(range(8)))
    outs = [r["out"].reshape(B, N, E) for r in res.results]
    return np.concatenate(outs, axis=0)
